# version 07
# baseline (speedup 1.0000x reference)
import numpy as np
import concourse.bass as bass
import concourse.mybir as mybir
from bass_rust import ReduceOp

F32 = mybir.dt.float32
F16 = mybir.dt.float16
BF16 = mybir.dt.bfloat16
AF = mybir.ActivationFunctionType
ALU = mybir.AluOpType
AX = mybir.AxisListType.X

P = 128
T = 8192
TOK = 1024
E = 2048
H = 16
D = 128
HID = 5632
EC = E // P
NJT = T // P
NHT = HID // P
SCALING = float(D) ** -0.5
EPS_LN = 1e-5
EPS_Q = 1e-8
MAGIC = 1.5 * 2 ** 23
C_EXTRA = 2.0
NQIN = 4
QCOUNT = float(HID * E)

ENGS = ("sync", "gpsimd", "vector", "scalar", "tensor")
SEM_NAMES = ["qs", "qg", "qa", "pe", "act", "dve", "pool", "outq"]
QSEM = {"sync": "qs", "gpsimd": "qg", "scalar": "qa", "vector": "qa"}
DMA_SEMS = {"qs", "qg", "qa", "outq"}


def sl(i, n):
    return slice(i * n, (i + 1) * n)


def lat(*toks):
    best = {}
    for t in toks:
        if t is None:
            continue
        items = t if isinstance(t, list) else [t]
        for u in items:
            if u is None:
                continue
            s2, v2 = u
            if best.get(s2, -1) < v2:
                best[s2] = v2
    return [(k, v) for k, v in best.items()]


class Sched:
    def __init__(self):
        self.ops = {e: [] for e in ENGS}
        self.cnt = {k: 0 for k in SEM_NAMES}

    def _op(self, eng, fn, waits, inc, amt):
        ws = lat(*waits) if waits else []
        ws = [(sn, min(v + 32, self.cnt[sn]) if sn in DMA_SEMS else v)
              for (sn, v) in ws]
        tok = None
        if inc is not None:
            self.cnt[inc] += amt
            tok = (inc, self.cnt[inc])
        self.ops[eng].append((fn, ws, inc, amt))
        return tok

    def dma(self, eng, out, in_, waits=(), **kw):
        return self._op(eng, lambda e, o=out, i=in_, k=kw: e.dma_start(out=o, in_=i, **k),
                        waits, QSEM[eng], 16)

    def dma_out(self, eng, out, in_, waits=(), **kw):
        return self._op(eng, lambda e, o=out, i=in_, k=kw: e.dma_start(out=o, in_=i, **k),
                        waits, "outq", 16)

    def mm(self, out, lhsT, rhs, start, stop, waits=(), inc=False):
        return self._op("tensor",
                        lambda e, o=out, l=lhsT, r=rhs, st=start, sp=stop:
                        e.matmul(o, l, r, start=st, stop=sp),
                        waits, "pe" if inc else None, 1)

    def act(self, out, in_, func, bias=None, scale=None, accum_out=None, waits=()):
        def fn(e, o=out, i=in_, f=func, b=bias, sc=scale, ao=accum_out):
            kw = {}
            if b is not None:
                kw["bias"] = b
            if sc is not None:
                kw["scale"] = sc
            if ao is not None:
                kw["accum_out"] = ao
            return e.activation(out=o, in_=i, func=f, **kw)
        return self._op("scalar", fn, waits, "act", 1)

    def dve(self, fn, waits=()):
        return self._op("vector", fn, waits, "dve", 1)

    def pool(self, fn, waits=()):
        return self._op("gpsimd", fn, waits, "pool", 1)

    def fin(self, eng, waits):
        return self._op(eng, lambda e: e.nop(), waits, None, 1)

    def emit(self, eng_obj, eng_name, sems):
        hw = {}
        for fn, waits, inc, amt in self.ops[eng_name]:
            for (sname, v) in waits:
                if hw.get(sname, 0) < v:
                    eng_obj.wait_ge(sems[sname], v)
                    hw[sname] = v
            inst = fn(eng_obj)
            if inc is not None:
                inst.then_inc(sems[inc], amt)


def build_program(dbg=0):
    nc = bass.Bass("TRN2", target_bir_lowering=False, debug=False)
    s = Sched()
    dk = "ExternalOutput" if dbg else "Internal"

    xf = nc.dram_tensor("x_full", [T, E], F32, kind="ExternalInput").ap()
    xo = nc.dram_tensor("x_own", [TOK, E], F32, kind="ExternalInput").ap()
    Wq = nc.dram_tensor("Wq", [E, E], F32, kind="ExternalInput").ap()
    Wk = nc.dram_tensor("Wk", [E, E], F32, kind="ExternalInput").ap()
    Wv = nc.dram_tensor("Wv", [E, E], F32, kind="ExternalInput").ap()
    Wo = nc.dram_tensor("Wo", [E, E], F32, kind="ExternalInput").ap()
    bqv = nc.dram_tensor("bq", [1, E], F32, kind="ExternalInput").ap()
    bkv = nc.dram_tensor("bk", [1, E], F32, kind="ExternalInput").ap()
    bvv = nc.dram_tensor("bv", [1, E], F32, kind="ExternalInput").ap()
    bov = nc.dram_tensor("bo", [1, E], F32, kind="ExternalInput").ap()
    slopes = nc.dram_tensor("slopes", [1, H], F32, kind="ExternalInput").ap()
    w1 = nc.dram_tensor("w1", [HID, E], F32, kind="ExternalInput").ap()
    w2 = nc.dram_tensor("w2", [HID, E], F32, kind="ExternalInput").ap()
    w3 = nc.dram_tensor("w3", [E, HID], F32, kind="ExternalInput").ap()
    wofs = nc.dram_tensor("wofs", [P, 1], F32, kind="ExternalInput").ap()
    yout = nc.dram_tensor("y", [TOK, E], F32, kind="ExternalOutput").ap()

    xn_d = nc.dram_tensor("xn_d", [T, E], BF16, kind=dk).ap()
    xno_d = nc.dram_tensor("xno_d", [TOK, E], BF16, kind="Internal").ap()
    kscr = nc.dram_tensor("kscr", [H, P, T], BF16, kind=dk).ap()
    vscr = nc.dram_tensor("vscr", [H, T, D], BF16, kind=dk).ap()
    qpp_d = nc.dram_tensor("qpp_d", [H, P, TOK], BF16, kind=dk).ap()
    attn_d = nc.dram_tensor("attn_d", [H, P, TOK], BF16, kind=dk).ap()
    wbf = [nc.dram_tensor(f"wbf{i}", [E, E], BF16, kind="Internal").ap() for i in range(2)]
    q1_d = nc.dram_tensor("q1_d", [HID, E], BF16, kind=dk).ap()
    q2_d = nc.dram_tensor("q2_d", [HID, E], BF16, kind="Internal").ap()
    q3_d = nc.dram_tensor("q3_d", [E, HID], BF16, kind=dk).ap()
    w1t_d = nc.dram_tensor("w1t_d", [E, HID], BF16, kind="Internal").ap()
    w2t_d = nc.dram_tensor("w2t_d", [E, HID], BF16, kind="Internal").ap()
    q3t_d = nc.dram_tensor("q3t_d", [HID, E], BF16, kind="Internal").ap()
    xres_d = nc.dram_tensor("xres_d", [TOK, E], F32, kind=dk).ap()
    xn2_d = nc.dram_tensor("xn2_d", [TOK, E], BF16, kind=dk).ap()
    hid_d = nc.dram_tensor("hid_d", [TOK // 2, HID], BF16, kind=dk).ap()
    mqrow_d = nc.dram_tensor("mqrow_d", [1, H], F32, kind="Internal").ap()
    negc_d = nc.dram_tensor("negc_d", [H, 1, 1], F32, kind="Internal").ap()
    gsc_d = nc.dram_tensor("gsc_d", [P, 3], F32, kind="Internal").ap()

    SM = nc.alloc_sbuf_tensor("SM", [P, 256], F32).ap()
    rowf = nc.alloc_sbuf_tensor("rowf", [1, 2112], F32).ap()
    rowbf = nc.alloc_sbuf_tensor("rowbf", [1, 5120], BF16).ap()
    onescol = nc.alloc_sbuf_tensor("onescol", [P, 2], BF16).ap()
    onesrow = nc.alloc_sbuf_tensor("onesrow", [1, 128], BF16).ap()
    grow = nc.alloc_sbuf_tensor("grow", [1, 384], F32).ap()

    rem = nc.sbuf_bytes_remaining - 512
    arena = nc.alloc_sbuf_tensor("arena", [P, rem], mybir.dt.uint8).ap()
    assert rem >= 190464, f"arena too small: {rem}"

    def reg(off, cols, dtype):
        esz = {F32: 4, BF16: 2, F16: 2}[dtype]
        assert off + cols * esz <= rem, (off, cols)
        return arena[:, off:off + cols * esz].bitcast(dtype)

    xslab = [reg(i * 8192, E, F32) for i in range(2)]
    kvst = [reg(i * 4096, E, BF16) for i in range(4)]
    qh = [reg(i * 2048, TOK, BF16) for i in range(2)]
    p0sb = [reg(4096 + i * 2048, TOK, BF16) for i in range(2)]
    ptsb = [reg(8192 + i * 2048, TOK, BF16) for i in range(2)]
    atst = [reg(12288 + i * 2048, TOK, BF16) for i in range(2)]
    xnst = [reg(16384 + i * 4096, E, BF16) for i in range(2)]
    atch = [reg(16384 + i * 2048, TOK, BF16) for i in range(2)]
    sgst = [reg(20480 + i * 1024, 512, BF16) for i in range(2)]
    q3ch = [reg(16384 + i * 2048, TOK, BF16) for i in range(2)]
    WTc = [reg(24576 + ec * E * 2, E, BF16) for ec in range(EC)]
    WalRaw = reg(24576, 9216, F32)
    qin = [reg(174080 + i * 4096, TOK, F32) for i in range(4)]
    tb_a2 = reg(24576, TOK, F32)
    tb_b2 = reg(28672, TOK, BF16)
    qst = [reg(30720 + i * 2048, TOK, BF16) for i in range(2)]
    Eh = reg(43008, 9216, BF16)
    rbsb = reg(61440, TOK, F32)
    xn2Tc = [reg(24576 + ec * TOK * 2, TOK, BF16) for ec in range(EC)]
    w1sl = [reg(57344 + i * 16384, EC * 512, BF16) for i in range(2)]
    Wal = reg(90112, 9216, F16)
    xnoTc = [reg(108544 + ec * 2048, TOK, BF16) for ec in range(EC)]
    ksb = [reg(108544 + b * 16384, T, BF16) for b in range(2)]
    xrst = [reg(108544 + i * 8192, E, F32) for i in range(2)]
    xn2st = [reg(124928 + i * 4096, E, BF16) for i in range(2)]
    hid_tt = [reg(108544 + i * 11264, HID, BF16) for i in range(4)]
    hidTc = [reg(108544 + hc * 1024, 512, BF16) for hc in range(NHT)]
    xnTc = [[reg(141312 + b * 16384 + ec * 1024, 512, BF16) for ec in range(EC)]
            for b in range(2)]
    vsb = [reg(141312 + b * 16384, T, BF16) for b in range(2)]
    outst2 = [reg(174080 + i * 4096, TOK, F32) for i in range(2)]
    xrtile = [reg(182272 + i * 4096, TOK, F32) for i in range(2)]
    wstg2 = [reg(174080 + b * 8192, 4096, BF16) for b in range(2)]
    sqst = reg(20480, 512, BF16)

    psb = [nc.alloc_psum_tensor(f"pb{i}", [P, 512], F32) for i in range(8)]

    t_slopes = s.dma("sync", rowf[0:1, 0:H], slopes)
    t_wofs = s.dma("sync", SM[:, 176:177], wofs)
    t_bvrow = s.dma("gpsimd", rowbf[0:1, 0:E], bvv)
    t_borow = s.dma("gpsimd", rowbf[0:1, E:2 * E], bov)
    t_bqT = s.dma("gpsimd", SM[:, 64:80], bqv.rearrange("o (c p) -> p (o c)", p=P),
                  allow_slow_non_contiguous=True)
    t_bkT = s.dma("gpsimd", SM[:, 80:96], bkv.rearrange("o (c p) -> p (o c)", p=P),
                  allow_slow_non_contiguous=True)
    t_bvT = s.dma("gpsimd", SM[:, 96:112], bvv.rearrange("o (c p) -> p (o c)", p=P),
                  allow_slow_non_contiguous=True)

    t_slopec = s.dma("sync", SM[:, 0:16], slopes.partition_broadcast(P))
    t_rslope = s.dve(lambda v: v.reciprocal(out=SM[:, 16:32], in_=SM[:, 0:16]),
                     waits=(t_slopec,))
    t_sfold = s.dve(lambda v: v.tensor_scalar_mul(out=SM[:, 32:48], in0=SM[:, 16:32],
                                                  scalar1=SCALING), waits=(t_rslope,))
    t_qbias = s.dve(lambda v: v.tensor_mul(out=SM[:, 48:64], in0=SM[:, 64:80],
                                           in1=SM[:, 32:48]), waits=(t_sfold, t_bqT))
    t_negs = s.dve(lambda v: v.tensor_scalar_mul(out=SM[:, 112:128], in0=SM[:, 0:16],
                                                 scalar1=-1.0), waits=(t_slopec,))
    t_ones = s.dve(lambda v: v.memset(onescol[:, 0:1], 1.0))
    t_onesr = s.dve(lambda v: v.memset(onesrow[0:1, :], 1.0))
    t_eps = s.dve(lambda v: v.memset(SM[:, 213:214], EPS_LN))
    t_nh = s.dve(lambda v: v.memset(SM[:, 214:215], -0.5))
    t_mg = s.dve(lambda v: v.memset(SM[:, 216:217], MAGIC))
    t_one = s.dve(lambda v: v.memset(SM[:, 217:218], 1.0))
    t_zero = s.dve(lambda v: v.memset(SM[:, 218:219], 0.0))
    t_kn0 = s.dve(lambda v: v.memset(rowf[0:1, 32:48], 0.0))

    t_wraw = s.pool(lambda g: g.iota(out=WalRaw, pattern=[[1, 9216]], base=0,
                                     channel_multiplier=-1,
                                     allow_small_or_imprecise_dtypes=True))
    t_wal = s.act(Wal, WalRaw, AF.Abs, bias=SM[:, 176:177], waits=(t_wraw, t_wofs))

    def ln_apply(xbuf, dst_bf, waits):
        waits = lat(*waits) if not isinstance(waits, list) else lat(*waits)
        t1 = s.dve(lambda v: v.bn_stats(out=SM[:, 184:190], in_=xbuf[:, 0:512]),
                   waits=waits)
        t2 = s.dve(lambda v: v.bn_stats(out=SM[:, 190:196], in_=xbuf[:, 512:1024]))
        t3 = s.dve(lambda v: v.bn_stats(out=SM[:, 196:202], in_=xbuf[:, 1024:1536]))
        t4 = s.dve(lambda v: v.bn_stats(out=SM[:, 202:208], in_=xbuf[:, 1536:2048]))
        t5 = s.dve(lambda v: v.bn_aggr(out=SM[:, 208:210], in_=SM[:, 184:208]),
                   waits=(t4,))
        t6 = s.act(SM[:, 210:211], SM[:, 209:210], AF.Ln, bias=SM[:, 213:214],
                   waits=(t5, t_eps))
        t7 = s.act(SM[:, 211:212], SM[:, 210:211], AF.Exp, scale=SM[:, 214:215],
                   waits=(t6, t_nh))
        t8 = s.dve(lambda v: v.scalar_tensor_tensor(out=SM[:, 212:213],
                                                    in0=SM[:, 208:209], scalar=-1.0,
                                                    in1=SM[:, 211:212],
                                                    op0=ALU.mult, op1=ALU.mult),
                   waits=(t7,))
        t9 = s.dve(lambda v: v.tensor_scalar(out=dst_bf, in0=xbuf,
                                             scalar1=SM[:, 211:212],
                                             scalar2=SM[:, 212:213],
                                             op0=ALU.mult, op1=ALU.add),
                   waits=(t8,))
        return t9

    last_apply = [None, None]
    t_xn_store = []
    for i in range(NJT):
        b = i % 2
        qeng = "sync" if b == 0 else "gpsimd"
        tl = s.dma(qeng, xslab[b], xf[sl(i, P), :], waits=(last_apply[b],))
        t9 = ln_apply(xslab[b], xnst[b], (tl, t_wal if i == 0 else None))
        last_apply[b] = t9
        t_xn_store.append(s.dma(qeng, xn_d[sl(i, P), :], xnst[b], waits=(t9,)))
    t_xno_store = []
    for i in range(8):
        b = i % 2
        qeng = "sync" if b == 0 else "gpsimd"
        tl = s.dma(qeng, xslab[b], xo[sl(i, P), :], waits=(last_apply[b],))
        t9 = ln_apply(xslab[b], xnst[b], (tl,))
        last_apply[b] = t9
        t_xno_store.append(s.dma(qeng, xno_d[sl(i, P), :], xnst[b], waits=(t9,)))
    t_xn_all = t_xn_store[-1]
    t_xno_all = t_xno_store[-1]

    t_cast_q = s.dma("gpsimd", wbf[0], Wq)
    t_cast_k = s.dma("gpsimd", wbf[1], Wk)

    def build_WT(src, waits):
        toks = []
        for ec in range(EC):
            toks.append(s.dma("sync", WTc[ec], src[:, sl(ec, P)],
                              waits=waits if ec == 0 else (), transpose=True))
        return toks

    t_wtq = build_WT(wbf[0], (t_cast_q, t_wal, t_xn_all))
    t_xnoT = []
    for ec in range(EC):
        t_xnoT.append(s.dma("sync", xnoTc[ec], xno_d[:, sl(ec, P)],
                            waits=(t_xno_all,) if ec == 0 else (), transpose=True))
    t_q_evac = {}
    t_qn_red = {}
    qpp_done = []
    for ft in range(H):
        b = ft % 2
        mm_half = []
        for half in range(2):
            pb = psb[b * 2 + half]
            mmt = None
            for ec in range(EC):
                w = []
                if ft < 2 and half == 0 and ec == 0:
                    w = [t_wtq[-1], t_xnoT[-1], t_ones]
                if ec == 0:
                    w = w + [t_q_evac.get((b, half))]
                mmt = s.mm(pb[:, 0:512], WTc[ec][:, sl(ft, P)],
                           xnoTc[ec][:, sl(half, 512)],
                           start=(ec == 0), stop=(ec == EC - 1), waits=w,
                           inc=(ec == EC - 1))
            mm_half.append(mmt)
        evs = []
        for half in range(2):
            pb = psb[b * 2 + half]
            ev = s.dve(lambda v, _b=b, _half=half, _pb=pb, _ft=ft: v.tensor_scalar(
                out=qh[_b][:, sl(_half, 512)], in0=_pb[:, 0:512],
                scalar1=SM[:, 32 + _ft:33 + _ft], scalar2=SM[:, 48 + _ft:49 + _ft],
                op0=ALU.mult, op1=ALU.add),
                waits=(mm_half[half], t_qbias))
            t_q_evac[(b, half)] = ev
            evs.append(ev)
        tq = s.dma("gpsimd", qpp_d[ft], qh[b], waits=evs)
        qpp_done.append(tq)
        tsq = s.act(sqst, qh[b][:, 0:512], AF.Square, waits=evs)
        pbq = psb[4 + b]
        tn1 = s.mm(pbq[0:1, 0:512], onescol[:, 0:1], sqst, start=True, stop=False,
                   waits=[tsq, t_ones, t_qn_red.get(b)], inc=True)
        tsq2 = s.act(sqst, qh[b][:, 512:1024], AF.Square, waits=(tn1,))
        tn2 = s.mm(pbq[0:1, 0:512], onescol[:, 0:1], sqst, start=False, stop=True,
                   waits=(tsq2,), inc=True)
        trd = s.dve(lambda v, _pb=pbq, _ft=ft: v.reduce_max(
            out=SM[0:1, 160 + _ft:161 + _ft], in_=_pb[0:1, 0:512], axis=AX),
            waits=(tn2,))
        t_qn_red[b] = trd
        qpp_done.append(trd)
    t_qpp_all = lat(*qpp_done)
    t_mq1 = s.dma("gpsimd", mqrow_d, SM[0:1, 160:176], waits=t_qpp_all)
    t_maxq2 = s.dma("gpsimd", SM[:, 160:176], mqrow_d.partition_broadcast(P),
                    waits=(t_mq1,))

    qunits = []
    for wsrc, qdst, widx in ((w1, q1_d, 0), (w2, q2_d, 1)):
        for i in range(NHT):
            for ch in range(2):
                qunits.append((wsrc[sl(i, P), sl(ch, TOK)],
                               qdst[sl(i, P), sl(ch, TOK)], widx, TOK))
    for i in range(EC):
        for cc in range(6):
            c0 = cc * TOK
            c1 = min((cc + 1) * TOK, HID)
            qunits.append((w3[sl(i, P), c0:c1], q3_d[sl(i, P), c0:c1], 2, c1 - c0))
    t_gz = s.pool(lambda g: g.memset(SM[:, 180:183], 0.0))
    QS = {"in": [None, None, None, None], "acc": t_gz, "t": None, "t2": None,
          "qs": [None, None], "rgam": None, "v0": None, "p1": 0, "p2": 0,
          "loaded": set(), "ld": {}}

    def q_load(i):
        if i >= len(qunits) or i in QS["loaded"]:
            return
        QS["loaded"].add(i)
        src_, dst_, widx, cols = qunits[i]
        b = i % NQIN
        eng = "gpsimd" if i % 2 == 0 else "scalar"
        QS["ld"][i] = s.dma(eng, qin[b][:, 0:cols], src_, waits=(QS["in"][b],))

    def emit_qpass1(n):
        while n > 0 and QS["p1"] < len(qunits):
            i = QS["p1"]
            QS["p1"] += 1
            n -= 1
            for la in range(NQIN):
                q_load(i + la)
            src_, dst_, widx, cols = qunits[i]
            b = i % NQIN
            tr = s.dve(lambda v, _b=b, _c=cols: v.tensor_reduce(
                out=SM[:, 215:216], in_=qin[_b][:, 0:_c], axis=AX, op=ALU.add,
                apply_absolute_value=True), waits=(QS["ld"][i], QS["acc"]))
            ta = s.dve(lambda v, _w=widx: v.tensor_tensor(
                out=SM[:, 180 + _w:181 + _w], in0=SM[:, 180 + _w:181 + _w],
                in1=SM[:, 215:216], op=ALU.add), waits=(tr,))
            QS["in"][b] = ta
            QS["acc"] = ta

    def emit_gamma_final():
        tw = s.dma("gpsimd", gsc_d, SM[:, 180:183], waits=(QS["acc"],))
        trd = s.dma("gpsimd", grow[0:1, 0:384],
                    gsc_d.rearrange("p c -> (p c)").unsqueeze(0), waits=(tw,))
        tred = s.dve(lambda v: v.tensor_reduce(
            out=rowf[0:1, 1088:1091], in_=grow[0:1, 0:384].rearrange("o (p c) -> o c p", c=3),
            axis=AX, op=ALU.add), waits=(trd,))
        tg = s.dve(lambda v: v.tensor_scalar(out=rowf[0:1, 1092:1095], in0=rowf[0:1, 1088:1091],
                                             scalar1=1.0 / QCOUNT, scalar2=EPS_Q,
                                             op0=ALU.mult, op1=ALU.add), waits=(tred,))
        tr = s.dve(lambda v: v.reciprocal(out=rowf[0:1, 1088:1091], in_=rowf[0:1, 1092:1095]),
                   waits=(tg,))
        tw2 = s.dma("gpsimd", gsc_d[0:1, 0:3], rowf[0:1, 1088:1091], waits=(tr,))
        tb = s.dma("gpsimd", SM[:, 177:180], gsc_d[0:1, 0:3].partition_broadcast(P),
                   waits=(tw2,))
        QS["rgam"] = tb
        QS["acc"] = tb
        QS["loaded"] = set()
        QS["ld"] = {}

    def emit_qpass2(n):
        while n > 0 and QS["p2"] < len(qunits):
            i = QS["p2"]
            QS["p2"] += 1
            n -= 1
            for la in range(NQIN):
                q_load(i + la)
            src_, dst_, widx, cols = qunits[i]
            b = i % NQIN
            d1 = s.dve(lambda v, _b=b, _c=cols, _w=widx: v.tensor_scalar(
                out=tb_a2[:, 0:_c], in0=qin[_b][:, 0:_c],
                scalar1=SM[:, 177 + _w:178 + _w], scalar2=SM[:, 216:217],
                op0=ALU.mult, op1=ALU.add),
                waits=(QS["ld"][i], QS["rgam"], QS["t"], QS["v0"]))
            d2 = s.dve(lambda v, _c=cols: v.tensor_scalar(
                out=tb_b2[:, 0:_c], in0=tb_a2[:, 0:_c], scalar1=MAGIC, scalar2=1.0,
                op0=ALU.subtract, op1=ALU.min), waits=(d1, QS["t2"]))
            QS["t"] = d2
            b2 = i % 2
            t3 = s.pool(lambda g, _b=b2, _c=cols: g.tensor_scalar(
                out=qst[_b][:, 0:_c], in0=tb_b2[:, 0:_c], scalar1=-1.0,
                scalar2=1.0, op0=ALU.max, op1=ALU.min), waits=(d2, QS["qs"][b2]))
            QS["t2"] = t3
            tso = s.dma("gpsimd", dst_, qst[b2][:, 0:cols], waits=(t3,))
            QS["qs"][b2] = tso
            QS["in"][b] = d1


    t_cast_v = s.dma("gpsimd", wbf[0], Wv, waits=(t_wtq[-1], t_qpp_all))
    t_wtk = build_WT(wbf[1], (t_cast_k, t_qpp_all))
    t_kev = {}
    kst_last = {}
    knorm_last = None
    t_kscr = []
    xnT_lastuse = [None, None]
    for sp in range(16):
        bsp = sp % 2
        tsp = []
        for ec in range(EC):
            w = (t_xn_all, xnT_lastuse[bsp]) if ec == 0 else ()
            tsp.append(s.dma("sync", xnTc[bsp][ec], xn_d[sl(sp, 512), sl(ec, P)],
                             waits=w, transpose=True))
        sp_last_mm = None
        for ft in range(H):
            pb = psb[ft % 2]
            mmt = None
            for ec in range(EC):
                w = []
                if ec == 0:
                    w = [tsp[-1], t_kev.get(ft % 2)]
                    if sp == 0 and ft == 0:
                        w.append(t_wtk[-1])
                mmt = s.mm(pb[:, 0:512], WTc[ec][:, sl(ft, P)], xnTc[bsp][ec],
                           start=(ec == 0), stop=(ec == EC - 1), waits=w,
                           inc=(ec == EC - 1))
            sp_last_mm = mmt
            bst = ft % 2
            ev = s.dve(lambda v, _bst=bst, _pb=pb, _ft=ft: v.tensor_scalar_add(
                out=kvst[_bst][:, 0:512], in0=_pb[:, 0:512],
                scalar1=SM[:, 80 + _ft:81 + _ft]),
                waits=(mmt, t_bkT, kst_last.get(bst), knorm_last))
            t_kev[ft % 2] = ev
            tst = s.dma("sync", kscr[ft][:, sl(sp, 512)], kvst[bst][:, 0:512],
                        waits=(ev,))
            kst_last[bst] = tst
            t_kscr.append(tst)
            tks = s.act(sqst, kvst[bst][:, 0:512], AF.Square, waits=(ev, knorm_last))
            tkm = s.mm(psb[6][0:1, 0:512], onescol[:, 0:1], sqst, start=True, stop=True,
                       waits=(tks, knorm_last), inc=True)
            tkr = s.dve(lambda v: v.reduce_max(out=rowf[0:1, 24:25],
                                               in_=psb[6][0:1, 0:512], axis=AX),
                        waits=(tkm,))
            knorm_last = s.dve(lambda v, _ft=ft: v.tensor_max(
                out=rowf[0:1, 32 + _ft:33 + _ft], in0=rowf[0:1, 32 + _ft:33 + _ft],
                in1=rowf[0:1, 24:25]), waits=(tkr, t_kn0))
        xnT_lastuse[bsp] = sp_last_mm
        emit_qpass1(17)
    emit_qpass1(len(qunits))
    emit_gamma_final()
    t_kscr_all = lat(*t_kscr)

    t_cast_o = s.dma("gpsimd", wbf[1], Wo, waits=(t_wtk[-1],))
    t_wtv = build_WT(wbf[0], [t_cast_v] + [xnT_lastuse[0], xnT_lastuse[1]])
    t_vev = {}
    vst_last = {}
    t_vscr = []
    xnT_lastuse2 = [None, None]
    for sp in range(16):
        bsp = sp % 2
        tsp = []
        for ec in range(EC):
            w = [xnT_lastuse2[bsp]] if ec == 0 else []
            if sp < 2 and ec == 0:
                w += [xnT_lastuse[0], xnT_lastuse[1], t_kscr_all]
            tsp.append(s.dma("sync", xnTc[bsp][ec], xn_d[sl(sp, 512), sl(ec, P)],
                             waits=w, transpose=True))
        sp_last_mm = None
        for tt4 in range(4):
            gt = sp * 4 + tt4
            for rnd in range(2):
                mm_of = {}
                for j, fs in enumerate((rnd * 2, rnd * 2 + 1)):
                    pb = psb[2 + j]
                    mmt = None
                    for ec in range(EC):
                        w = []
                        if ec == 0:
                            w = [tsp[-1], t_vev.get(j)]
                            if sp == 0 and tt4 == 0 and rnd == 0:
                                w.append(t_wtv[-1])
                        mmt = s.mm(pb[:, 0:512], xnTc[bsp][ec][:, sl(tt4, P)],
                                   WTc[ec][:, sl(fs, 512)],
                                   start=(ec == 0), stop=False, waits=w)
                    mmt = s.mm(pb[:, 0:512], onesrow[0:1, :],
                               rowbf[0:1, fs * 512:(fs + 1) * 512],
                               start=False, stop=True, waits=(t_onesr, t_bvrow),
                               inc=True)
                    mm_of[j] = (fs, mmt)
                for j in range(2):
                    fs, mmt = mm_of[j]
                    stb = 2 + j
                    ev = s.act(kvst[stb][:, 0:512], psb[2 + j][:, 0:512], AF.Copy,
                               waits=(mmt, vst_last.get(stb)))
                    t_vev[j] = ev
                    lastst = None
                    for hh in range(4):
                        hd = fs * 4 + hh
                        lastst = s.dma("sync", vscr[hd][sl(gt, P), :],
                                       kvst[stb][:, sl(hh, P)], waits=(ev,))
                        t_vscr.append(lastst)
                    vst_last[stb] = lastst
                    sp_last_mm = mmt
        xnT_lastuse2[bsp] = sp_last_mm
    t_vscr_all = lat(*t_vscr)
    QS["v0"] = lat(xnT_lastuse2[0], xnT_lastuse2[1])

    t_kload = [None, None]
    t_vload = [None, None]
    t_qload = [None, None]
    t_k_lastuse = [None, None]
    t_v_lastuse = [None, None]
    t_q_lastuse = [None, None]
    stps_read = {0: None, 1: None}
    t_pv_done = None
    t_sum_done = None
    t_at_ev = [None, None]
    t_emul_last = None
    attn_store = []

    t_kload[0] = s.dma("sync", ksb[0], kscr[0],
                       waits=(t_kscr_all, xnT_lastuse2[0], xnT_lastuse2[1],
                              t_vev.get(0), t_vev.get(1)))
    def load_v(hh, nb, waits):
        tks = []
        for jc in range(NJT):
            tks.append(s.dma("sync", vsb[nb][:, sl(jc, P)],
                             vscr[hh][sl(jc, P), :], waits=waits if jc == 0 else ()))
        return lat(*tks)

    t_vload[0] = load_v(0, 0, (t_vscr_all,))
    t_qload[0] = s.dma("gpsimd", qh[0], qpp_d[0],
                       waits=(t_q_evac.get((0, 0)), t_q_evac.get((0, 1)),
                              t_kscr_all, knorm_last, t_vscr_all))

    tc1 = s.dve(lambda v: v.tensor_mul(out=rowf[0:1, 48:64], in0=SM[0:1, 160:176],
                                       in1=rowf[0:1, 32:48]),
                waits=[t_maxq2, knorm_last])
    tc2 = s.act(rowf[0:1, 16:32], rowf[0:1, 48:64], AF.Ln, waits=(tc1,))
    tc3 = s.dve(lambda v: v.tensor_scalar_mul(out=rowf[0:1, 48:64],
                                              in0=rowf[0:1, 16:32], scalar1=0.5),
                waits=(tc2,))
    tc4 = s.act(rowf[0:1, 16:32], rowf[0:1, 48:64], AF.Exp, waits=(tc3,))
    tc5 = s.dve(lambda v: v.scalar_tensor_tensor(
        out=rowf[0:1, 48:64], in0=rowf[0:1, 16:32], scalar=-1.0,
        in1=SM[0:1, 0:16], op0=ALU.mult, op1=ALU.mult), waits=(tc4,))
    tc6 = s.dve(lambda v: v.tensor_scalar_add(out=rowf[0:1, 16:32],
                                              in0=rowf[0:1, 48:64],
                                              scalar1=-C_EXTRA), waits=(tc5,))
    t_nc1 = s.dma("gpsimd", mqrow_d, rowf[0:1, 16:32], waits=(tc6,))
    t_negC_all = s.dma("gpsimd", SM[:, 224:240], mqrow_d.partition_broadcast(P),
                       waits=(t_nc1,))

    for h in range(H):
        hb = h % 2
        t_egen = s.act(Eh, Wal, AF.Exp, scale=SM[:, 112 + h:113 + h],
                       waits=(t_negs, t_emul_last))
        t_negC = t_negC_all

        if h + 1 < H:
            nb = (h + 1) % 2
            t_kload[nb] = s.dma("sync", ksb[nb], kscr[h + 1],
                                waits=(t_k_lastuse[nb],))
            t_vload[nb] = load_v(h + 1, nb, (t_v_lastuse[nb],))
            t_qload[nb] = s.dma("gpsimd", qh[nb], qpp_d[h + 1],
                                waits=(t_q_lastuse[nb],
                                       t_kscr_all if h == 0 else None,
                                       knorm_last if h == 0 else None))

        st_tok = {}
        emul = {}

        def emit_st(jt, _h=h, _hb=hb):
            bb = jt % 2
            w = [stps_read[bb]]
            if jt < 2:
                w += [t_kload[_hb], t_qload[_hb]]
            s.mm(psb[bb * 2][:, 0:512], ksb[_hb][:, sl(jt, P)], qh[_hb][:, 0:512],
                 start=True, stop=True, waits=w)
            st_tok[jt] = s.mm(psb[bb * 2 + 1][:, 0:512], ksb[_hb][:, sl(jt, P)],
                              qh[_hb][:, 512:1024], start=True, stop=True, inc=True)

        emit_st(0)
        pv_last = None
        for jt in range(NJT):
            bb = jt % 2
            texp = s.act(p0sb[bb][:, 0:512], psb[bb * 2][:, 0:512], AF.Exp,
                         bias=SM[:, 224 + h:225 + h], scale=SM[:, h:h + 1],
                         waits=(st_tok[jt], t_negC, emul.get(jt - 2)))
            texp2 = s.act(p0sb[bb][:, 512:1024], psb[bb * 2 + 1][:, 0:512], AF.Exp,
                          bias=SM[:, 224 + h:225 + h], scale=SM[:, h:h + 1])
            stps_read[bb] = texp2
            if jt + 1 < NJT:
                emit_st(jt + 1)
            off = 8192 - 128 * jt
            tmul = s.dve(lambda v, _bb=bb, _o=off: v.tensor_tensor(
                out=ptsb[_bb], in0=p0sb[_bb], in1=Eh[:, _o:_o + 1024], op=ALU.mult),
                waits=(texp2, t_egen,
                       t_vscr_all if (h == 0 and jt < 2) else None))
            emul[jt] = tmul
            t_emul_last = tmul
            w_pv = [tmul]
            if jt == 0:
                w_pv += [t_vload[hb], t_pv_done, t_sum_done]
            s.mm(psb[4][:, 0:512], vsb[hb][:, sl(jt, P)], ptsb[bb][:, 0:512],
                 start=(jt == 0), stop=(jt == NJT - 1), waits=w_pv)
            s.mm(psb[5][:, 0:512], vsb[hb][:, sl(jt, P)], ptsb[bb][:, 512:1024],
                 start=(jt == 0), stop=(jt == NJT - 1))
            s.mm(psb[6][0:1, 0:512], onescol[:, 0:1], ptsb[bb][:, 0:512],
                 start=(jt == 0), stop=(jt == NJT - 1))
            pv_last = s.mm(psb[7][0:1, 0:512], onescol[:, 0:1], ptsb[bb][:, 512:1024],
                           start=(jt == 0), stop=(jt == NJT - 1), inc=True)
            if jt % 4 == 3:
                emit_qpass2(1)
        t_k_lastuse[hb] = st_tok[NJT - 1]
        t_q_lastuse[hb] = st_tok[NJT - 1]
        t_v_lastuse[hb] = pv_last
        tse = s.act(rowf[0:1, 64:576], psb[6][0:1, 0:512], AF.Copy, waits=(pv_last,))
        tse2 = s.act(rowf[0:1, 576:1088], psb[7][0:1, 0:512], AF.Copy)
        trc = s.dve(lambda v: v.reciprocal(out=rowf[0:1, 1088:2112],
                                           in_=rowf[0:1, 64:1088]), waits=(tse2,))
        trcb = s.dve(lambda v: v.tensor_copy(out=rowbf[0:1, 4096:5120],
                                             in_=rowf[0:1, 1088:2112]), waits=(trc,))
        tb1 = s.mm(psb[2][:, 0:512], onesrow[0:1, :], rowbf[0:1, 4096:4608],
                   start=True, stop=True, waits=(trcb, stps_read[1]))
        tb2 = s.mm(psb[3][:, 0:512], onesrow[0:1, :], rowbf[0:1, 4608:5120],
                   start=True, stop=True, inc=True)
        trb = s.act(rbsb[:, 0:512], psb[2][:, 0:512], AF.Copy, waits=(tb2,))
        trb2 = s.act(rbsb[:, 512:1024], psb[3][:, 0:512], AF.Copy)
        stps_read[1] = trb2
        ab = h % 2
        tat = s.dve(lambda v, _ab=ab: v.tensor_tensor(out=atst[_ab][:, 0:512],
                                                      in0=psb[4][:, 0:512],
                                                      in1=rbsb[:, 0:512],
                                                      op=ALU.mult),
                    waits=(trb2, pv_last, t_at_ev[ab],
                           t_vscr_all if h < 2 else None))
        tat2 = s.dve(lambda v, _ab=ab: v.tensor_tensor(out=atst[_ab][:, 512:1024],
                                                       in0=psb[5][:, 0:512],
                                                       in1=rbsb[:, 512:1024],
                                                       op=ALU.mult))
        t_pv_done = tat2
        t_sum_done = tse2
        tats = s.dma("gpsimd", attn_d[h], atst[ab], waits=(tat2,))
        t_at_ev[ab] = tats
        attn_store.append(tats)
        emit_qpass2(1)
    emit_qpass2(len(qunits))
    t_quant_all = s.pool(lambda g: g.nop(),
                         waits=(QS["qs"][0], QS["qs"][1], QS["t2"]))

    HH = HID // 2
    ffn_t = []
    units_t = []
    for (qsrc, tdst) in ((q1_d, w1t_d), (q2_d, w2t_d)):
        for dc in range(EC):
            for ch in range(2):
                units_t.append((qsrc[sl(ch, HH), sl(dc, P)],
                                tdst[sl(dc, P), sl(ch, HH)]))
    wr_prev = [None, None]
    for i, (srcu, dstu) in enumerate(units_t):
        b = i % 2
        w = [wr_prev[b]]
        if i < 2:
            w.append(t_quant_all)
        ttr = s.dma("sync", wstg2[b][:, 0:HH], srcu, waits=w, transpose=True)
        wr_prev[b] = s.dma("sync", dstu, wstg2[b][:, 0:HH], waits=(ttr,))
        ffn_t.append(wr_prev[b])
    t_ffnw_all = lat(*ffn_t)

    t_wto = build_WT(wbf[1], [t_cast_o, stps_read[0], stps_read[1], t_pv_done,
                              t_quant_all, t_emul_last])
    t_atch = [None, None]
    t_p4ev = {}
    t_xres_store = []
    t_xn2_store = []
    xslab_last = [last_apply[0], last_apply[1]]
    xrst_last = [None, None]
    xn2st_last = [None, None]
    p4items = [(tt_, ec_) for tt_ in range(8) for ec_ in range(EC)]
    tld_tok = {}

    def emit_tld(k):
        if k >= len(p4items) or k in tld_tok:
            return
        tt_, ec_ = p4items[k]
        w = [t_atch[0], t_atch[1]]
        if tt_ == 0:
            w.append(attn_store[ec_])
        tld_tok[k] = s.dma("sync", atch[k % 2][:, 0:TOK], attn_d[ec_], waits=w)

    for tt in range(8):
        bx = tt % 2
        txo = s.dma("sync", xslab[bx], xo[sl(tt, P), :], waits=(xslab_last[bx],))
        mm_fs = {}
        for ec in range(EC):
            k = tt * EC + ec
            bc = k % 2
            emit_tld(k)
            emit_tld(k + 1)
            for fs in range(4):
                w2_ = [tld_tok[k]]
                if ec == 0:
                    w2_ += [t_wto[-1], t_p4ev.get(fs)]
                mm_fs[fs] = s.mm(psb[4 + fs][:, 0:512], atch[bc][:, sl(tt, P)],
                                 WTc[ec][:, sl(fs, 512)],
                                 start=(ec == 0), stop=False, waits=w2_,
                                 inc=(fs == 3))
            t_atch[bc] = mm_fs[3]
        for fs in range(4):
            mm_fs[fs] = s.mm(psb[4 + fs][:, 0:512], onesrow[0:1, :],
                             rowbf[0:1, E + fs * 512:E + (fs + 1) * 512],
                             start=False, stop=True, waits=(t_borow,), inc=True)
        tt_ev = []
        for fs in range(4):
            ev = s.dve(lambda v, _bx=bx, _fs=fs: v.tensor_tensor(
                out=xrst[_bx][:, sl(_fs, 512)], in0=psb[4 + _fs][:, 0:512],
                in1=xslab[_bx][:, sl(_fs, 512)], op=ALU.add),
                waits=(mm_fs[fs], txo, xrst_last[bx]))
            t_p4ev[fs] = ev
            tt_ev.append(ev)
        t9 = ln_apply(xrst[bx], xn2st[bx], tt_ev + [xn2st_last[bx]])
        ts1 = s.dma("sync", xres_d[sl(tt, P), :], xrst[bx], waits=(t9,))
        ts2 = s.dma("sync", xn2_d[sl(tt, P), :], xn2st[bx], waits=(t9,))
        xrst_last[bx] = ts1
        xn2st_last[bx] = ts2
        xslab_last[bx] = t_p4ev[3]
        t_xres_store.append(ts1)
        t_xn2_store.append(ts2)
    t_xres_all = lat(*t_xres_store)
    t_xn2_all = lat(*t_xn2_store)

    t_xn2T = []
    for ec in range(EC):
        w = [t_xn2_all, t_p4ev[3]] if ec == 0 else ()
        t_xn2T.append(s.dma("sync", xn2Tc[ec], xn2_d[:, sl(ec, P)], waits=w,
                            transpose=True))
    t_out_final = []
    f1_psum_free = [stps_read[0], stps_read[0], stps_read[1], stps_read[1],
                    t_p4ev[0], t_p4ev[1], t_p4ev[2], t_p4ev[3]]
    w1sl_last = [None, None]
    hid_last = lat(t_xres_all, t_xn2_all)
    f2_done_prev = None
    for th in range(2):
        hs_ev = []
        for hs in range(11):
            tw1 = s.dma("sync", w1sl[0].rearrange("p (ec c) -> p ec c", c=512),
                        w1t_d[:, sl(hs, 512)].rearrange("(ec p) c -> p ec c", p=P),
                        waits=[t_ffnw_all, w1sl_last[0]] if hs == 0 else (w1sl_last[0],))
            tw2 = s.dma("sync", w1sl[1].rearrange("p (ec c) -> p ec c", c=512),
                        w2t_d[:, sl(hs, 512)].rearrange("(ec p) c -> p ec c", p=P),
                        waits=[t_ffnw_all, w1sl_last[1]] if hs == 0 else (w1sl_last[1],))
            mm1 = {}
            mm2 = {}
            for dc in range(EC):
                for tt in range(4):
                    w = []
                    if dc == 0:
                        w = [tw1, f1_psum_free[tt],
                             t_xn2T[-1] if hs == 0 and th == 0 else None,
                             f2_done_prev if hs == 0 else None]
                    mm1[tt] = s.mm(psb[tt][:, 0:512],
                                   xn2Tc[dc][:, th * 512 + tt * 128:th * 512 + tt * 128 + 128],
                                   w1sl[0][:, sl(dc, 512)],
                                   start=(dc == 0), stop=(dc == EC - 1), waits=w,
                                   inc=(dc == EC - 1))
                    w = []
                    if dc == 0:
                        w = [tw2, f1_psum_free[4 + tt]]
                    mm2[tt] = s.mm(psb[4 + tt][:, 0:512],
                                   xn2Tc[dc][:, th * 512 + tt * 128:th * 512 + tt * 128 + 128],
                                   w1sl[1][:, sl(dc, 512)],
                                   start=(dc == 0), stop=(dc == EC - 1), waits=w,
                                   inc=(dc == EC - 1))
            last_w1_mm = mm2[3]
            w1sl_last[0] = last_w1_mm
            w1sl_last[1] = last_w1_mm
            for tt in range(4):
                bsg = tt % 2
                tsg = s.act(sgst[bsg], psb[tt][:, 0:512], AF.Sigmoid,
                            waits=(mm1[tt], f1_psum_free[(tt + 3) % 4]))
                f1_psum_free[tt] = tsg
                thd = s.dve(lambda v, _t=tt, _hs=hs, _b=bsg: v.tensor_tensor(
                    out=hid_tt[_t][:, _hs * 512:_hs * 512 + 512],
                    in0=sgst[_b], in1=psb[4 + _t][:, 0:512], op=ALU.mult),
                    waits=(tsg, mm2[tt], hid_last))
                f1_psum_free[4 + tt] = thd
                hs_ev.append(thd)
        thw_parts = []
        for _t in range(4):
            thw_parts.append(s.dma("sync", hid_d[sl(_t, P), :], hid_tt[_t],
                                   waits=hs_ev if _t == 0 else ()))
        thw = lat(*thw_parts)
        hid_last = thw
        t_hT = []
        for hc in range(NHT):
            w = [thw] if hc == 0 else ()
            t_hT.append(s.dma("sync", hidTc[hc], hid_d[:, sl(hc, P)], waits=w,
                              transpose=True))
        q3_last = [None, None]
        f2_mm = {}
        for fh in range(2):
            for hc in range(NHT):
                bq3 = hc % 2
                w = [q3_last[bq3]]
                if fh == 0 and hc == 0:
                    w += [t_ffnw_all]
                if fh == 0 and hc == 0:
                    w += [t_at_ev[0], t_at_ev[1]]
                tq3 = s.dma("sync", q3ch[bq3], q3_d[sl(fh, TOK), sl(hc, P)], waits=w,
                            transpose=True)
                for tt in range(4):
                    w2_ = [tq3, t_hT[-1] if hc == 0 else None]
                    if hc == 0:
                        w2_.append(f2_mm.get((1 - fh, tt)))
                        w2_.append(f1_psum_free[2 * tt])
                        w2_.append(f1_psum_free[2 * tt + 1])
                    s.mm(psb[2 * tt][:, 0:512], hidTc[hc][:, sl(tt, P)],
                         q3ch[bq3][:, 0:512], start=(hc == 0), stop=(hc == NHT - 1),
                         waits=w2_)
                    f2_mm[(fh, tt)] = s.mm(psb[2 * tt + 1][:, 0:512],
                                           hidTc[hc][:, sl(tt, P)],
                                           q3ch[bq3][:, 512:1024],
                                           start=(hc == 0), stop=(hc == NHT - 1),
                                           waits=(),
                                           inc=(hc == NHT - 1 or tt == 3))
                q3_last[bq3] = f2_mm[(fh, 3)]
            for tt in range(4):
                bxr = tt % 2
                bo2 = tt % 2
                txr = s.dma("sync", xrtile[bxr], xres_d[sl(th * 4 + tt, P), sl(fh, TOK)],
                            waits=(t_xres_all if fh == 0 and tt < 2 and th == 0 else None,
                                   t_ffnw_all if fh == 0 and tt < 2 and th == 0 else None,
                                   t_p4ev.get(("xr", bxr))))
                e1 = s.dve(lambda v, _tt=tt, _b=bxr, _o=bo2: v.tensor_tensor(
                    out=outst2[_o][:, 0:512], in0=psb[2 * _tt][:, 0:512],
                    in1=xrtile[_b][:, 0:512], op=ALU.add),
                    waits=(f2_mm[(fh, tt)], txr, t_p4ev.get(("out", bo2))))
                e2 = s.dve(lambda v, _tt=tt, _b=bxr, _o=bo2: v.tensor_tensor(
                    out=outst2[_o][:, 512:1024], in0=psb[2 * _tt + 1][:, 0:512],
                    in1=xrtile[_b][:, 512:1024], op=ALU.add))
                t_p4ev[("xr", bxr)] = e2
                f1_psum_free[2 * tt] = e2
                f1_psum_free[2 * tt + 1] = e2
                tso = s.dma_out("gpsimd", yout[sl(th * 4 + tt, P), sl(fh, TOK)],
                                outst2[bo2], waits=(e2,))
                t_p4ev[("out", bo2)] = tso
                t_out_final.append(tso)
                f2_done_prev = lat(t_p4ev.get(("xr", 0)), t_p4ev.get(("xr", 1)))
        if th == 0:
            hid_last = lat(hid_last, t_p4ev.get(("out", 0)), t_p4ev.get(("out", 1)))

    s.fin("gpsimd", lat(*t_out_final))
    s.fin("sync", lat(*t_out_final))

    return nc, s


def finalize(nc, s):
    sems = {}
    ctxs = []
    for name in SEM_NAMES:
        cm = nc.semaphore(name)
        sems[name] = cm.__enter__()
        ctxs.append(cm)
    blk = nc.Block()
    block = blk.__enter__()
    block.sync(lambda e: s.emit(e, "sync", sems))
    block.gpsimd(lambda e: s.emit(e, "gpsimd", sems))
    block.vector(lambda e: s.emit(e, "vector", sems))
    block.scalar(lambda e: s.emit(e, "scalar", sems))
    block.tensor(lambda e: s.emit(e, "tensor", sems))
    blk.__exit__(None, None, None)
    for cm in reversed(ctxs):
        cm.__exit__(None, None, None)
    return nc


def build(dbg=0):
    nc, s = build_program(dbg=dbg)
    return finalize(nc, s)


def make_inputs(inputs, core):
    x = np.asarray(inputs["L0_attn_x"]).reshape(T, E)
    im = {
        "x_full": np.ascontiguousarray(x),
        "x_own": np.ascontiguousarray(x[core * TOK:(core + 1) * TOK]),
        "Wq": np.asarray(inputs["L0_attn_Wq"]),
        "Wk": np.asarray(inputs["L0_attn_Wk"]),
        "Wv": np.asarray(inputs["L0_attn_Wv"]),
        "Wo": np.asarray(inputs["L0_attn_Wo"]),
        "bq": np.asarray(inputs["L0_attn_bq"]).reshape(1, E),
        "bk": np.asarray(inputs["L0_attn_bk"]).reshape(1, E),
        "bv": np.asarray(inputs["L0_attn_bv"]).reshape(1, E),
        "bo": np.asarray(inputs["L0_attn_bo"]).reshape(1, E),
        "slopes": np.asarray(inputs["L0_attn_slopes"]).reshape(1, H),
        "w1": np.asarray(inputs["L0_ffn_w1"]),
        "w2": np.asarray(inputs["L0_ffn_w2"]),
        "w3": np.asarray(inputs["L0_ffn_w3"]),
        "wofs": np.full((P, 1), core * TOK - 8192, np.float32),
    }
    return im


_NC_CACHE = {}


def kernel(**inputs):
    from concourse.bass_utils import run_bass_kernel_spmd
    if "nc" not in _NC_CACHE:
        _NC_CACHE["nc"] = build(dbg=0)
    nc = _NC_CACHE["nc"]
    ims = [make_inputs(inputs, c) for c in range(8)]
    res = run_bass_kernel_spmd(nc, ims, core_ids=list(range(8)))
    return np.concatenate([res.results[c]["y"] for c in range(8)], axis=0).reshape(1, T, E)


# version 09
# speedup vs baseline: 1.2627x; 1.2627x over previous
import numpy as np
import concourse.bass as bass
import concourse.mybir as mybir
from bass_rust import ReduceOp

F32 = mybir.dt.float32
F16 = mybir.dt.float16
BF16 = mybir.dt.bfloat16
AF = mybir.ActivationFunctionType
ALU = mybir.AluOpType
AX = mybir.AxisListType.X

P = 128
T = 8192
TOK = 1024
E = 2048
H = 16
D = 128
HID = 5632
EC = E // P
NJT = T // P
NHT = HID // P
SCALING = float(D) ** -0.5
EPS_LN = 1e-5
EPS_Q = 1e-8
MAGIC = 1.5 * 2 ** 23
C_EXTRA = 2.0
NQIN = 4
QCOUNT = float(HID * E)

ENGS = ("sync", "gpsimd", "vector", "scalar", "tensor")
SEM_NAMES = ["qs", "qg", "qa", "pe", "act", "dve", "pool", "outq"]
QSEM = {"sync": "qs", "gpsimd": "qg", "scalar": "qa", "vector": "qa"}
DMA_SEMS = {"qs", "qg", "qa", "outq"}


def sl(i, n):
    return slice(i * n, (i + 1) * n)


def lat(*toks):
    best = {}
    for t in toks:
        if t is None:
            continue
        items = t if isinstance(t, list) else [t]
        for u in items:
            if u is None:
                continue
            s2, v2 = u
            if best.get(s2, -1) < v2:
                best[s2] = v2
    return [(k, v) for k, v in best.items()]


class Sched:
    def __init__(self):
        self.ops = {e: [] for e in ENGS}
        self.cnt = {k: 0 for k in SEM_NAMES}

    def _op(self, eng, fn, waits, inc, amt):
        ws = lat(*waits) if waits else []
        ws = [(sn, min(v + 32, self.cnt[sn]) if sn in DMA_SEMS else v)
              for (sn, v) in ws]
        tok = None
        if inc is not None:
            self.cnt[inc] += amt
            tok = (inc, self.cnt[inc])
        self.ops[eng].append((fn, ws, inc, amt))
        return tok

    def dma(self, eng, out, in_, waits=(), **kw):
        return self._op(eng, lambda e, o=out, i=in_, k=kw: e.dma_start(out=o, in_=i, **k),
                        waits, QSEM[eng], 16)

    def dma_out(self, eng, out, in_, waits=(), **kw):
        return self._op(eng, lambda e, o=out, i=in_, k=kw: e.dma_start(out=o, in_=i, **k),
                        waits, "outq", 16)

    def mm(self, out, lhsT, rhs, start, stop, waits=(), inc=False):
        return self._op("tensor",
                        lambda e, o=out, l=lhsT, r=rhs, st=start, sp=stop:
                        e.matmul(o, l, r, start=st, stop=sp),
                        waits, "pe" if inc else None, 1)

    def act(self, out, in_, func, bias=None, scale=None, accum_out=None, waits=()):
        def fn(e, o=out, i=in_, f=func, b=bias, sc=scale, ao=accum_out):
            kw = {}
            if b is not None:
                kw["bias"] = b
            if sc is not None:
                kw["scale"] = sc
            if ao is not None:
                kw["accum_out"] = ao
            return e.activation(out=o, in_=i, func=f, **kw)
        return self._op("scalar", fn, waits, "act", 1)

    def dve(self, fn, waits=()):
        return self._op("vector", fn, waits, "dve", 1)

    def pool(self, fn, waits=()):
        return self._op("gpsimd", fn, waits, "pool", 1)

    def fin(self, eng, waits):
        return self._op(eng, lambda e: e.nop(), waits, None, 1)

    def emit(self, eng_obj, eng_name, sems):
        hw = {}
        for fn, waits, inc, amt in self.ops[eng_name]:
            for (sname, v) in waits:
                if hw.get(sname, 0) < v:
                    eng_obj.wait_ge(sems[sname], v)
                    hw[sname] = v
            inst = fn(eng_obj)
            if inc is not None:
                inst.then_inc(sems[inc], amt)


def build_program(dbg=0):
    nc = bass.Bass("TRN2", target_bir_lowering=False, debug=False)
    s = Sched()
    dk = "ExternalOutput" if dbg else "Internal"

    xf = nc.dram_tensor("x_full", [T, E], F32, kind="ExternalInput").ap()
    xo = nc.dram_tensor("x_own", [TOK, E], F32, kind="ExternalInput").ap()
    Wq = nc.dram_tensor("Wq", [E, E], F32, kind="ExternalInput").ap()
    Wk = nc.dram_tensor("Wk", [E, E], F32, kind="ExternalInput").ap()
    Wv = nc.dram_tensor("Wv", [E, E], F32, kind="ExternalInput").ap()
    Wo = nc.dram_tensor("Wo", [E, E], F32, kind="ExternalInput").ap()
    bqv = nc.dram_tensor("bq", [1, E], F32, kind="ExternalInput").ap()
    bkv = nc.dram_tensor("bk", [1, E], F32, kind="ExternalInput").ap()
    bvv = nc.dram_tensor("bv", [1, E], F32, kind="ExternalInput").ap()
    bov = nc.dram_tensor("bo", [1, E], F32, kind="ExternalInput").ap()
    slopes = nc.dram_tensor("slopes", [1, H], F32, kind="ExternalInput").ap()
    w1 = nc.dram_tensor("w1", [HID, E], F32, kind="ExternalInput").ap()
    w2 = nc.dram_tensor("w2", [HID, E], F32, kind="ExternalInput").ap()
    w3 = nc.dram_tensor("w3", [E, HID], F32, kind="ExternalInput").ap()
    wofs = nc.dram_tensor("wofs", [P, 1], F32, kind="ExternalInput").ap()
    yout = nc.dram_tensor("y", [TOK, E], F32, kind="ExternalOutput").ap()

    xn_d = nc.dram_tensor("xn_d", [T, E], BF16, kind=dk).ap()
    xno_d = nc.dram_tensor("xno_d", [TOK, E], BF16, kind="Internal").ap()
    kscr = nc.dram_tensor("kscr", [H, P, T], BF16, kind=dk).ap()
    vscr = nc.dram_tensor("vscr", [H, T, D], BF16, kind=dk).ap()
    qpp_d = nc.dram_tensor("qpp_d", [H, P, TOK], BF16, kind=dk).ap()
    attn_d = nc.dram_tensor("attn_d", [H, P, TOK], BF16, kind=dk).ap()
    wbf = [nc.dram_tensor(f"wbf{i}", [E, E], BF16, kind="Internal").ap() for i in range(2)]
    q1_d = nc.dram_tensor("q1_d", [HID, E], BF16, kind=dk).ap()
    q2_d = nc.dram_tensor("q2_d", [HID, E], BF16, kind="Internal").ap()
    q3_d = nc.dram_tensor("q3_d", [E, HID], BF16, kind=dk).ap()
    w1t_d = nc.dram_tensor("w1t_d", [E, HID], BF16, kind="Internal").ap()
    w2t_d = nc.dram_tensor("w2t_d", [E, HID], BF16, kind="Internal").ap()
    q3t_d = nc.dram_tensor("q3t_d", [HID, E], BF16, kind="Internal").ap()
    xres_d = nc.dram_tensor("xres_d", [TOK, E], F32, kind=dk).ap()
    xn2_d = nc.dram_tensor("xn2_d", [TOK, E], BF16, kind=dk).ap()
    hid_d = nc.dram_tensor("hid_d", [TOK // 2, HID], BF16, kind=dk).ap()
    mqrow_d = nc.dram_tensor("mqrow_d", [1, H], F32, kind="Internal").ap()
    negc_d = nc.dram_tensor("negc_d", [H, 1, 1], F32, kind="Internal").ap()
    gsc_d = nc.dram_tensor("gsc_d", [P, 3], F32, kind="Internal").ap()

    SM = nc.alloc_sbuf_tensor("SM", [P, 256], F32).ap()
    rowf = nc.alloc_sbuf_tensor("rowf", [1, 2112], F32).ap()
    rowbf = nc.alloc_sbuf_tensor("rowbf", [1, 5120], BF16).ap()
    onescol = nc.alloc_sbuf_tensor("onescol", [P, 2], BF16).ap()
    onesrow = nc.alloc_sbuf_tensor("onesrow", [1, 128], BF16).ap()
    grow = nc.alloc_sbuf_tensor("grow", [1, 384], F32).ap()

    rem = nc.sbuf_bytes_remaining - 512
    arena = nc.alloc_sbuf_tensor("arena", [P, rem], mybir.dt.uint8).ap()
    assert rem >= 190464, f"arena too small: {rem}"

    def reg(off, cols, dtype):
        esz = {F32: 4, BF16: 2, F16: 2}[dtype]
        assert off + cols * esz <= rem, (off, cols)
        return arena[:, off:off + cols * esz].bitcast(dtype)

    xslab = [reg(i * 8192, E, F32) for i in range(2)]
    kvst = [reg(i * 4096, E, BF16) for i in range(4)]
    qh = [reg(i * 2048, TOK, BF16) for i in range(2)]
    p0sb = [reg(4096 + i * 2048, TOK, BF16) for i in range(2)]
    ptsb = [reg(8192 + i * 2048, TOK, BF16) for i in range(2)]
    atst = [reg(12288 + i * 2048, TOK, BF16) for i in range(2)]
    xnst = [reg(16384 + i * 4096, E, BF16) for i in range(2)]
    atch = [reg(16384 + i * 2048, TOK, BF16) for i in range(2)]
    sgst = [reg(20480 + i * 1024, 512, BF16) for i in range(2)]
    q3ch = [reg(16384 + i * 2048, TOK, BF16) for i in range(2)]
    WTc = [reg(24576 + ec * E * 2, E, BF16) for ec in range(EC)]
    WalRaw = reg(24576, 9216, F32)
    qin = [reg(174080 + i * 4096, TOK, F32) for i in range(4)]
    tb_a2 = reg(108544, TOK, F32)
    tb_b2 = reg(112640, TOK, BF16)
    qst = [reg(114688 + i * 2048, TOK, BF16) for i in range(2)]
    Eh = reg(43008, 9216, BF16)
    rbsb = reg(61440, TOK, F32)
    xn2Tc = [reg(24576 + ec * TOK * 2, TOK, BF16) for ec in range(EC)]
    w1sl = [reg(57344 + i * 16384, EC * 512, BF16) for i in range(2)]
    Wal = reg(90112, 9216, F16)
    xnoTc = [reg(108544 + ec * 2048, TOK, BF16) for ec in range(EC)]
    ksb = [reg(108544 + b * 16384, T, BF16) for b in range(2)]
    xrst = [reg(108544 + i * 8192, E, F32) for i in range(2)]
    xn2st = [reg(124928 + i * 4096, E, BF16) for i in range(2)]
    hid_tt = [reg(108544 + i * 11264, HID, BF16) for i in range(4)]
    hidTc = [reg(108544 + hc * 1024, 512, BF16) for hc in range(NHT)]
    xnTc = [[reg(141312 + b * 16384 + ec * 1024, 512, BF16) for ec in range(EC)]
            for b in range(2)]
    vsb = [reg(141312 + b * 16384, T, BF16) for b in range(2)]
    outst2 = [reg(174080 + i * 4096, TOK, F32) for i in range(2)]
    xrtile = [reg(182272 + i * 4096, TOK, F32) for i in range(2)]
    wstg2 = [reg(174080 + b * 8192, 4096, BF16) for b in range(2)]
    sqst = reg(20480, 512, BF16)

    psb = [nc.alloc_psum_tensor(f"pb{i}", [P, 512], F32) for i in range(8)]

    t_slopes = s.dma("sync", rowf[0:1, 0:H], slopes)
    t_wofs = s.dma("sync", SM[:, 176:177], wofs)
    t_bvrow = s.dma("gpsimd", rowbf[0:1, 0:E], bvv)
    t_borow = s.dma("gpsimd", rowbf[0:1, E:2 * E], bov)
    t_bqT = s.dma("gpsimd", SM[:, 64:80], bqv.rearrange("o (c p) -> p (o c)", p=P),
                  allow_slow_non_contiguous=True)
    t_bkT = s.dma("gpsimd", SM[:, 80:96], bkv.rearrange("o (c p) -> p (o c)", p=P),
                  allow_slow_non_contiguous=True)
    t_bvT = s.dma("gpsimd", SM[:, 96:112], bvv.rearrange("o (c p) -> p (o c)", p=P),
                  allow_slow_non_contiguous=True)

    t_slopec = s.dma("sync", SM[:, 0:16], slopes.partition_broadcast(P))
    t_rslope = s.dve(lambda v: v.reciprocal(out=SM[:, 16:32], in_=SM[:, 0:16]),
                     waits=(t_slopec,))
    t_sfold = s.dve(lambda v: v.tensor_scalar_mul(out=SM[:, 32:48], in0=SM[:, 16:32],
                                                  scalar1=SCALING), waits=(t_rslope,))
    t_qbias = s.dve(lambda v: v.tensor_mul(out=SM[:, 48:64], in0=SM[:, 64:80],
                                           in1=SM[:, 32:48]), waits=(t_sfold, t_bqT))
    t_negs = s.dve(lambda v: v.tensor_scalar_mul(out=SM[:, 112:128], in0=SM[:, 0:16],
                                                 scalar1=-1.0), waits=(t_slopec,))
    t_ones = s.dve(lambda v: v.memset(onescol[:, 0:1], 1.0))
    t_onesr = s.dve(lambda v: v.memset(onesrow[0:1, :], 1.0))
    t_eps = s.dve(lambda v: v.memset(SM[:, 213:214], EPS_LN))
    t_nh = s.dve(lambda v: v.memset(SM[:, 214:215], -0.5))
    t_mg = s.dve(lambda v: v.memset(SM[:, 216:217], MAGIC))
    t_one = s.dve(lambda v: v.memset(SM[:, 217:218], 1.0))
    t_zero = s.dve(lambda v: v.memset(SM[:, 218:219], 0.0))
    t_kn0 = s.dve(lambda v: v.memset(rowf[0:1, 32:48], 0.0))

    t_wraw = s.pool(lambda g: g.iota(out=WalRaw, pattern=[[1, 9216]], base=0,
                                     channel_multiplier=-1,
                                     allow_small_or_imprecise_dtypes=True))
    t_wal = s.act(Wal, WalRaw, AF.Abs, bias=SM[:, 176:177], waits=(t_wraw, t_wofs))

    def ln_apply(xbuf, dst_bf, waits):
        waits = lat(*waits) if not isinstance(waits, list) else lat(*waits)
        t1 = s.dve(lambda v: v.bn_stats(out=SM[:, 184:190], in_=xbuf[:, 0:512]),
                   waits=waits)
        t2 = s.dve(lambda v: v.bn_stats(out=SM[:, 190:196], in_=xbuf[:, 512:1024]))
        t3 = s.dve(lambda v: v.bn_stats(out=SM[:, 196:202], in_=xbuf[:, 1024:1536]))
        t4 = s.dve(lambda v: v.bn_stats(out=SM[:, 202:208], in_=xbuf[:, 1536:2048]))
        t5 = s.dve(lambda v: v.bn_aggr(out=SM[:, 208:210], in_=SM[:, 184:208]),
                   waits=(t4,))
        t6 = s.act(SM[:, 210:211], SM[:, 209:210], AF.Ln, bias=SM[:, 213:214],
                   waits=(t5, t_eps))
        t7 = s.act(SM[:, 211:212], SM[:, 210:211], AF.Exp, scale=SM[:, 214:215],
                   waits=(t6, t_nh))
        t8 = s.dve(lambda v: v.scalar_tensor_tensor(out=SM[:, 212:213],
                                                    in0=SM[:, 208:209], scalar=-1.0,
                                                    in1=SM[:, 211:212],
                                                    op0=ALU.mult, op1=ALU.mult),
                   waits=(t7,))
        t9 = s.dve(lambda v: v.tensor_scalar(out=dst_bf, in0=xbuf,
                                             scalar1=SM[:, 211:212],
                                             scalar2=SM[:, 212:213],
                                             op0=ALU.mult, op1=ALU.add),
                   waits=(t8,))
        return t9

    last_apply = [None, None]
    t_xn_store = []
    for i in range(NJT):
        b = i % 2
        qeng = "sync" if b == 0 else "gpsimd"
        tl = s.dma(qeng, xslab[b], xf[sl(i, P), :], waits=(last_apply[b],))
        t9 = ln_apply(xslab[b], xnst[b], (tl, t_wal if i == 0 else None))
        last_apply[b] = t9
        t_xn_store.append(s.dma(qeng, xn_d[sl(i, P), :], xnst[b], waits=(t9,)))
    t_xno_store = []
    for i in range(8):
        b = i % 2
        qeng = "sync" if b == 0 else "gpsimd"
        tl = s.dma(qeng, xslab[b], xo[sl(i, P), :], waits=(last_apply[b],))
        t9 = ln_apply(xslab[b], xnst[b], (tl,))
        last_apply[b] = t9
        t_xno_store.append(s.dma(qeng, xno_d[sl(i, P), :], xnst[b], waits=(t9,)))
    t_xn_all = t_xn_store[-1]
    t_xno_all = t_xno_store[-1]

    t_cast_q = s.dma("gpsimd", wbf[0], Wq)
    t_cast_k = s.dma("gpsimd", wbf[1], Wk)

    def build_WT(src, waits):
        toks = []
        for ec in range(EC):
            toks.append(s.dma("sync", WTc[ec], src[:, sl(ec, P)],
                              waits=waits if ec == 0 else (), transpose=True))
        return toks

    t_wtq = build_WT(wbf[0], (t_cast_q, t_wal, t_xn_all))
    t_xnoT = []
    for ec in range(EC):
        t_xnoT.append(s.dma("sync", xnoTc[ec], xno_d[:, sl(ec, P)],
                            waits=(t_xno_all,) if ec == 0 else (), transpose=True))
    t_q_evac = {}
    t_qn_red = {}
    qpp_done = []
    for ft in range(H):
        b = ft % 2
        mm_half = []
        for half in range(2):
            pb = psb[b * 2 + half]
            mmt = None
            for ec in range(EC):
                w = []
                if ft < 2 and half == 0 and ec == 0:
                    w = [t_wtq[-1], t_xnoT[-1], t_ones]
                if ec == 0:
                    w = w + [t_q_evac.get((b, half))]
                mmt = s.mm(pb[:, 0:512], WTc[ec][:, sl(ft, P)],
                           xnoTc[ec][:, sl(half, 512)],
                           start=(ec == 0), stop=(ec == EC - 1), waits=w,
                           inc=(ec == EC - 1))
            mm_half.append(mmt)
        evs = []
        for half in range(2):
            pb = psb[b * 2 + half]
            ev = s.dve(lambda v, _b=b, _half=half, _pb=pb, _ft=ft: v.tensor_scalar(
                out=qh[_b][:, sl(_half, 512)], in0=_pb[:, 0:512],
                scalar1=SM[:, 32 + _ft:33 + _ft], scalar2=SM[:, 48 + _ft:49 + _ft],
                op0=ALU.mult, op1=ALU.add),
                waits=(mm_half[half], t_qbias))
            t_q_evac[(b, half)] = ev
            evs.append(ev)
        tq = s.dma("gpsimd", qpp_d[ft], qh[b], waits=evs)
        qpp_done.append(tq)
        tsq = s.act(sqst, qh[b][:, 0:512], AF.Square, waits=evs)
        pbq = psb[4 + b]
        tn1 = s.mm(pbq[0:1, 0:512], onescol[:, 0:1], sqst, start=True, stop=False,
                   waits=[tsq, t_ones, t_qn_red.get(b)], inc=True)
        tsq2 = s.act(sqst, qh[b][:, 512:1024], AF.Square, waits=(tn1,))
        tn2 = s.mm(pbq[0:1, 0:512], onescol[:, 0:1], sqst, start=False, stop=True,
                   waits=(tsq2,), inc=True)
        trd = s.dve(lambda v, _pb=pbq, _ft=ft: v.reduce_max(
            out=SM[0:1, 160 + _ft:161 + _ft], in_=_pb[0:1, 0:512], axis=AX),
            waits=(tn2,))
        t_qn_red[b] = trd
        qpp_done.append(trd)
    t_qpp_all = lat(*qpp_done)
    t_mq1 = s.dma("gpsimd", mqrow_d, SM[0:1, 160:176], waits=t_qpp_all)
    t_maxq2 = s.dma("gpsimd", SM[:, 160:176], mqrow_d.partition_broadcast(P),
                    waits=(t_mq1,))

    qunits = []
    for wsrc, qdst, widx in ((w1, q1_d, 0), (w2, q2_d, 1)):
        for i in range(NHT):
            for ch in range(2):
                qunits.append((wsrc[sl(i, P), sl(ch, TOK)],
                               qdst[sl(i, P), sl(ch, TOK)], widx, TOK))
    for i in range(EC):
        for cc in range(6):
            c0 = cc * TOK
            c1 = min((cc + 1) * TOK, HID)
            qunits.append((w3[sl(i, P), c0:c1], q3_d[sl(i, P), c0:c1], 2, c1 - c0))
    t_gz = s.pool(lambda g: g.memset(SM[:, 180:183], 0.0))
    QS = {"in": [None, None, None, None], "acc": t_gz, "t": None, "t2": None,
          "qs": [None, None], "rgam": None, "v0": None, "p1": 0, "p2": 0,
          "loaded": set(), "ld": {}}

    def q_load(i):
        if i >= len(qunits) or i in QS["loaded"]:
            return
        QS["loaded"].add(i)
        src_, dst_, widx, cols = qunits[i]
        b = i % NQIN
        eng = "gpsimd" if i % 2 == 0 else "scalar"
        QS["ld"][i] = s.dma(eng, qin[b][:, 0:cols], src_, waits=(QS["in"][b],))

    def emit_qpass1(n):
        while n > 0 and QS["p1"] < len(qunits):
            i = QS["p1"]
            QS["p1"] += 1
            n -= 1
            for la in range(NQIN):
                q_load(i + la)
            src_, dst_, widx, cols = qunits[i]
            b = i % NQIN
            tr = s.dve(lambda v, _b=b, _c=cols: v.tensor_reduce(
                out=SM[:, 215:216], in_=qin[_b][:, 0:_c], axis=AX, op=ALU.add,
                apply_absolute_value=True), waits=(QS["ld"][i], QS["acc"]))
            ta = s.dve(lambda v, _w=widx: v.tensor_tensor(
                out=SM[:, 180 + _w:181 + _w], in0=SM[:, 180 + _w:181 + _w],
                in1=SM[:, 215:216], op=ALU.add), waits=(tr,))
            QS["in"][b] = ta
            QS["acc"] = ta

    def emit_gamma_final():
        tw = s.dma("gpsimd", gsc_d, SM[:, 180:183], waits=(QS["acc"],))
        trd = s.dma("gpsimd", grow[0:1, 0:384],
                    gsc_d.rearrange("p c -> (p c)").unsqueeze(0), waits=(tw,))
        tred = s.dve(lambda v: v.tensor_reduce(
            out=rowf[0:1, 1088:1091], in_=grow[0:1, 0:384].rearrange("o (p c) -> o c p", c=3),
            axis=AX, op=ALU.add), waits=(trd,))
        tg = s.dve(lambda v: v.tensor_scalar(out=rowf[0:1, 1092:1095], in0=rowf[0:1, 1088:1091],
                                             scalar1=1.0 / QCOUNT, scalar2=EPS_Q,
                                             op0=ALU.mult, op1=ALU.add), waits=(tred,))
        tr = s.dve(lambda v: v.reciprocal(out=rowf[0:1, 1088:1091], in_=rowf[0:1, 1092:1095]),
                   waits=(tg,))
        tw2 = s.dma("gpsimd", gsc_d[0:1, 0:3], rowf[0:1, 1088:1091], waits=(tr,))
        tb = s.dma("gpsimd", SM[:, 177:180], gsc_d[0:1, 0:3].partition_broadcast(P),
                   waits=(tw2,))
        QS["rgam"] = tb
        QS["acc"] = tb
        QS["loaded"] = set()
        QS["ld"] = {}

    def emit_qpass2(n):
        while n > 0 and QS["p2"] < len(qunits):
            i = QS["p2"]
            QS["p2"] += 1
            n -= 1
            for la in range(NQIN):
                q_load(i + la)
            src_, dst_, widx, cols = qunits[i]
            b = i % NQIN
            d1 = s.dve(lambda v, _b=b, _c=cols, _w=widx: v.tensor_scalar(
                out=tb_a2[:, 0:_c], in0=qin[_b][:, 0:_c],
                scalar1=SM[:, 177 + _w:178 + _w], scalar2=SM[:, 216:217],
                op0=ALU.mult, op1=ALU.add),
                waits=(QS["ld"][i], QS["rgam"], QS["t"], QS["v0"]))
            d2 = s.dve(lambda v, _c=cols: v.tensor_scalar(
                out=tb_b2[:, 0:_c], in0=tb_a2[:, 0:_c], scalar1=MAGIC, scalar2=1.0,
                op0=ALU.subtract, op1=ALU.min), waits=(d1, QS["t2"]))
            QS["t"] = d2
            b2 = i % 2
            t3 = s.dve(lambda v, _b=b2, _c=cols: v.tensor_scalar(
                out=qst[_b][:, 0:_c], in0=tb_b2[:, 0:_c], scalar1=-1.0,
                scalar2=1.0, op0=ALU.max, op1=ALU.mult), waits=(d2, QS["qs"][b2]))
            QS["t2"] = t3
            tso = s.dma("gpsimd", dst_, qst[b2][:, 0:cols], waits=(t3,))
            QS["qs"][b2] = tso
            QS["in"][b] = d1


    t_cast_v = s.dma("gpsimd", wbf[0], Wv, waits=(t_wtq[-1], t_qpp_all))
    t_wtk = build_WT(wbf[1], (t_cast_k, t_qpp_all))
    t_kev = {}
    kst_last = {}
    knorm_last = None
    t_kscr = []
    xnT_lastuse = [None, None]
    for sp in range(16):
        bsp = sp % 2
        tsp = []
        for ec in range(EC):
            w = (t_xn_all, xnT_lastuse[bsp]) if ec == 0 else ()
            tsp.append(s.dma("sync", xnTc[bsp][ec], xn_d[sl(sp, 512), sl(ec, P)],
                             waits=w, transpose=True))
        sp_last_mm = None
        for ft in range(H):
            pb = psb[ft % 2]
            mmt = None
            for ec in range(EC):
                w = []
                if ec == 0:
                    w = [tsp[-1], t_kev.get(ft % 2)]
                    if sp == 0 and ft == 0:
                        w.append(t_wtk[-1])
                mmt = s.mm(pb[:, 0:512], WTc[ec][:, sl(ft, P)], xnTc[bsp][ec],
                           start=(ec == 0), stop=(ec == EC - 1), waits=w,
                           inc=(ec == EC - 1))
            sp_last_mm = mmt
            bst = ft % 2
            ev = s.dve(lambda v, _bst=bst, _pb=pb, _ft=ft: v.tensor_scalar_add(
                out=kvst[_bst][:, 0:512], in0=_pb[:, 0:512],
                scalar1=SM[:, 80 + _ft:81 + _ft]),
                waits=(mmt, t_bkT, kst_last.get(bst), knorm_last))
            t_kev[ft % 2] = ev
            tst = s.dma("sync", kscr[ft][:, sl(sp, 512)], kvst[bst][:, 0:512],
                        waits=(ev,))
            kst_last[bst] = tst
            t_kscr.append(tst)
            tks = s.act(sqst, kvst[bst][:, 0:512], AF.Square, waits=(ev, knorm_last))
            tkm = s.mm(psb[6][0:1, 0:512], onescol[:, 0:1], sqst, start=True, stop=True,
                       waits=(tks, knorm_last), inc=True)
            tkr = s.dve(lambda v: v.reduce_max(out=rowf[0:1, 24:25],
                                               in_=psb[6][0:1, 0:512], axis=AX),
                        waits=(tkm,))
            knorm_last = s.dve(lambda v, _ft=ft: v.tensor_max(
                out=rowf[0:1, 32 + _ft:33 + _ft], in0=rowf[0:1, 32 + _ft:33 + _ft],
                in1=rowf[0:1, 24:25]), waits=(tkr, t_kn0))
        xnT_lastuse[bsp] = sp_last_mm
        emit_qpass1(17)
    emit_qpass1(len(qunits))
    emit_gamma_final()
    t_kscr_all = lat(*t_kscr)

    t_cast_o = s.dma("gpsimd", wbf[1], Wo, waits=(t_wtk[-1],))
    t_wtv = build_WT(wbf[0], [t_cast_v] + [xnT_lastuse[0], xnT_lastuse[1]])
    t_vev = {}
    vst_last = {}
    t_vscr = []
    xnT_lastuse2 = [None, None]
    for sp in range(16):
        bsp = sp % 2
        tsp = []
        for ec in range(EC):
            w = [xnT_lastuse2[bsp]] if ec == 0 else []
            if sp < 2 and ec == 0:
                w += [xnT_lastuse[0], xnT_lastuse[1], t_kscr_all]
            tsp.append(s.dma("sync", xnTc[bsp][ec], xn_d[sl(sp, 512), sl(ec, P)],
                             waits=w, transpose=True))
        sp_last_mm = None
        for tt4 in range(4):
            gt = sp * 4 + tt4
            for rnd in range(2):
                mm_of = {}
                for j, fs in enumerate((rnd * 2, rnd * 2 + 1)):
                    pb = psb[2 + j]
                    mmt = None
                    for ec in range(EC):
                        w = []
                        if ec == 0:
                            w = [tsp[-1], t_vev.get(j)]
                            if sp == 0 and tt4 == 0 and rnd == 0:
                                w.append(t_wtv[-1])
                        mmt = s.mm(pb[:, 0:512], xnTc[bsp][ec][:, sl(tt4, P)],
                                   WTc[ec][:, sl(fs, 512)],
                                   start=(ec == 0), stop=False, waits=w)
                    mmt = s.mm(pb[:, 0:512], onesrow[0:1, :],
                               rowbf[0:1, fs * 512:(fs + 1) * 512],
                               start=False, stop=True, waits=(t_onesr, t_bvrow),
                               inc=True)
                    mm_of[j] = (fs, mmt)
                for j in range(2):
                    fs, mmt = mm_of[j]
                    stb = 2 + j
                    ev = s.act(kvst[stb][:, 0:512], psb[2 + j][:, 0:512], AF.Copy,
                               waits=(mmt, vst_last.get(stb)))
                    t_vev[j] = ev
                    lastst = None
                    for hh in range(4):
                        hd = fs * 4 + hh
                        lastst = s.dma("sync", vscr[hd][sl(gt, P), :],
                                       kvst[stb][:, sl(hh, P)], waits=(ev,))
                        t_vscr.append(lastst)
                    vst_last[stb] = lastst
                    sp_last_mm = mmt
        xnT_lastuse2[bsp] = sp_last_mm
        emit_qpass2(17)
    t_vscr_all = lat(*t_vscr)
    emit_qpass2(len(qunits))
    t_quant_all = s.pool(lambda g: g.nop(),
                         waits=(QS["qs"][0], QS["qs"][1], QS["t2"]))

    t_kload = [None, None]
    t_vload = [None, None]
    t_qload = [None, None]
    t_k_lastuse = [None, None]
    t_v_lastuse = [None, None]
    t_q_lastuse = [None, None]
    stps_read = {0: None, 1: None}
    t_pv_done = None
    t_sum_done = None
    t_at_ev = [None, None]
    t_emul_last = None
    attn_store = []

    t_kload[0] = s.dma("sync", ksb[0], kscr[0],
                       waits=(t_kscr_all, xnT_lastuse2[0], xnT_lastuse2[1],
                              t_vev.get(0), t_vev.get(1), t_quant_all))
    def load_v(hh, nb, waits):
        tks = []
        for jc in range(NJT):
            tks.append(s.dma("sync", vsb[nb][:, sl(jc, P)],
                             vscr[hh][sl(jc, P), :], waits=waits if jc == 0 else ()))
        return lat(*tks)

    t_vload[0] = load_v(0, 0, (t_vscr_all,))
    t_qload[0] = s.dma("gpsimd", qh[0], qpp_d[0],
                       waits=(t_q_evac.get((0, 0)), t_q_evac.get((0, 1)),
                              t_kscr_all, knorm_last, t_vscr_all))

    tc1 = s.dve(lambda v: v.tensor_mul(out=rowf[0:1, 48:64], in0=SM[0:1, 160:176],
                                       in1=rowf[0:1, 32:48]),
                waits=[t_maxq2, knorm_last])
    tc2 = s.act(rowf[0:1, 16:32], rowf[0:1, 48:64], AF.Ln, waits=(tc1,))
    tc3 = s.dve(lambda v: v.tensor_scalar_mul(out=rowf[0:1, 48:64],
                                              in0=rowf[0:1, 16:32], scalar1=0.5),
                waits=(tc2,))
    tc4 = s.act(rowf[0:1, 16:32], rowf[0:1, 48:64], AF.Exp, waits=(tc3,))
    tc5 = s.dve(lambda v: v.scalar_tensor_tensor(
        out=rowf[0:1, 48:64], in0=rowf[0:1, 16:32], scalar=-1.0,
        in1=SM[0:1, 0:16], op0=ALU.mult, op1=ALU.mult), waits=(tc4,))
    tc6 = s.dve(lambda v: v.tensor_scalar_add(out=rowf[0:1, 16:32],
                                              in0=rowf[0:1, 48:64],
                                              scalar1=-C_EXTRA), waits=(tc5,))
    t_nc1 = s.dma("gpsimd", mqrow_d, rowf[0:1, 16:32], waits=(tc6,))
    t_negC_all = s.dma("gpsimd", SM[:, 224:240], mqrow_d.partition_broadcast(P),
                       waits=(t_nc1,))

    HH = HID // 2
    ffn_t = []
    units_t = []
    for (qsrc, tdst) in ((q1_d, w1t_d), (q2_d, w2t_d)):
        for dc in range(EC):
            for ch in range(2):
                units_t.append((qsrc[sl(ch, HH), sl(dc, P)],
                                tdst[sl(dc, P), sl(ch, HH)]))
    wr_prev = [None, None]
    wstg_i = [0]

    def emit_wstg(n):
        while n > 0 and wstg_i[0] < len(units_t):
            i = wstg_i[0]
            wstg_i[0] += 1
            n -= 1
            b = i % 2
            w = [wr_prev[b]]
            if i < 2:
                w.append(t_quant_all)
            ttr = s.dma("sync", wstg2[b][:, 0:HH], units_t[i][0], waits=w,
                        transpose=True)
            wr_prev[b] = s.dma("sync", units_t[i][1], wstg2[b][:, 0:HH],
                               waits=(ttr,))
            ffn_t.append(wr_prev[b])

    for h in range(H):
        hb = h % 2
        t_egen = s.act(Eh, Wal, AF.Exp, scale=SM[:, 112 + h:113 + h],
                       waits=(t_negs, t_emul_last))
        t_negC = t_negC_all

        if h + 1 < H:
            nb = (h + 1) % 2
            t_kload[nb] = s.dma("sync", ksb[nb], kscr[h + 1],
                                waits=(t_k_lastuse[nb],
                                       t_quant_all if h == 0 else None))
            t_vload[nb] = load_v(h + 1, nb, (t_v_lastuse[nb],))
            t_qload[nb] = s.dma("gpsimd", qh[nb], qpp_d[h + 1],
                                waits=(t_q_lastuse[nb],
                                       t_kscr_all if h == 0 else None,
                                       knorm_last if h == 0 else None))

        st_tok = {}
        emul = {}

        def emit_st(jt, _h=h, _hb=hb):
            bb = jt % 2
            w = [stps_read[bb]]
            if jt < 2:
                w += [t_kload[_hb], t_qload[_hb]]
            s.mm(psb[bb * 2][:, 0:512], ksb[_hb][:, sl(jt, P)], qh[_hb][:, 0:512],
                 start=True, stop=True, waits=w)
            st_tok[jt] = s.mm(psb[bb * 2 + 1][:, 0:512], ksb[_hb][:, sl(jt, P)],
                              qh[_hb][:, 512:1024], start=True, stop=True, inc=True)

        emit_st(0)
        pv_last = None
        for jt in range(NJT):
            bb = jt % 2
            texp = s.act(p0sb[bb][:, 0:512], psb[bb * 2][:, 0:512], AF.Exp,
                         bias=SM[:, 224 + h:225 + h], scale=SM[:, h:h + 1],
                         waits=(st_tok[jt], t_negC, emul.get(jt - 2)))
            texp2 = s.act(p0sb[bb][:, 512:1024], psb[bb * 2 + 1][:, 0:512], AF.Exp,
                          bias=SM[:, 224 + h:225 + h], scale=SM[:, h:h + 1])
            stps_read[bb] = texp2
            if jt + 1 < NJT:
                emit_st(jt + 1)
            off = 8192 - 128 * jt
            tmul = s.dve(lambda v, _bb=bb, _o=off: v.tensor_tensor(
                out=ptsb[_bb], in0=p0sb[_bb], in1=Eh[:, _o:_o + 1024], op=ALU.mult),
                waits=(texp2, t_egen,
                       t_vscr_all if (h == 0 and jt < 2) else None))
            emul[jt] = tmul
            t_emul_last = tmul
            w_pv = [tmul]
            if jt == 0:
                w_pv += [t_vload[hb], t_pv_done, t_sum_done]
            s.mm(psb[4][:, 0:512], vsb[hb][:, sl(jt, P)], ptsb[bb][:, 0:512],
                 start=(jt == 0), stop=(jt == NJT - 1), waits=w_pv)
            s.mm(psb[5][:, 0:512], vsb[hb][:, sl(jt, P)], ptsb[bb][:, 512:1024],
                 start=(jt == 0), stop=(jt == NJT - 1))
            s.mm(psb[6][0:1, 0:512], onescol[:, 0:1], ptsb[bb][:, 0:512],
                 start=(jt == 0), stop=(jt == NJT - 1))
            pv_last = s.mm(psb[7][0:1, 0:512], onescol[:, 0:1], ptsb[bb][:, 512:1024],
                           start=(jt == 0), stop=(jt == NJT - 1), inc=True)
        t_k_lastuse[hb] = st_tok[NJT - 1]
        t_q_lastuse[hb] = st_tok[NJT - 1]
        t_v_lastuse[hb] = pv_last
        tse = s.act(rowf[0:1, 64:576], psb[6][0:1, 0:512], AF.Copy, waits=(pv_last,))
        tse2 = s.act(rowf[0:1, 576:1088], psb[7][0:1, 0:512], AF.Copy)
        trc = s.dve(lambda v: v.reciprocal(out=rowf[0:1, 1088:2112],
                                           in_=rowf[0:1, 64:1088]), waits=(tse2,))
        trcb = s.dve(lambda v: v.tensor_copy(out=rowbf[0:1, 4096:5120],
                                             in_=rowf[0:1, 1088:2112]), waits=(trc,))
        tb1 = s.mm(psb[2][:, 0:512], onesrow[0:1, :], rowbf[0:1, 4096:4608],
                   start=True, stop=True, waits=(trcb, stps_read[1]))
        tb2 = s.mm(psb[3][:, 0:512], onesrow[0:1, :], rowbf[0:1, 4608:5120],
                   start=True, stop=True, inc=True)
        trb = s.act(rbsb[:, 0:512], psb[2][:, 0:512], AF.Copy, waits=(tb2,))
        trb2 = s.act(rbsb[:, 512:1024], psb[3][:, 0:512], AF.Copy)
        stps_read[1] = trb2
        ab = h % 2
        tat = s.dve(lambda v, _ab=ab: v.tensor_tensor(out=atst[_ab][:, 0:512],
                                                      in0=psb[4][:, 0:512],
                                                      in1=rbsb[:, 0:512],
                                                      op=ALU.mult),
                    waits=(trb2, pv_last, t_at_ev[ab],
                           t_vscr_all if h < 2 else None))
        tat2 = s.dve(lambda v, _ab=ab: v.tensor_tensor(out=atst[_ab][:, 512:1024],
                                                       in0=psb[5][:, 0:512],
                                                       in1=rbsb[:, 512:1024],
                                                       op=ALU.mult))
        t_pv_done = tat2
        t_sum_done = tse2
        tats = s.dma("gpsimd", attn_d[h], atst[ab], waits=(tat2,))
        t_at_ev[ab] = tats
        attn_store.append(tats)
        if h >= 1:
            emit_wstg(5)
    emit_wstg(len(units_t))
    t_ffnw_all = lat(*ffn_t)

    t_wto = build_WT(wbf[1], [t_cast_o, stps_read[0], stps_read[1], t_pv_done,
                              t_quant_all, t_emul_last])
    t_atch = [None, None]
    t_p4ev = {}
    t_xres_store = []
    t_xn2_store = []
    xslab_last = [last_apply[0], last_apply[1]]
    xrst_last = [None, None]
    xn2st_last = [None, None]
    p4items = [(tt_, ec_) for tt_ in range(8) for ec_ in range(EC)]
    tld_tok = {}

    def emit_tld(k):
        if k >= len(p4items) or k in tld_tok:
            return
        tt_, ec_ = p4items[k]
        w = [t_atch[0], t_atch[1]]
        if tt_ == 0:
            w.append(attn_store[ec_])
        tld_tok[k] = s.dma("sync", atch[k % 2][:, 0:TOK], attn_d[ec_], waits=w)

    for tt in range(8):
        bx = tt % 2
        txo = s.dma("sync", xslab[bx], xo[sl(tt, P), :], waits=(xslab_last[bx],))
        mm_fs = {}
        for ec in range(EC):
            k = tt * EC + ec
            bc = k % 2
            emit_tld(k)
            emit_tld(k + 1)
            for fs in range(4):
                w2_ = [tld_tok[k]]
                if ec == 0:
                    w2_ += [t_wto[-1], t_p4ev.get(fs)]
                mm_fs[fs] = s.mm(psb[4 + fs][:, 0:512], atch[bc][:, sl(tt, P)],
                                 WTc[ec][:, sl(fs, 512)],
                                 start=(ec == 0), stop=False, waits=w2_,
                                 inc=(fs == 3))
            t_atch[bc] = mm_fs[3]
        for fs in range(4):
            mm_fs[fs] = s.mm(psb[4 + fs][:, 0:512], onesrow[0:1, :],
                             rowbf[0:1, E + fs * 512:E + (fs + 1) * 512],
                             start=False, stop=True, waits=(t_borow,), inc=True)
        tt_ev = []
        for fs in range(4):
            ev = s.dve(lambda v, _bx=bx, _fs=fs: v.tensor_tensor(
                out=xrst[_bx][:, sl(_fs, 512)], in0=psb[4 + _fs][:, 0:512],
                in1=xslab[_bx][:, sl(_fs, 512)], op=ALU.add),
                waits=(mm_fs[fs], txo, xrst_last[bx]))
            t_p4ev[fs] = ev
            tt_ev.append(ev)
        t9 = ln_apply(xrst[bx], xn2st[bx], tt_ev + [xn2st_last[bx]])
        ts1 = s.dma("sync", xres_d[sl(tt, P), :], xrst[bx], waits=(t9,))
        ts2 = s.dma("sync", xn2_d[sl(tt, P), :], xn2st[bx], waits=(t9,))
        xrst_last[bx] = ts1
        xn2st_last[bx] = ts2
        xslab_last[bx] = t_p4ev[3]
        t_xres_store.append(ts1)
        t_xn2_store.append(ts2)
    t_xres_all = lat(*t_xres_store)
    t_xn2_all = lat(*t_xn2_store)

    t_xn2T = []
    for ec in range(EC):
        w = [t_xn2_all, t_p4ev[3]] if ec == 0 else ()
        t_xn2T.append(s.dma("sync", xn2Tc[ec], xn2_d[:, sl(ec, P)], waits=w,
                            transpose=True))
    t_out_final = []
    f1_psum_free = [stps_read[0], stps_read[0], stps_read[1], stps_read[1],
                    t_p4ev[0], t_p4ev[1], t_p4ev[2], t_p4ev[3]]
    w1sl_last = [None, None]
    hid_last = lat(t_xres_all, t_xn2_all)
    f2_done_prev = None
    for th in range(2):
        hs_ev = []
        for hs in range(11):
            tw1 = s.dma("sync", w1sl[0].rearrange("p (ec c) -> p ec c", c=512),
                        w1t_d[:, sl(hs, 512)].rearrange("(ec p) c -> p ec c", p=P),
                        waits=[t_ffnw_all, w1sl_last[0]] if hs == 0 else (w1sl_last[0],))
            tw2 = s.dma("sync", w1sl[1].rearrange("p (ec c) -> p ec c", c=512),
                        w2t_d[:, sl(hs, 512)].rearrange("(ec p) c -> p ec c", p=P),
                        waits=[t_ffnw_all, w1sl_last[1]] if hs == 0 else (w1sl_last[1],))
            mm1 = {}
            mm2 = {}
            for dc in range(EC):
                for tt in range(4):
                    w = []
                    if dc == 0:
                        w = [tw1, f1_psum_free[tt],
                             t_xn2T[-1] if hs == 0 and th == 0 else None,
                             f2_done_prev if hs == 0 else None]
                    mm1[tt] = s.mm(psb[tt][:, 0:512],
                                   xn2Tc[dc][:, th * 512 + tt * 128:th * 512 + tt * 128 + 128],
                                   w1sl[0][:, sl(dc, 512)],
                                   start=(dc == 0), stop=(dc == EC - 1), waits=w,
                                   inc=(dc == EC - 1))
                    w = []
                    if dc == 0:
                        w = [tw2, f1_psum_free[4 + tt]]
                    mm2[tt] = s.mm(psb[4 + tt][:, 0:512],
                                   xn2Tc[dc][:, th * 512 + tt * 128:th * 512 + tt * 128 + 128],
                                   w1sl[1][:, sl(dc, 512)],
                                   start=(dc == 0), stop=(dc == EC - 1), waits=w,
                                   inc=(dc == EC - 1))
            last_w1_mm = mm2[3]
            w1sl_last[0] = last_w1_mm
            w1sl_last[1] = last_w1_mm
            for tt in range(4):
                bsg = tt % 2
                tsg = s.act(sgst[bsg], psb[tt][:, 0:512], AF.Sigmoid,
                            waits=(mm1[tt], f1_psum_free[(tt + 3) % 4]))
                f1_psum_free[tt] = tsg
                thd = s.dve(lambda v, _t=tt, _hs=hs, _b=bsg: v.tensor_tensor(
                    out=hid_tt[_t][:, _hs * 512:_hs * 512 + 512],
                    in0=sgst[_b], in1=psb[4 + _t][:, 0:512], op=ALU.mult),
                    waits=(tsg, mm2[tt], hid_last))
                f1_psum_free[4 + tt] = thd
                hs_ev.append(thd)
        thw_parts = []
        for _t in range(4):
            thw_parts.append(s.dma("sync", hid_d[sl(_t, P), :], hid_tt[_t],
                                   waits=hs_ev if _t == 0 else ()))
        thw = lat(*thw_parts)
        hid_last = thw
        t_hT = []
        for hc in range(NHT):
            w = [thw] if hc == 0 else ()
            t_hT.append(s.dma("sync", hidTc[hc], hid_d[:, sl(hc, P)], waits=w,
                              transpose=True))
        q3_last = [None, None]
        f2_mm = {}
        for fh in range(2):
            for hc in range(NHT):
                bq3 = hc % 2
                w = [q3_last[bq3]]
                if fh == 0 and hc == 0:
                    w += [t_ffnw_all]
                if fh == 0 and hc == 0:
                    w += [t_at_ev[0], t_at_ev[1]]
                tq3 = s.dma("sync", q3ch[bq3], q3_d[sl(fh, TOK), sl(hc, P)], waits=w,
                            transpose=True)
                for tt in range(4):
                    w2_ = [tq3, t_hT[-1] if hc == 0 else None]
                    if hc == 0:
                        w2_.append(f2_mm.get((1 - fh, tt)))
                        w2_.append(f1_psum_free[2 * tt])
                        w2_.append(f1_psum_free[2 * tt + 1])
                    s.mm(psb[2 * tt][:, 0:512], hidTc[hc][:, sl(tt, P)],
                         q3ch[bq3][:, 0:512], start=(hc == 0), stop=(hc == NHT - 1),
                         waits=w2_)
                    f2_mm[(fh, tt)] = s.mm(psb[2 * tt + 1][:, 0:512],
                                           hidTc[hc][:, sl(tt, P)],
                                           q3ch[bq3][:, 512:1024],
                                           start=(hc == 0), stop=(hc == NHT - 1),
                                           waits=(),
                                           inc=(hc == NHT - 1 or tt == 3))
                q3_last[bq3] = f2_mm[(fh, 3)]
            for tt in range(4):
                bxr = tt % 2
                bo2 = tt % 2
                txr = s.dma("sync", xrtile[bxr], xres_d[sl(th * 4 + tt, P), sl(fh, TOK)],
                            waits=(t_xres_all if fh == 0 and tt < 2 and th == 0 else None,
                                   t_ffnw_all if fh == 0 and tt < 2 and th == 0 else None,
                                   t_p4ev.get(("xr", bxr))))
                e1 = s.dve(lambda v, _tt=tt, _b=bxr, _o=bo2: v.tensor_tensor(
                    out=outst2[_o][:, 0:512], in0=psb[2 * _tt][:, 0:512],
                    in1=xrtile[_b][:, 0:512], op=ALU.add),
                    waits=(f2_mm[(fh, tt)], txr, t_p4ev.get(("out", bo2))))
                e2 = s.dve(lambda v, _tt=tt, _b=bxr, _o=bo2: v.tensor_tensor(
                    out=outst2[_o][:, 512:1024], in0=psb[2 * _tt + 1][:, 0:512],
                    in1=xrtile[_b][:, 512:1024], op=ALU.add))
                t_p4ev[("xr", bxr)] = e2
                f1_psum_free[2 * tt] = e2
                f1_psum_free[2 * tt + 1] = e2
                tso = s.dma_out("gpsimd", yout[sl(th * 4 + tt, P), sl(fh, TOK)],
                                outst2[bo2], waits=(e2,))
                t_p4ev[("out", bo2)] = tso
                t_out_final.append(tso)
                f2_done_prev = lat(t_p4ev.get(("xr", 0)), t_p4ev.get(("xr", 1)))
        if th == 0:
            hid_last = lat(hid_last, t_p4ev.get(("out", 0)), t_p4ev.get(("out", 1)))

    s.fin("gpsimd", lat(*t_out_final))
    s.fin("sync", lat(*t_out_final))

    return nc, s


def finalize(nc, s):
    sems = {}
    ctxs = []
    for name in SEM_NAMES:
        cm = nc.semaphore(name)
        sems[name] = cm.__enter__()
        ctxs.append(cm)
    blk = nc.Block()
    block = blk.__enter__()
    block.sync(lambda e: s.emit(e, "sync", sems))
    block.gpsimd(lambda e: s.emit(e, "gpsimd", sems))
    block.vector(lambda e: s.emit(e, "vector", sems))
    block.scalar(lambda e: s.emit(e, "scalar", sems))
    block.tensor(lambda e: s.emit(e, "tensor", sems))
    blk.__exit__(None, None, None)
    for cm in reversed(ctxs):
        cm.__exit__(None, None, None)
    return nc


def build(dbg=0):
    nc, s = build_program(dbg=dbg)
    return finalize(nc, s)


def make_inputs(inputs, core):
    x = np.asarray(inputs["L0_attn_x"]).reshape(T, E)
    im = {
        "x_full": np.ascontiguousarray(x),
        "x_own": np.ascontiguousarray(x[core * TOK:(core + 1) * TOK]),
        "Wq": np.asarray(inputs["L0_attn_Wq"]),
        "Wk": np.asarray(inputs["L0_attn_Wk"]),
        "Wv": np.asarray(inputs["L0_attn_Wv"]),
        "Wo": np.asarray(inputs["L0_attn_Wo"]),
        "bq": np.asarray(inputs["L0_attn_bq"]).reshape(1, E),
        "bk": np.asarray(inputs["L0_attn_bk"]).reshape(1, E),
        "bv": np.asarray(inputs["L0_attn_bv"]).reshape(1, E),
        "bo": np.asarray(inputs["L0_attn_bo"]).reshape(1, E),
        "slopes": np.asarray(inputs["L0_attn_slopes"]).reshape(1, H),
        "w1": np.asarray(inputs["L0_ffn_w1"]),
        "w2": np.asarray(inputs["L0_ffn_w2"]),
        "w3": np.asarray(inputs["L0_ffn_w3"]),
        "wofs": np.full((P, 1), core * TOK - 8192, np.float32),
    }
    return im


_NC_CACHE = {}


def kernel(**inputs):
    from concourse.bass_utils import run_bass_kernel_spmd
    if "nc" not in _NC_CACHE:
        _NC_CACHE["nc"] = build(dbg=0)
    nc = _NC_CACHE["nc"]
    ims = [make_inputs(inputs, c) for c in range(8)]
    res = run_bass_kernel_spmd(nc, ims, core_ids=list(range(8)))
    return np.concatenate([res.results[c]["y"] for c in range(8)], axis=0).reshape(1, T, E)


# version 10 (final)
# speedup vs baseline: 1.2670x; 1.0034x over previous
import numpy as np
import concourse.bass as bass
import concourse.mybir as mybir
from bass_rust import ReduceOp

F32 = mybir.dt.float32
F16 = mybir.dt.float16
BF16 = mybir.dt.bfloat16
AF = mybir.ActivationFunctionType
ALU = mybir.AluOpType
AX = mybir.AxisListType.X

P = 128
T = 8192
TOK = 1024
E = 2048
H = 16
D = 128
HID = 5632
EC = E // P
NJT = T // P
NHT = HID // P
SCALING = float(D) ** -0.5
EPS_LN = 1e-5
EPS_Q = 1e-8
MAGIC = 1.5 * 2 ** 23
C_EXTRA = 2.0
NQIN = 4
QCOUNT = float(HID * E)

ENGS = ("sync", "gpsimd", "vector", "scalar", "tensor")
SEM_NAMES = ["qs", "qg", "qa", "pe", "act", "dve", "pool", "outq"]
QSEM = {"sync": "qs", "gpsimd": "qg", "scalar": "qa", "vector": "qa"}
DMA_SEMS = {"qs", "qg", "qa", "outq"}


def sl(i, n):
    return slice(i * n, (i + 1) * n)


def lat(*toks):
    best = {}
    for t in toks:
        if t is None:
            continue
        items = t if isinstance(t, list) else [t]
        for u in items:
            if u is None:
                continue
            s2, v2 = u
            if best.get(s2, -1) < v2:
                best[s2] = v2
    return [(k, v) for k, v in best.items()]


class Sched:
    def __init__(self):
        self.ops = {e: [] for e in ENGS}
        self.cnt = {k: 0 for k in SEM_NAMES}

    def _op(self, eng, fn, waits, inc, amt):
        ws = lat(*waits) if waits else []
        ws = [(sn, min(v + 32, self.cnt[sn]) if sn in DMA_SEMS else v)
              for (sn, v) in ws]
        tok = None
        if inc is not None:
            self.cnt[inc] += amt
            tok = (inc, self.cnt[inc])
        self.ops[eng].append((fn, ws, inc, amt))
        return tok

    def dma(self, eng, out, in_, waits=(), **kw):
        return self._op(eng, lambda e, o=out, i=in_, k=kw: e.dma_start(out=o, in_=i, **k),
                        waits, QSEM[eng], 16)

    def dma_out(self, eng, out, in_, waits=(), **kw):
        return self._op(eng, lambda e, o=out, i=in_, k=kw: e.dma_start(out=o, in_=i, **k),
                        waits, "outq", 16)

    def mm(self, out, lhsT, rhs, start, stop, waits=(), inc=False):
        return self._op("tensor",
                        lambda e, o=out, l=lhsT, r=rhs, st=start, sp=stop:
                        e.matmul(o, l, r, start=st, stop=sp),
                        waits, "pe" if inc else None, 1)

    def act(self, out, in_, func, bias=None, scale=None, accum_out=None, waits=()):
        def fn(e, o=out, i=in_, f=func, b=bias, sc=scale, ao=accum_out):
            kw = {}
            if b is not None:
                kw["bias"] = b
            if sc is not None:
                kw["scale"] = sc
            if ao is not None:
                kw["accum_out"] = ao
            return e.activation(out=o, in_=i, func=f, **kw)
        return self._op("scalar", fn, waits, "act", 1)

    def dve(self, fn, waits=()):
        return self._op("vector", fn, waits, "dve", 1)

    def pool(self, fn, waits=()):
        return self._op("gpsimd", fn, waits, "pool", 1)

    def fin(self, eng, waits):
        return self._op(eng, lambda e: e.nop(), waits, None, 1)

    def emit(self, eng_obj, eng_name, sems):
        hw = {}
        for fn, waits, inc, amt in self.ops[eng_name]:
            for (sname, v) in waits:
                if hw.get(sname, 0) < v:
                    eng_obj.wait_ge(sems[sname], v)
                    hw[sname] = v
            inst = fn(eng_obj)
            if inc is not None:
                inst.then_inc(sems[inc], amt)


def build_program(dbg=0):
    nc = bass.Bass("TRN2", target_bir_lowering=False, debug=False)
    s = Sched()
    dk = "ExternalOutput" if dbg else "Internal"

    xf = nc.dram_tensor("x_full", [T, E], F32, kind="ExternalInput").ap()
    xo = nc.dram_tensor("x_own", [TOK, E], F32, kind="ExternalInput").ap()
    Wq = nc.dram_tensor("Wq", [E, E], F32, kind="ExternalInput").ap()
    Wk = nc.dram_tensor("Wk", [E, E], F32, kind="ExternalInput").ap()
    Wv = nc.dram_tensor("Wv", [E, E], F32, kind="ExternalInput").ap()
    Wo = nc.dram_tensor("Wo", [E, E], F32, kind="ExternalInput").ap()
    bqv = nc.dram_tensor("bq", [1, E], F32, kind="ExternalInput").ap()
    bkv = nc.dram_tensor("bk", [1, E], F32, kind="ExternalInput").ap()
    bvv = nc.dram_tensor("bv", [1, E], F32, kind="ExternalInput").ap()
    bov = nc.dram_tensor("bo", [1, E], F32, kind="ExternalInput").ap()
    slopes = nc.dram_tensor("slopes", [1, H], F32, kind="ExternalInput").ap()
    w1 = nc.dram_tensor("w1", [HID, E], F32, kind="ExternalInput").ap()
    w2 = nc.dram_tensor("w2", [HID, E], F32, kind="ExternalInput").ap()
    w3 = nc.dram_tensor("w3", [E, HID], F32, kind="ExternalInput").ap()
    wofs = nc.dram_tensor("wofs", [P, 1], F32, kind="ExternalInput").ap()
    yout = nc.dram_tensor("y", [TOK, E], F32, kind="ExternalOutput").ap()

    xn_d = nc.dram_tensor("xn_d", [T, E], BF16, kind=dk).ap()
    xno_d = nc.dram_tensor("xno_d", [TOK, E], BF16, kind="Internal").ap()
    kscr = nc.dram_tensor("kscr", [H, P, T], BF16, kind=dk).ap()
    vscr = nc.dram_tensor("vscr", [H, T, D], BF16, kind=dk).ap()
    qpp_d = nc.dram_tensor("qpp_d", [H, P, TOK], BF16, kind=dk).ap()
    attn_d = nc.dram_tensor("attn_d", [H, P, TOK], BF16, kind=dk).ap()
    wbf = [nc.dram_tensor(f"wbf{i}", [E, E], BF16, kind="Internal").ap() for i in range(2)]
    q1_d = nc.dram_tensor("q1_d", [HID, E], BF16, kind=dk).ap()
    q2_d = nc.dram_tensor("q2_d", [HID, E], BF16, kind="Internal").ap()
    q3_d = nc.dram_tensor("q3_d", [E, HID], BF16, kind=dk).ap()
    w1t_d = nc.dram_tensor("w1t_d", [E, HID], BF16, kind="Internal").ap()
    w2t_d = nc.dram_tensor("w2t_d", [E, HID], BF16, kind="Internal").ap()
    q3t_d = nc.dram_tensor("q3t_d", [HID, E], BF16, kind="Internal").ap()
    xres_d = nc.dram_tensor("xres_d", [TOK, E], F32, kind=dk).ap()
    xn2_d = nc.dram_tensor("xn2_d", [TOK, E], BF16, kind=dk).ap()
    hid_d = nc.dram_tensor("hid_d", [TOK // 2, HID], BF16, kind=dk).ap()
    mqrow_d = nc.dram_tensor("mqrow_d", [1, H], F32, kind="Internal").ap()
    negc_d = nc.dram_tensor("negc_d", [H, 1, 1], F32, kind="Internal").ap()
    gsc_d = nc.dram_tensor("gsc_d", [P, 3], F32, kind="Internal").ap()

    SM = nc.alloc_sbuf_tensor("SM", [P, 256], F32).ap()
    rowf = nc.alloc_sbuf_tensor("rowf", [1, 2112], F32).ap()
    rowbf = nc.alloc_sbuf_tensor("rowbf", [1, 5120], BF16).ap()
    onescol = nc.alloc_sbuf_tensor("onescol", [P, 2], BF16).ap()
    onesrow = nc.alloc_sbuf_tensor("onesrow", [1, 128], BF16).ap()
    grow = nc.alloc_sbuf_tensor("grow", [1, 384], F32).ap()

    rem = nc.sbuf_bytes_remaining - 512
    arena = nc.alloc_sbuf_tensor("arena", [P, rem], mybir.dt.uint8).ap()
    assert rem >= 190464, f"arena too small: {rem}"

    def reg(off, cols, dtype):
        esz = {F32: 4, BF16: 2, F16: 2}[dtype]
        assert off + cols * esz <= rem, (off, cols)
        return arena[:, off:off + cols * esz].bitcast(dtype)

    xslab = [reg(i * 8192, E, F32) for i in range(2)]
    kvst = [reg(i * 4096, E, BF16) for i in range(4)]
    qh = [reg(i * 2048, TOK, BF16) for i in range(2)]
    p0sb = [reg(4096 + i * 2048, TOK, BF16) for i in range(2)]
    ptsb = [reg(8192 + i * 2048, TOK, BF16) for i in range(2)]
    atst = [reg(12288 + i * 2048, TOK, BF16) for i in range(2)]
    xnst = [reg(16384 + i * 4096, E, BF16) for i in range(2)]
    atch = [reg(16384 + i * 2048, TOK, BF16) for i in range(2)]
    sgst = [reg(20480 + i * 1024, 512, BF16) for i in range(2)]
    q3ch = [reg(16384 + i * 2048, TOK, BF16) for i in range(2)]
    WTc = [reg(24576 + ec * E * 2, E, BF16) for ec in range(EC)]
    WalRaw = reg(24576, 9216, F32)
    qin = [reg(174080 + i * 4096, TOK, F32) for i in range(4)]
    tb_a2 = reg(108544, TOK, F32)
    tb_b2 = reg(112640, TOK, BF16)
    qst = [reg(114688 + i * 2048, TOK, BF16) for i in range(2)]
    Eh = reg(43008, 9216, BF16)
    rbsb = reg(61440, TOK, F32)
    xn2Tc = [reg(24576 + ec * TOK * 2, TOK, BF16) for ec in range(EC)]
    w1sl = [reg(57344 + i * 16384, EC * 512, BF16) for i in range(2)]
    Wal = reg(90112, 9216, F16)
    xnoTc = [reg(108544 + ec * 2048, TOK, BF16) for ec in range(EC)]
    ksb = [reg(108544 + b * 16384, T, BF16) for b in range(2)]
    xrst = [reg(108544 + i * 8192, E, F32) for i in range(2)]
    xn2st = [reg(124928 + i * 4096, E, BF16) for i in range(2)]
    hid_tt = [reg(108544 + i * 11264, HID, BF16) for i in range(4)]
    hidTc = [reg(108544 + hc * 1024, 512, BF16) for hc in range(NHT)]
    xnTc = [[reg(141312 + b * 16384 + ec * 1024, 512, BF16) for ec in range(EC)]
            for b in range(2)]
    vsb = [reg(141312 + b * 16384, T, BF16) for b in range(2)]
    outst2 = [reg(174080 + i * 4096, TOK, F32) for i in range(2)]
    xrtile = [reg(182272 + i * 4096, TOK, F32) for i in range(2)]
    wstg2 = [reg(174080 + b * 8192, 4096, BF16) for b in range(2)]
    sqst = reg(20480, 512, BF16)

    psb = [nc.alloc_psum_tensor(f"pb{i}", [P, 512], F32) for i in range(8)]

    t_slopes = s.dma("sync", rowf[0:1, 0:H], slopes)
    t_wofs = s.dma("sync", SM[:, 176:177], wofs)
    t_bvrow = s.dma("gpsimd", rowbf[0:1, 0:E], bvv)
    t_borow = s.dma("gpsimd", rowbf[0:1, E:2 * E], bov)
    t_bqT = s.dma("gpsimd", SM[:, 64:80], bqv.rearrange("o (c p) -> p (o c)", p=P),
                  allow_slow_non_contiguous=True)
    t_bkT = s.dma("gpsimd", SM[:, 80:96], bkv.rearrange("o (c p) -> p (o c)", p=P),
                  allow_slow_non_contiguous=True)
    t_bvT = s.dma("gpsimd", SM[:, 96:112], bvv.rearrange("o (c p) -> p (o c)", p=P),
                  allow_slow_non_contiguous=True)

    t_slopec = s.dma("sync", SM[:, 0:16], slopes.partition_broadcast(P))
    t_rslope = s.dve(lambda v: v.reciprocal(out=SM[:, 16:32], in_=SM[:, 0:16]),
                     waits=(t_slopec,))
    t_sfold = s.dve(lambda v: v.tensor_scalar_mul(out=SM[:, 32:48], in0=SM[:, 16:32],
                                                  scalar1=SCALING), waits=(t_rslope,))
    t_qbias = s.dve(lambda v: v.tensor_mul(out=SM[:, 48:64], in0=SM[:, 64:80],
                                           in1=SM[:, 32:48]), waits=(t_sfold, t_bqT))
    t_negs = s.dve(lambda v: v.tensor_scalar_mul(out=SM[:, 112:128], in0=SM[:, 0:16],
                                                 scalar1=-1.0), waits=(t_slopec,))
    t_ones = s.dve(lambda v: v.memset(onescol[:, 0:1], 1.0))
    t_onesr = s.dve(lambda v: v.memset(onesrow[0:1, :], 1.0))
    t_eps = s.dve(lambda v: v.memset(SM[:, 213:214], EPS_LN))
    t_nh = s.dve(lambda v: v.memset(SM[:, 214:215], -0.5))
    t_mg = s.dve(lambda v: v.memset(SM[:, 216:217], MAGIC))
    t_one = s.dve(lambda v: v.memset(SM[:, 217:218], 1.0))
    t_zero = s.dve(lambda v: v.memset(SM[:, 218:219], 0.0))
    t_kn0 = s.dve(lambda v: v.memset(rowf[0:1, 32:48], 0.0))

    t_wraw = s.pool(lambda g: g.iota(out=WalRaw, pattern=[[1, 9216]], base=0,
                                     channel_multiplier=-1,
                                     allow_small_or_imprecise_dtypes=True))
    t_wal = s.act(Wal, WalRaw, AF.Abs, bias=SM[:, 176:177], waits=(t_wraw, t_wofs))

    def ln_apply(xbuf, dst_bf, waits):
        waits = lat(*waits) if not isinstance(waits, list) else lat(*waits)
        t1 = s.dve(lambda v: v.bn_stats(out=SM[:, 184:190], in_=xbuf[:, 0:512]),
                   waits=waits)
        t2 = s.dve(lambda v: v.bn_stats(out=SM[:, 190:196], in_=xbuf[:, 512:1024]))
        t3 = s.dve(lambda v: v.bn_stats(out=SM[:, 196:202], in_=xbuf[:, 1024:1536]))
        t4 = s.dve(lambda v: v.bn_stats(out=SM[:, 202:208], in_=xbuf[:, 1536:2048]))
        t5 = s.dve(lambda v: v.bn_aggr(out=SM[:, 208:210], in_=SM[:, 184:208]),
                   waits=(t4,))
        t6 = s.act(SM[:, 210:211], SM[:, 209:210], AF.Ln, bias=SM[:, 213:214],
                   waits=(t5, t_eps))
        t7 = s.act(SM[:, 211:212], SM[:, 210:211], AF.Exp, scale=SM[:, 214:215],
                   waits=(t6, t_nh))
        t8 = s.dve(lambda v: v.scalar_tensor_tensor(out=SM[:, 212:213],
                                                    in0=SM[:, 208:209], scalar=-1.0,
                                                    in1=SM[:, 211:212],
                                                    op0=ALU.mult, op1=ALU.mult),
                   waits=(t7,))
        t9 = s.dve(lambda v: v.tensor_scalar(out=dst_bf, in0=xbuf,
                                             scalar1=SM[:, 211:212],
                                             scalar2=SM[:, 212:213],
                                             op0=ALU.mult, op1=ALU.add),
                   waits=(t8,))
        return t9

    last_apply = [None, None]
    t_xn_store = []
    for i in range(NJT):
        b = i % 2
        qeng = "sync" if b == 0 else "gpsimd"
        tl = s.dma(qeng, xslab[b], xf[sl(i, P), :], waits=(last_apply[b],))
        t9 = ln_apply(xslab[b], xnst[b], (tl, t_wal if i == 0 else None))
        last_apply[b] = t9
        t_xn_store.append(s.dma(qeng, xn_d[sl(i, P), :], xnst[b], waits=(t9,)))
    t_xno_store = []
    for i in range(8):
        b = i % 2
        qeng = "sync" if b == 0 else "gpsimd"
        tl = s.dma(qeng, xslab[b], xo[sl(i, P), :], waits=(last_apply[b],))
        t9 = ln_apply(xslab[b], xnst[b], (tl,))
        last_apply[b] = t9
        t_xno_store.append(s.dma(qeng, xno_d[sl(i, P), :], xnst[b], waits=(t9,)))
    t_xn_all = t_xn_store[-1]
    t_xno_all = t_xno_store[-1]

    t_cast_q = s.dma("gpsimd", wbf[0], Wq)
    t_cast_k = s.dma("gpsimd", wbf[1], Wk)

    def build_WT(src, waits):
        toks = []
        for ec in range(EC):
            toks.append(s.dma("sync", WTc[ec], src[:, sl(ec, P)],
                              waits=waits if ec == 0 else (), transpose=True))
        return toks

    t_wtq = build_WT(wbf[0], (t_cast_q, t_wal, t_xn_all))
    t_xnoT = []
    for ec in range(EC):
        t_xnoT.append(s.dma("sync", xnoTc[ec], xno_d[:, sl(ec, P)],
                            waits=(t_xno_all,) if ec == 0 else (), transpose=True))
    t_q_evac = {}
    t_qn_red = {}
    qpp_done = []
    for ft in range(H):
        b = ft % 2
        mm_half = []
        for half in range(2):
            pb = psb[b * 2 + half]
            mmt = None
            for ec in range(EC):
                w = []
                if ft < 2 and half == 0 and ec == 0:
                    w = [t_wtq[-1], t_xnoT[-1], t_ones]
                if ec == 0:
                    w = w + [t_q_evac.get((b, half))]
                mmt = s.mm(pb[:, 0:512], WTc[ec][:, sl(ft, P)],
                           xnoTc[ec][:, sl(half, 512)],
                           start=(ec == 0), stop=(ec == EC - 1), waits=w,
                           inc=(ec == EC - 1))
            mm_half.append(mmt)
        evs = []
        for half in range(2):
            pb = psb[b * 2 + half]
            ev = s.dve(lambda v, _b=b, _half=half, _pb=pb, _ft=ft: v.tensor_scalar(
                out=qh[_b][:, sl(_half, 512)], in0=_pb[:, 0:512],
                scalar1=SM[:, 32 + _ft:33 + _ft], scalar2=SM[:, 48 + _ft:49 + _ft],
                op0=ALU.mult, op1=ALU.add),
                waits=(mm_half[half], t_qbias))
            t_q_evac[(b, half)] = ev
            evs.append(ev)
        tq = s.dma("gpsimd", qpp_d[ft], qh[b], waits=evs)
        qpp_done.append(tq)
        tsq = s.act(sqst, qh[b][:, 0:512], AF.Square, waits=evs)
        pbq = psb[4 + b]
        tn1 = s.mm(pbq[0:1, 0:512], onescol[:, 0:1], sqst, start=True, stop=False,
                   waits=[tsq, t_ones, t_qn_red.get(b)], inc=True)
        tsq2 = s.act(sqst, qh[b][:, 512:1024], AF.Square, waits=(tn1,))
        tn2 = s.mm(pbq[0:1, 0:512], onescol[:, 0:1], sqst, start=False, stop=True,
                   waits=(tsq2,), inc=True)
        trd = s.dve(lambda v, _pb=pbq, _ft=ft: v.reduce_max(
            out=SM[0:1, 160 + _ft:161 + _ft], in_=_pb[0:1, 0:512], axis=AX),
            waits=(tn2,))
        t_qn_red[b] = trd
        qpp_done.append(trd)
    t_qpp_all = lat(*qpp_done)
    t_mq1 = s.dma("gpsimd", mqrow_d, SM[0:1, 160:176], waits=t_qpp_all)
    t_maxq2 = s.dma("gpsimd", SM[:, 160:176], mqrow_d.partition_broadcast(P),
                    waits=(t_mq1,))

    qunits = []
    for wsrc, qdst, widx in ((w1, q1_d, 0), (w2, q2_d, 1)):
        for i in range(NHT):
            for ch in range(2):
                qunits.append((wsrc[sl(i, P), sl(ch, TOK)],
                               qdst[sl(i, P), sl(ch, TOK)], widx, TOK))
    for i in range(EC):
        for cc in range(6):
            c0 = cc * TOK
            c1 = min((cc + 1) * TOK, HID)
            qunits.append((w3[sl(i, P), c0:c1], q3_d[sl(i, P), c0:c1], 2, c1 - c0))
    t_gz = s.pool(lambda g: g.memset(SM[:, 180:183], 0.0))
    QS = {"in": [None, None, None, None], "acc": t_gz, "t": None, "t2": None,
          "qs": [None, None], "rgam": None, "v0": None, "p1": 0, "p2": 0,
          "loaded": set(), "ld": {}}

    def q_load(i):
        if i >= len(qunits) or i in QS["loaded"]:
            return
        QS["loaded"].add(i)
        src_, dst_, widx, cols = qunits[i]
        b = i % NQIN
        eng = "gpsimd" if i % 2 == 0 else "scalar"
        QS["ld"][i] = s.dma(eng, qin[b][:, 0:cols], src_, waits=(QS["in"][b],))

    def emit_qpass1(n):
        while n > 0 and QS["p1"] < len(qunits):
            i = QS["p1"]
            QS["p1"] += 1
            n -= 1
            for la in range(NQIN):
                q_load(i + la)
            src_, dst_, widx, cols = qunits[i]
            b = i % NQIN
            tr = s.dve(lambda v, _b=b, _c=cols: v.tensor_reduce(
                out=SM[:, 215:216], in_=qin[_b][:, 0:_c], axis=AX, op=ALU.add,
                apply_absolute_value=True), waits=(QS["ld"][i], QS["acc"]))
            ta = s.dve(lambda v, _w=widx: v.tensor_tensor(
                out=SM[:, 180 + _w:181 + _w], in0=SM[:, 180 + _w:181 + _w],
                in1=SM[:, 215:216], op=ALU.add), waits=(tr,))
            QS["in"][b] = ta
            QS["acc"] = ta

    def emit_gamma_final():
        tw = s.dma("gpsimd", gsc_d, SM[:, 180:183], waits=(QS["acc"],))
        trd = s.dma("gpsimd", grow[0:1, 0:384],
                    gsc_d.rearrange("p c -> (p c)").unsqueeze(0), waits=(tw,))
        tred = s.dve(lambda v: v.tensor_reduce(
            out=rowf[0:1, 1088:1091], in_=grow[0:1, 0:384].rearrange("o (p c) -> o c p", c=3),
            axis=AX, op=ALU.add), waits=(trd,))
        tg = s.dve(lambda v: v.tensor_scalar(out=rowf[0:1, 1092:1095], in0=rowf[0:1, 1088:1091],
                                             scalar1=1.0 / QCOUNT, scalar2=EPS_Q,
                                             op0=ALU.mult, op1=ALU.add), waits=(tred,))
        tr = s.dve(lambda v: v.reciprocal(out=rowf[0:1, 1088:1091], in_=rowf[0:1, 1092:1095]),
                   waits=(tg,))
        tw2 = s.dma("gpsimd", gsc_d[0:1, 0:3], rowf[0:1, 1088:1091], waits=(tr,))
        tb = s.dma("gpsimd", SM[:, 177:180], gsc_d[0:1, 0:3].partition_broadcast(P),
                   waits=(tw2,))
        QS["rgam"] = tb
        QS["acc"] = tb
        QS["loaded"] = set()
        QS["ld"] = {}

    def emit_qpass2(n):
        while n > 0 and QS["p2"] < len(qunits):
            i = QS["p2"]
            QS["p2"] += 1
            n -= 1
            for la in range(NQIN):
                q_load(i + la)
            src_, dst_, widx, cols = qunits[i]
            b = i % NQIN
            d1 = s.dve(lambda v, _b=b, _c=cols, _w=widx: v.tensor_scalar(
                out=tb_a2[:, 0:_c], in0=qin[_b][:, 0:_c],
                scalar1=SM[:, 177 + _w:178 + _w], scalar2=SM[:, 216:217],
                op0=ALU.mult, op1=ALU.add),
                waits=(QS["ld"][i], QS["rgam"], QS["t"], QS["v0"]))
            d2 = s.dve(lambda v, _c=cols: v.tensor_scalar(
                out=tb_b2[:, 0:_c], in0=tb_a2[:, 0:_c], scalar1=MAGIC, scalar2=1.0,
                op0=ALU.subtract, op1=ALU.min), waits=(d1, QS["t2"]))
            QS["t"] = d2
            b2 = i % 2
            t3 = s.dve(lambda v, _b=b2, _c=cols: v.tensor_scalar(
                out=qst[_b][:, 0:_c], in0=tb_b2[:, 0:_c], scalar1=-1.0,
                scalar2=1.0, op0=ALU.max, op1=ALU.mult), waits=(d2, QS["qs"][b2]))
            QS["t2"] = t3
            tso = s.dma("gpsimd", dst_, qst[b2][:, 0:cols], waits=(t3,))
            QS["qs"][b2] = tso
            QS["in"][b] = d1


    t_cast_v = s.dma("gpsimd", wbf[0], Wv, waits=(t_wtq[-1], t_qpp_all))
    t_wtk = build_WT(wbf[1], (t_cast_k, t_qpp_all))
    t_kev = {}
    kst_last = {}
    knorm_last = None
    t_kscr = []
    xnT_lastuse = [None, None]
    def k_tsp(sp_):
        bsp_ = sp_ % 2
        out = []
        for ec in range(EC):
            w = (t_xn_all, xnT_lastuse[bsp_]) if ec == 0 else ()
            out.append(s.dma("sync", xnTc[bsp_][ec], xn_d[sl(sp_, 512), sl(ec, P)],
                             waits=w, transpose=True))
        return out

    tsp_pre = k_tsp(0)
    for sp in range(16):
        bsp = sp % 2
        tsp = tsp_pre
        if sp + 1 < 16:
            tsp_pre = k_tsp(sp + 1)
        sp_last_mm = None
        for ft in range(H):
            pb = psb[ft % 2]
            mmt = None
            for ec in range(EC):
                w = []
                if ec == 0:
                    w = [tsp[-1], t_kev.get(ft % 2)]
                    if sp == 0 and ft == 0:
                        w.append(t_wtk[-1])
                mmt = s.mm(pb[:, 0:512], WTc[ec][:, sl(ft, P)], xnTc[bsp][ec],
                           start=(ec == 0), stop=(ec == EC - 1), waits=w,
                           inc=(ec == EC - 1))
            sp_last_mm = mmt
            bst = ft % 2
            ev = s.dve(lambda v, _bst=bst, _pb=pb, _ft=ft: v.tensor_scalar_add(
                out=kvst[_bst][:, 0:512], in0=_pb[:, 0:512],
                scalar1=SM[:, 80 + _ft:81 + _ft]),
                waits=(mmt, t_bkT, kst_last.get(bst), knorm_last))
            t_kev[ft % 2] = ev
            tst = s.dma("sync", kscr[ft][:, sl(sp, 512)], kvst[bst][:, 0:512],
                        waits=(ev,))
            kst_last[bst] = tst
            t_kscr.append(tst)
            tks = s.act(sqst, kvst[bst][:, 0:512], AF.Square, waits=(ev, knorm_last))
            tkm = s.mm(psb[6][0:1, 0:512], onescol[:, 0:1], sqst, start=True, stop=True,
                       waits=(tks, knorm_last), inc=True)
            tkr = s.dve(lambda v: v.reduce_max(out=rowf[0:1, 24:25],
                                               in_=psb[6][0:1, 0:512], axis=AX),
                        waits=(tkm,))
            knorm_last = s.dve(lambda v, _ft=ft: v.tensor_max(
                out=rowf[0:1, 32 + _ft:33 + _ft], in0=rowf[0:1, 32 + _ft:33 + _ft],
                in1=rowf[0:1, 24:25]), waits=(tkr, t_kn0))
        xnT_lastuse[bsp] = sp_last_mm
        emit_qpass1(17)
    emit_qpass1(len(qunits))
    emit_gamma_final()
    t_kscr_all = lat(*t_kscr)

    t_cast_o = s.dma("gpsimd", wbf[1], Wo, waits=(t_wtk[-1],))
    t_wtv = build_WT(wbf[0], [t_cast_v] + [xnT_lastuse[0], xnT_lastuse[1]])
    t_vev = {}
    vst_last = {}
    t_vscr = []
    xnT_lastuse2 = [None, None]
    def v_tsp(sp_):
        bsp_ = sp_ % 2
        out = []
        for ec in range(EC):
            w = [xnT_lastuse2[bsp_]] if ec == 0 else []
            if sp_ < 2 and ec == 0:
                w += [xnT_lastuse[0], xnT_lastuse[1], t_kscr_all]
            out.append(s.dma("sync", xnTc[bsp_][ec], xn_d[sl(sp_, 512), sl(ec, P)],
                             waits=w, transpose=True))
        return out

    tsp_pre2 = v_tsp(0)
    for sp in range(16):
        bsp = sp % 2
        tsp = tsp_pre2
        if sp + 1 < 16:
            tsp_pre2 = v_tsp(sp + 1)
        sp_last_mm = None
        for tt4 in range(4):
            gt = sp * 4 + tt4
            for rnd in range(2):
                mm_of = {}
                for j, fs in enumerate((rnd * 2, rnd * 2 + 1)):
                    pb = psb[2 + j]
                    mmt = None
                    for ec in range(EC):
                        w = []
                        if ec == 0:
                            w = [tsp[-1], t_vev.get(j)]
                            if sp == 0 and tt4 == 0 and rnd == 0:
                                w.append(t_wtv[-1])
                        mmt = s.mm(pb[:, 0:512], xnTc[bsp][ec][:, sl(tt4, P)],
                                   WTc[ec][:, sl(fs, 512)],
                                   start=(ec == 0), stop=False, waits=w)
                    mmt = s.mm(pb[:, 0:512], onesrow[0:1, :],
                               rowbf[0:1, fs * 512:(fs + 1) * 512],
                               start=False, stop=True, waits=(t_onesr, t_bvrow),
                               inc=True)
                    mm_of[j] = (fs, mmt)
                for j in range(2):
                    fs, mmt = mm_of[j]
                    stb = 2 + j
                    ev = s.act(kvst[stb][:, 0:512], psb[2 + j][:, 0:512], AF.Copy,
                               waits=(mmt, vst_last.get(stb)))
                    t_vev[j] = ev
                    lastst = None
                    for hh in range(4):
                        hd = fs * 4 + hh
                        lastst = s.dma("sync", vscr[hd][sl(gt, P), :],
                                       kvst[stb][:, sl(hh, P)], waits=(ev,))
                        t_vscr.append(lastst)
                    vst_last[stb] = lastst
                    sp_last_mm = mmt
        xnT_lastuse2[bsp] = sp_last_mm
        emit_qpass2(17)
    t_vscr_all = lat(*t_vscr)
    emit_qpass2(len(qunits))
    t_quant_all = s.pool(lambda g: g.nop(),
                         waits=(QS["qs"][0], QS["qs"][1], QS["t2"]))

    t_kload = [None, None]
    t_vload = [None, None]
    t_qload = [None, None]
    t_k_lastuse = [None, None]
    t_v_lastuse = [None, None]
    t_q_lastuse = [None, None]
    stps_read = {0: None, 1: None}
    t_pv_done = None
    t_sum_done = None
    t_at_ev = [None, None]
    t_emul_last = None
    attn_store = []

    t_kload[0] = s.dma("sync", ksb[0], kscr[0],
                       waits=(t_kscr_all, xnT_lastuse2[0], xnT_lastuse2[1],
                              t_vev.get(0), t_vev.get(1), t_quant_all))
    def load_v(hh, nb, waits):
        tks = []
        for jc in range(NJT):
            tks.append(s.dma("sync", vsb[nb][:, sl(jc, P)],
                             vscr[hh][sl(jc, P), :], waits=waits if jc == 0 else ()))
        return lat(*tks)

    t_vload[0] = load_v(0, 0, (t_vscr_all,))
    t_qload[0] = s.dma("gpsimd", qh[0], qpp_d[0],
                       waits=(t_q_evac.get((0, 0)), t_q_evac.get((0, 1)),
                              t_kscr_all, knorm_last, t_vscr_all))

    tc1 = s.dve(lambda v: v.tensor_mul(out=rowf[0:1, 48:64], in0=SM[0:1, 160:176],
                                       in1=rowf[0:1, 32:48]),
                waits=[t_maxq2, knorm_last])
    tc2 = s.act(rowf[0:1, 16:32], rowf[0:1, 48:64], AF.Ln, waits=(tc1,))
    tc3 = s.dve(lambda v: v.tensor_scalar_mul(out=rowf[0:1, 48:64],
                                              in0=rowf[0:1, 16:32], scalar1=0.5),
                waits=(tc2,))
    tc4 = s.act(rowf[0:1, 16:32], rowf[0:1, 48:64], AF.Exp, waits=(tc3,))
    tc5 = s.dve(lambda v: v.scalar_tensor_tensor(
        out=rowf[0:1, 48:64], in0=rowf[0:1, 16:32], scalar=-1.0,
        in1=SM[0:1, 0:16], op0=ALU.mult, op1=ALU.mult), waits=(tc4,))
    tc6 = s.dve(lambda v: v.tensor_scalar_add(out=rowf[0:1, 16:32],
                                              in0=rowf[0:1, 48:64],
                                              scalar1=-C_EXTRA), waits=(tc5,))
    t_nc1 = s.dma("gpsimd", mqrow_d, rowf[0:1, 16:32], waits=(tc6,))
    t_negC_all = s.dma("gpsimd", SM[:, 224:240], mqrow_d.partition_broadcast(P),
                       waits=(t_nc1,))

    HH = HID // 2
    ffn_t = []
    units_t = []
    for (qsrc, tdst) in ((q1_d, w1t_d), (q2_d, w2t_d)):
        for dc in range(EC):
            for ch in range(2):
                units_t.append((qsrc[sl(ch, HH), sl(dc, P)],
                                tdst[sl(dc, P), sl(ch, HH)]))
    wr_prev = [None, None]
    wstg_i = [0]

    def emit_wstg(n):
        while n > 0 and wstg_i[0] < len(units_t):
            i = wstg_i[0]
            wstg_i[0] += 1
            n -= 1
            b = i % 2
            w = [wr_prev[b]]
            if i < 2:
                w.append(t_quant_all)
            ttr = s.dma("sync", wstg2[b][:, 0:HH], units_t[i][0], waits=w,
                        transpose=True)
            wr_prev[b] = s.dma("sync", units_t[i][1], wstg2[b][:, 0:HH],
                               waits=(ttr,))
            ffn_t.append(wr_prev[b])

    for h in range(H):
        hb = h % 2
        t_egen = s.act(Eh, Wal, AF.Exp, scale=SM[:, 112 + h:113 + h],
                       waits=(t_negs, t_emul_last))
        t_negC = t_negC_all

        if h + 1 < H:
            nb = (h + 1) % 2
            t_kload[nb] = s.dma("sync", ksb[nb], kscr[h + 1],
                                waits=(t_k_lastuse[nb],
                                       t_quant_all if h == 0 else None))
            t_vload[nb] = load_v(h + 1, nb, (t_v_lastuse[nb],))
            t_qload[nb] = s.dma("gpsimd", qh[nb], qpp_d[h + 1],
                                waits=(t_q_lastuse[nb],
                                       t_kscr_all if h == 0 else None,
                                       knorm_last if h == 0 else None))

        st_tok = {}
        emul = {}

        def emit_st(jt, _h=h, _hb=hb):
            bb = jt % 2
            w = [stps_read[bb]]
            if jt < 2:
                w += [t_kload[_hb], t_qload[_hb]]
            s.mm(psb[bb * 2][:, 0:512], ksb[_hb][:, sl(jt, P)], qh[_hb][:, 0:512],
                 start=True, stop=True, waits=w)
            st_tok[jt] = s.mm(psb[bb * 2 + 1][:, 0:512], ksb[_hb][:, sl(jt, P)],
                              qh[_hb][:, 512:1024], start=True, stop=True, inc=True)

        emit_st(0)
        pv_last = None
        for jt in range(NJT):
            bb = jt % 2
            texp = s.act(p0sb[bb][:, 0:512], psb[bb * 2][:, 0:512], AF.Exp,
                         bias=SM[:, 224 + h:225 + h], scale=SM[:, h:h + 1],
                         waits=(st_tok[jt], t_negC, emul.get(jt - 2)))
            texp2 = s.act(p0sb[bb][:, 512:1024], psb[bb * 2 + 1][:, 0:512], AF.Exp,
                          bias=SM[:, 224 + h:225 + h], scale=SM[:, h:h + 1])
            stps_read[bb] = texp2
            if jt + 1 < NJT:
                emit_st(jt + 1)
            off = 8192 - 128 * jt
            tmul = s.dve(lambda v, _bb=bb, _o=off: v.tensor_tensor(
                out=ptsb[_bb], in0=p0sb[_bb], in1=Eh[:, _o:_o + 1024], op=ALU.mult),
                waits=(texp2, t_egen,
                       t_vscr_all if (h == 0 and jt < 2) else None))
            emul[jt] = tmul
            t_emul_last = tmul
            w_pv = [tmul]
            if jt == 0:
                w_pv += [t_vload[hb], t_pv_done, t_sum_done]
            s.mm(psb[4][:, 0:512], vsb[hb][:, sl(jt, P)], ptsb[bb][:, 0:512],
                 start=(jt == 0), stop=(jt == NJT - 1), waits=w_pv)
            s.mm(psb[5][:, 0:512], vsb[hb][:, sl(jt, P)], ptsb[bb][:, 512:1024],
                 start=(jt == 0), stop=(jt == NJT - 1))
            s.mm(psb[6][0:1, 0:512], onescol[:, 0:1], ptsb[bb][:, 0:512],
                 start=(jt == 0), stop=(jt == NJT - 1))
            pv_last = s.mm(psb[7][0:1, 0:512], onescol[:, 0:1], ptsb[bb][:, 512:1024],
                           start=(jt == 0), stop=(jt == NJT - 1), inc=True)
        t_k_lastuse[hb] = st_tok[NJT - 1]
        t_q_lastuse[hb] = st_tok[NJT - 1]
        t_v_lastuse[hb] = pv_last
        tse = s.act(rowf[0:1, 64:576], psb[6][0:1, 0:512], AF.Copy, waits=(pv_last,))
        tse2 = s.act(rowf[0:1, 576:1088], psb[7][0:1, 0:512], AF.Copy)
        trc = s.dve(lambda v: v.reciprocal(out=rowf[0:1, 1088:2112],
                                           in_=rowf[0:1, 64:1088]), waits=(tse2,))
        trcb = s.dve(lambda v: v.tensor_copy(out=rowbf[0:1, 4096:5120],
                                             in_=rowf[0:1, 1088:2112]), waits=(trc,))
        tb1 = s.mm(psb[2][:, 0:512], onesrow[0:1, :], rowbf[0:1, 4096:4608],
                   start=True, stop=True, waits=(trcb, stps_read[1]))
        tb2 = s.mm(psb[3][:, 0:512], onesrow[0:1, :], rowbf[0:1, 4608:5120],
                   start=True, stop=True, inc=True)
        trb = s.act(rbsb[:, 0:512], psb[2][:, 0:512], AF.Copy, waits=(tb2,))
        trb2 = s.act(rbsb[:, 512:1024], psb[3][:, 0:512], AF.Copy)
        stps_read[1] = trb2
        ab = h % 2
        tat = s.dve(lambda v, _ab=ab: v.tensor_tensor(out=atst[_ab][:, 0:512],
                                                      in0=psb[4][:, 0:512],
                                                      in1=rbsb[:, 0:512],
                                                      op=ALU.mult),
                    waits=(trb2, pv_last, t_at_ev[ab],
                           t_vscr_all if h < 2 else None))
        tat2 = s.dve(lambda v, _ab=ab: v.tensor_tensor(out=atst[_ab][:, 512:1024],
                                                       in0=psb[5][:, 0:512],
                                                       in1=rbsb[:, 512:1024],
                                                       op=ALU.mult))
        t_pv_done = tat2
        t_sum_done = tse2
        tats = s.dma("gpsimd", attn_d[h], atst[ab], waits=(tat2,))
        t_at_ev[ab] = tats
        attn_store.append(tats)
        if h >= 1:
            emit_wstg(5)
    emit_wstg(len(units_t))
    t_ffnw_all = lat(*ffn_t)

    t_wto = build_WT(wbf[1], [t_cast_o, stps_read[0], stps_read[1], t_pv_done,
                              t_quant_all, t_emul_last])
    t_atch = [None, None]
    t_p4ev = {}
    t_xres_store = []
    t_xn2_store = []
    xslab_last = [last_apply[0], last_apply[1]]
    xrst_last = [None, None]
    xn2st_last = [None, None]
    p4items = [(tt_, ec_) for tt_ in range(8) for ec_ in range(EC)]
    tld_tok = {}

    def emit_tld(k):
        if k >= len(p4items) or k in tld_tok:
            return
        tt_, ec_ = p4items[k]
        w = [t_atch[0], t_atch[1]]
        if tt_ == 0:
            w.append(attn_store[ec_])
        tld_tok[k] = s.dma("sync", atch[k % 2][:, 0:TOK], attn_d[ec_], waits=w)

    for tt in range(8):
        bx = tt % 2
        txo = s.dma("sync", xslab[bx], xo[sl(tt, P), :], waits=(xslab_last[bx],))
        mm_fs = {}
        for ec in range(EC):
            k = tt * EC + ec
            bc = k % 2
            emit_tld(k)
            emit_tld(k + 1)
            for fs in range(4):
                w2_ = [tld_tok[k]]
                if ec == 0:
                    w2_ += [t_wto[-1], t_p4ev.get(fs)]
                mm_fs[fs] = s.mm(psb[4 + fs][:, 0:512], atch[bc][:, sl(tt, P)],
                                 WTc[ec][:, sl(fs, 512)],
                                 start=(ec == 0), stop=False, waits=w2_,
                                 inc=(fs == 3))
            t_atch[bc] = mm_fs[3]
        for fs in range(4):
            mm_fs[fs] = s.mm(psb[4 + fs][:, 0:512], onesrow[0:1, :],
                             rowbf[0:1, E + fs * 512:E + (fs + 1) * 512],
                             start=False, stop=True, waits=(t_borow,), inc=True)
        tt_ev = []
        for fs in range(4):
            ev = s.dve(lambda v, _bx=bx, _fs=fs: v.tensor_tensor(
                out=xrst[_bx][:, sl(_fs, 512)], in0=psb[4 + _fs][:, 0:512],
                in1=xslab[_bx][:, sl(_fs, 512)], op=ALU.add),
                waits=(mm_fs[fs], txo, xrst_last[bx]))
            t_p4ev[fs] = ev
            tt_ev.append(ev)
        t9 = ln_apply(xrst[bx], xn2st[bx], tt_ev + [xn2st_last[bx]])
        ts1 = s.dma("sync", xres_d[sl(tt, P), :], xrst[bx], waits=(t9,))
        ts2 = s.dma("sync", xn2_d[sl(tt, P), :], xn2st[bx], waits=(t9,))
        xrst_last[bx] = ts1
        xn2st_last[bx] = ts2
        xslab_last[bx] = t_p4ev[3]
        t_xres_store.append(ts1)
        t_xn2_store.append(ts2)
    t_xres_all = lat(*t_xres_store)
    t_xn2_all = lat(*t_xn2_store)

    t_xn2T = []
    for ec in range(EC):
        w = [t_xn2_all, t_p4ev[3]] if ec == 0 else ()
        t_xn2T.append(s.dma("sync", xn2Tc[ec], xn2_d[:, sl(ec, P)], waits=w,
                            transpose=True))
    t_out_final = []
    f1_psum_free = [stps_read[0], stps_read[0], stps_read[1], stps_read[1],
                    t_p4ev[0], t_p4ev[1], t_p4ev[2], t_p4ev[3]]
    w1sl_last = [None, None]
    hid_last = lat(t_xres_all, t_xn2_all)
    f2_done_prev = None
    for th in range(2):
        hs_ev = []
        for hs in range(11):
            tw1 = s.dma("sync", w1sl[0].rearrange("p (ec c) -> p ec c", c=512),
                        w1t_d[:, sl(hs, 512)].rearrange("(ec p) c -> p ec c", p=P),
                        waits=[t_ffnw_all, w1sl_last[0]] if hs == 0 else (w1sl_last[0],))
            tw2 = s.dma("sync", w1sl[1].rearrange("p (ec c) -> p ec c", c=512),
                        w2t_d[:, sl(hs, 512)].rearrange("(ec p) c -> p ec c", p=P),
                        waits=[t_ffnw_all, w1sl_last[1]] if hs == 0 else (w1sl_last[1],))
            mm1 = {}
            mm2 = {}
            for dc in range(EC):
                for tt in range(4):
                    w = []
                    if dc == 0:
                        w = [tw1, f1_psum_free[tt],
                             t_xn2T[-1] if hs == 0 and th == 0 else None,
                             f2_done_prev if hs == 0 else None]
                    mm1[tt] = s.mm(psb[tt][:, 0:512],
                                   xn2Tc[dc][:, th * 512 + tt * 128:th * 512 + tt * 128 + 128],
                                   w1sl[0][:, sl(dc, 512)],
                                   start=(dc == 0), stop=(dc == EC - 1), waits=w,
                                   inc=(dc == EC - 1))
                    w = []
                    if dc == 0:
                        w = [tw2, f1_psum_free[4 + tt]]
                    mm2[tt] = s.mm(psb[4 + tt][:, 0:512],
                                   xn2Tc[dc][:, th * 512 + tt * 128:th * 512 + tt * 128 + 128],
                                   w1sl[1][:, sl(dc, 512)],
                                   start=(dc == 0), stop=(dc == EC - 1), waits=w,
                                   inc=(dc == EC - 1))
            last_w1_mm = mm2[3]
            w1sl_last[0] = last_w1_mm
            w1sl_last[1] = last_w1_mm
            for tt in range(4):
                bsg = tt % 2
                tsg = s.act(sgst[bsg], psb[tt][:, 0:512], AF.Sigmoid,
                            waits=(mm1[tt], f1_psum_free[(tt + 3) % 4]))
                f1_psum_free[tt] = tsg
                thd = s.dve(lambda v, _t=tt, _hs=hs, _b=bsg: v.tensor_tensor(
                    out=hid_tt[_t][:, _hs * 512:_hs * 512 + 512],
                    in0=sgst[_b], in1=psb[4 + _t][:, 0:512], op=ALU.mult),
                    waits=(tsg, mm2[tt], hid_last))
                f1_psum_free[4 + tt] = thd
                hs_ev.append(thd)
        thw_parts = []
        for _t in range(4):
            thw_parts.append(s.dma("sync", hid_d[sl(_t, P), :], hid_tt[_t],
                                   waits=hs_ev if _t == 0 else ()))
        thw = lat(*thw_parts)
        hid_last = thw
        t_hT = []
        for hc in range(NHT):
            w = [thw] if hc == 0 else ()
            t_hT.append(s.dma("sync", hidTc[hc], hid_d[:, sl(hc, P)], waits=w,
                              transpose=True))
        q3_last = [None, None]
        f2_mm = {}
        for fh in range(2):
            for hc in range(NHT):
                bq3 = hc % 2
                w = [q3_last[bq3]]
                if fh == 0 and hc == 0:
                    w += [t_ffnw_all]
                if fh == 0 and hc == 0:
                    w += [t_at_ev[0], t_at_ev[1]]
                tq3 = s.dma("sync", q3ch[bq3], q3_d[sl(fh, TOK), sl(hc, P)], waits=w,
                            transpose=True)
                for tt in range(4):
                    w2_ = [tq3, t_hT[-1] if hc == 0 else None]
                    if hc == 0:
                        w2_.append(f2_mm.get((1 - fh, tt)))
                        w2_.append(f1_psum_free[2 * tt])
                        w2_.append(f1_psum_free[2 * tt + 1])
                    s.mm(psb[2 * tt][:, 0:512], hidTc[hc][:, sl(tt, P)],
                         q3ch[bq3][:, 0:512], start=(hc == 0), stop=(hc == NHT - 1),
                         waits=w2_)
                    f2_mm[(fh, tt)] = s.mm(psb[2 * tt + 1][:, 0:512],
                                           hidTc[hc][:, sl(tt, P)],
                                           q3ch[bq3][:, 512:1024],
                                           start=(hc == 0), stop=(hc == NHT - 1),
                                           waits=(),
                                           inc=(hc == NHT - 1 or tt == 3))
                q3_last[bq3] = f2_mm[(fh, 3)]
            for tt in range(4):
                bxr = tt % 2
                bo2 = tt % 2
                txr = s.dma("sync", xrtile[bxr], xres_d[sl(th * 4 + tt, P), sl(fh, TOK)],
                            waits=(t_xres_all if fh == 0 and tt < 2 and th == 0 else None,
                                   t_ffnw_all if fh == 0 and tt < 2 and th == 0 else None,
                                   t_p4ev.get(("xr", bxr))))
                e1 = s.dve(lambda v, _tt=tt, _b=bxr, _o=bo2: v.tensor_tensor(
                    out=outst2[_o][:, 0:512], in0=psb[2 * _tt][:, 0:512],
                    in1=xrtile[_b][:, 0:512], op=ALU.add),
                    waits=(f2_mm[(fh, tt)], txr, t_p4ev.get(("out", bo2))))
                e2 = s.dve(lambda v, _tt=tt, _b=bxr, _o=bo2: v.tensor_tensor(
                    out=outst2[_o][:, 512:1024], in0=psb[2 * _tt + 1][:, 0:512],
                    in1=xrtile[_b][:, 512:1024], op=ALU.add))
                t_p4ev[("xr", bxr)] = e2
                f1_psum_free[2 * tt] = e2
                f1_psum_free[2 * tt + 1] = e2
                tso = s.dma_out("gpsimd", yout[sl(th * 4 + tt, P), sl(fh, TOK)],
                                outst2[bo2], waits=(e2,))
                t_p4ev[("out", bo2)] = tso
                t_out_final.append(tso)
                f2_done_prev = lat(t_p4ev.get(("xr", 0)), t_p4ev.get(("xr", 1)))
        if th == 0:
            hid_last = lat(hid_last, t_p4ev.get(("out", 0)), t_p4ev.get(("out", 1)))

    s.fin("gpsimd", lat(*t_out_final))
    s.fin("sync", lat(*t_out_final))

    return nc, s


def finalize(nc, s):
    sems = {}
    ctxs = []
    for name in SEM_NAMES:
        cm = nc.semaphore(name)
        sems[name] = cm.__enter__()
        ctxs.append(cm)
    blk = nc.Block()
    block = blk.__enter__()
    block.sync(lambda e: s.emit(e, "sync", sems))
    block.gpsimd(lambda e: s.emit(e, "gpsimd", sems))
    block.vector(lambda e: s.emit(e, "vector", sems))
    block.scalar(lambda e: s.emit(e, "scalar", sems))
    block.tensor(lambda e: s.emit(e, "tensor", sems))
    blk.__exit__(None, None, None)
    for cm in reversed(ctxs):
        cm.__exit__(None, None, None)
    return nc


def build(dbg=0):
    nc, s = build_program(dbg=dbg)
    return finalize(nc, s)


def make_inputs(inputs, core):
    x = np.asarray(inputs["L0_attn_x"]).reshape(T, E)
    im = {
        "x_full": np.ascontiguousarray(x),
        "x_own": np.ascontiguousarray(x[core * TOK:(core + 1) * TOK]),
        "Wq": np.asarray(inputs["L0_attn_Wq"]),
        "Wk": np.asarray(inputs["L0_attn_Wk"]),
        "Wv": np.asarray(inputs["L0_attn_Wv"]),
        "Wo": np.asarray(inputs["L0_attn_Wo"]),
        "bq": np.asarray(inputs["L0_attn_bq"]).reshape(1, E),
        "bk": np.asarray(inputs["L0_attn_bk"]).reshape(1, E),
        "bv": np.asarray(inputs["L0_attn_bv"]).reshape(1, E),
        "bo": np.asarray(inputs["L0_attn_bo"]).reshape(1, E),
        "slopes": np.asarray(inputs["L0_attn_slopes"]).reshape(1, H),
        "w1": np.asarray(inputs["L0_ffn_w1"]),
        "w2": np.asarray(inputs["L0_ffn_w2"]),
        "w3": np.asarray(inputs["L0_ffn_w3"]),
        "wofs": np.full((P, 1), core * TOK - 8192, np.float32),
    }
    return im


_NC_CACHE = {}


def kernel(**inputs):
    from concourse.bass_utils import run_bass_kernel_spmd
    if "nc" not in _NC_CACHE:
        _NC_CACHE["nc"] = build(dbg=0)
    nc = _NC_CACHE["nc"]
    ims = [make_inputs(inputs, c) for c in range(8)]
    res = run_bass_kernel_spmd(nc, ims, core_ids=list(range(8)))
    return np.concatenate([res.results[c]["y"] for c in range(8)], axis=0).reshape(1, T, E)


# version 11
# speedup vs baseline: 1.2793x; 1.0097x over previous
import numpy as np
import concourse.bass as bass
import concourse.mybir as mybir
from bass_rust import ReduceOp

F32 = mybir.dt.float32
F16 = mybir.dt.float16
BF16 = mybir.dt.bfloat16
AF = mybir.ActivationFunctionType
ALU = mybir.AluOpType
AX = mybir.AxisListType.X

P = 128
T = 8192
TOK = 1024
E = 2048
H = 16
D = 128
HID = 5632
EC = E // P
NJT = T // P
NHT = HID // P
SCALING = float(D) ** -0.5
EPS_LN = 1e-5
EPS_Q = 1e-8
MAGIC = 1.5 * 2 ** 23
C_EXTRA = 2.0
NQIN = 4
QCOUNT = float(HID * E)

ENGS = ("sync", "gpsimd", "vector", "scalar", "tensor")
SEM_NAMES = ["qs", "qg", "qa", "pe", "act", "dve", "pool", "outq"]
QSEM = {"sync": "qs", "gpsimd": "qg", "scalar": "qa", "vector": "qa"}
DMA_SEMS = {"qs", "qg", "qa", "outq"}


def sl(i, n):
    return slice(i * n, (i + 1) * n)


def lat(*toks):
    best = {}
    for t in toks:
        if t is None:
            continue
        items = t if isinstance(t, list) else [t]
        for u in items:
            if u is None:
                continue
            s2, v2 = u
            if best.get(s2, -1) < v2:
                best[s2] = v2
    return [(k, v) for k, v in best.items()]


class Sched:
    def __init__(self):
        self.ops = {e: [] for e in ENGS}
        self.cnt = {k: 0 for k in SEM_NAMES}

    def _op(self, eng, fn, waits, inc, amt):
        ws = lat(*waits) if waits else []
        ws = [(sn, min(v + 32, self.cnt[sn]) if sn in DMA_SEMS else v)
              for (sn, v) in ws]
        tok = None
        if inc is not None:
            self.cnt[inc] += amt
            tok = (inc, self.cnt[inc])
        self.ops[eng].append((fn, ws, inc, amt))
        return tok

    def dma(self, eng, out, in_, waits=(), **kw):
        return self._op(eng, lambda e, o=out, i=in_, k=kw: e.dma_start(out=o, in_=i, **k),
                        waits, QSEM[eng], 16)

    def dma_out(self, eng, out, in_, waits=(), **kw):
        return self._op(eng, lambda e, o=out, i=in_, k=kw: e.dma_start(out=o, in_=i, **k),
                        waits, "outq", 16)

    def mm(self, out, lhsT, rhs, start, stop, waits=(), inc=False):
        return self._op("tensor",
                        lambda e, o=out, l=lhsT, r=rhs, st=start, sp=stop:
                        e.matmul(o, l, r, start=st, stop=sp),
                        waits, "pe" if inc else None, 1)

    def act(self, out, in_, func, bias=None, scale=None, accum_out=None, waits=()):
        def fn(e, o=out, i=in_, f=func, b=bias, sc=scale, ao=accum_out):
            kw = {}
            if b is not None:
                kw["bias"] = b
            if sc is not None:
                kw["scale"] = sc
            if ao is not None:
                kw["accum_out"] = ao
            return e.activation(out=o, in_=i, func=f, **kw)
        return self._op("scalar", fn, waits, "act", 1)

    def dve(self, fn, waits=()):
        return self._op("vector", fn, waits, "dve", 1)

    def pool(self, fn, waits=()):
        return self._op("gpsimd", fn, waits, "pool", 1)

    def fin(self, eng, waits):
        return self._op(eng, lambda e: e.nop(), waits, None, 1)

    def emit(self, eng_obj, eng_name, sems):
        hw = {}
        for fn, waits, inc, amt in self.ops[eng_name]:
            for (sname, v) in waits:
                if hw.get(sname, 0) < v:
                    eng_obj.wait_ge(sems[sname], v)
                    hw[sname] = v
            inst = fn(eng_obj)
            if inc is not None:
                inst.then_inc(sems[inc], amt)


def build_program(dbg=0):
    nc = bass.Bass("TRN2", target_bir_lowering=False, debug=False)
    s = Sched()
    dk = "ExternalOutput" if dbg else "Internal"

    xf = nc.dram_tensor("x_full", [T, E], F32, kind="ExternalInput").ap()
    xo = nc.dram_tensor("x_own", [TOK, E], F32, kind="ExternalInput").ap()
    Wq = nc.dram_tensor("Wq", [E, E], F32, kind="ExternalInput").ap()
    Wk = nc.dram_tensor("Wk", [E, E], F32, kind="ExternalInput").ap()
    Wv = nc.dram_tensor("Wv", [E, E], F32, kind="ExternalInput").ap()
    Wo = nc.dram_tensor("Wo", [E, E], F32, kind="ExternalInput").ap()
    bqv = nc.dram_tensor("bq", [1, E], F32, kind="ExternalInput").ap()
    bkv = nc.dram_tensor("bk", [1, E], F32, kind="ExternalInput").ap()
    bvv = nc.dram_tensor("bv", [1, E], F32, kind="ExternalInput").ap()
    bov = nc.dram_tensor("bo", [1, E], F32, kind="ExternalInput").ap()
    slopes = nc.dram_tensor("slopes", [1, H], F32, kind="ExternalInput").ap()
    w1 = nc.dram_tensor("w1", [HID, E], F32, kind="ExternalInput").ap()
    w2 = nc.dram_tensor("w2", [HID, E], F32, kind="ExternalInput").ap()
    w3 = nc.dram_tensor("w3", [E, HID], F32, kind="ExternalInput").ap()
    wofs = nc.dram_tensor("wofs", [P, 1], F32, kind="ExternalInput").ap()
    yout = nc.dram_tensor("y", [TOK, E], F32, kind="ExternalOutput").ap()

    xn_d = nc.dram_tensor("xn_d", [T, E], BF16, kind=dk).ap()
    xno_d = nc.dram_tensor("xno_d", [TOK, E], BF16, kind="Internal").ap()
    kscr = nc.dram_tensor("kscr", [H, P, T], BF16, kind=dk).ap()
    vscr = nc.dram_tensor("vscr", [H, T, D], BF16, kind=dk).ap()
    qpp_d = nc.dram_tensor("qpp_d", [H, P, TOK], BF16, kind=dk).ap()
    attn_d = nc.dram_tensor("attn_d", [H, P, TOK], BF16, kind=dk).ap()
    wbf = [nc.dram_tensor(f"wbf{i}", [E, E], BF16, kind="Internal").ap() for i in range(2)]
    q1_d = nc.dram_tensor("q1_d", [HID, E], BF16, kind=dk).ap()
    q2_d = nc.dram_tensor("q2_d", [HID, E], BF16, kind="Internal").ap()
    q3_d = nc.dram_tensor("q3_d", [E, HID], BF16, kind=dk).ap()
    w1t_d = nc.dram_tensor("w1t_d", [E, HID], BF16, kind="Internal").ap()
    w2t_d = nc.dram_tensor("w2t_d", [E, HID], BF16, kind="Internal").ap()
    q3t_d = nc.dram_tensor("q3t_d", [HID, E], BF16, kind="Internal").ap()
    xres_d = nc.dram_tensor("xres_d", [TOK, E], F32, kind=dk).ap()
    xn2_d = nc.dram_tensor("xn2_d", [TOK, E], BF16, kind=dk).ap()
    hid_d = nc.dram_tensor("hid_d", [TOK // 2, HID], BF16, kind=dk).ap()
    mqrow_d = nc.dram_tensor("mqrow_d", [1, H], F32, kind="Internal").ap()
    negc_d = nc.dram_tensor("negc_d", [H, 1, 1], F32, kind="Internal").ap()
    gsc_d = nc.dram_tensor("gsc_d", [P, 3], F32, kind="Internal").ap()

    SM = nc.alloc_sbuf_tensor("SM", [P, 256], F32).ap()
    rowf = nc.alloc_sbuf_tensor("rowf", [1, 2112], F32).ap()
    rowbf = nc.alloc_sbuf_tensor("rowbf", [1, 5120], BF16).ap()
    onescol = nc.alloc_sbuf_tensor("onescol", [P, 2], BF16).ap()
    onesrow = nc.alloc_sbuf_tensor("onesrow", [1, 128], BF16).ap()
    grow = nc.alloc_sbuf_tensor("grow", [1, 384], F32).ap()

    rem = nc.sbuf_bytes_remaining - 512
    arena = nc.alloc_sbuf_tensor("arena", [P, rem], mybir.dt.uint8).ap()
    assert rem >= 190464, f"arena too small: {rem}"

    def reg(off, cols, dtype):
        esz = {F32: 4, BF16: 2, F16: 2}[dtype]
        assert off + cols * esz <= rem, (off, cols)
        return arena[:, off:off + cols * esz].bitcast(dtype)

    xslab = [reg(i * 8192, E, F32) for i in range(2)]
    kvst = [reg(i * 4096, E, BF16) for i in range(4)]
    qh = [reg(i * 2048, TOK, BF16) for i in range(2)]
    p0sb = [reg(4096 + i * 2048, TOK, BF16) for i in range(2)]
    ptsb = [reg(8192 + i * 2048, TOK, BF16) for i in range(2)]
    atst = [reg(12288 + i * 2048, TOK, BF16) for i in range(2)]
    xnst = [reg(16384 + i * 4096, E, BF16) for i in range(2)]
    atch = [reg(16384 + i * 2048, TOK, BF16) for i in range(2)]
    sgst = [reg(20480 + i * 1024, 512, BF16) for i in range(2)]
    q3ch = [reg(16384 + i * 2048, TOK, BF16) for i in range(2)]
    WTc = [reg(24576 + ec * E * 2, E, BF16) for ec in range(EC)]
    WalRaw = reg(24576, 9216, F32)
    qin = [reg(174080 + i * 4096, TOK, F32) for i in range(4)]
    tb_a2 = reg(108544, TOK, F32)
    tb_b2 = reg(112640, TOK, BF16)
    qst = [reg(114688 + i * 2048, TOK, BF16) for i in range(2)]
    Eh = reg(43008, 9216, BF16)
    rbsb = reg(61440, TOK, F32)
    xn2Tc = [reg(24576 + ec * TOK * 2, TOK, BF16) for ec in range(EC)]
    w1sl = [reg(57344 + i * 16384, EC * 512, BF16) for i in range(2)]
    Wal = reg(90112, 9216, F16)
    xnoTc = [reg(108544 + ec * 2048, TOK, BF16) for ec in range(EC)]
    ksb = [reg(108544 + b * 16384, T, BF16) for b in range(2)]
    xrst = [reg(108544 + i * 8192, E, F32) for i in range(2)]
    xn2st = [reg(124928 + i * 4096, E, BF16) for i in range(2)]
    hid_tt = [reg(108544 + i * 11264, HID, BF16) for i in range(4)]
    hidTc = [reg(108544 + hc * 1024, 512, BF16) for hc in range(NHT)]
    xnTc = [[reg(141312 + b * 16384 + ec * 1024, 512, BF16) for ec in range(EC)]
            for b in range(2)]
    vsb = [reg(141312 + b * 16384, T, BF16) for b in range(2)]
    outst2 = [reg(174080 + i * 4096, TOK, F32) for i in range(2)]
    xrtile = [reg(182272 + i * 4096, TOK, F32) for i in range(2)]
    wstg2 = [reg(174080 + b * 8192, 4096, BF16) for b in range(2)]
    sqst = reg(20480, 512, BF16)

    psb = [nc.alloc_psum_tensor(f"pb{i}", [P, 512], F32) for i in range(8)]

    t_slopes = s.dma("sync", rowf[0:1, 0:H], slopes)
    t_wofs = s.dma("sync", SM[:, 176:177], wofs)
    t_bvrow = s.dma("gpsimd", rowbf[0:1, 0:E], bvv)
    t_borow = s.dma("gpsimd", rowbf[0:1, E:2 * E], bov)
    t_bqT = s.dma("gpsimd", SM[:, 64:80], bqv.rearrange("o (c p) -> p (o c)", p=P),
                  allow_slow_non_contiguous=True)
    t_bkT = s.dma("gpsimd", SM[:, 80:96], bkv.rearrange("o (c p) -> p (o c)", p=P),
                  allow_slow_non_contiguous=True)
    t_bvT = s.dma("gpsimd", SM[:, 96:112], bvv.rearrange("o (c p) -> p (o c)", p=P),
                  allow_slow_non_contiguous=True)

    t_slopec = s.dma("sync", SM[:, 0:16], slopes.partition_broadcast(P))
    t_rslope = s.dve(lambda v: v.reciprocal(out=SM[:, 16:32], in_=SM[:, 0:16]),
                     waits=(t_slopec,))
    t_sfold = s.dve(lambda v: v.tensor_scalar_mul(out=SM[:, 32:48], in0=SM[:, 16:32],
                                                  scalar1=SCALING), waits=(t_rslope,))
    t_qbias = s.dve(lambda v: v.tensor_mul(out=SM[:, 48:64], in0=SM[:, 64:80],
                                           in1=SM[:, 32:48]), waits=(t_sfold, t_bqT))
    t_negs = s.dve(lambda v: v.tensor_scalar_mul(out=SM[:, 112:128], in0=SM[:, 0:16],
                                                 scalar1=-1.0), waits=(t_slopec,))
    t_ones = s.dve(lambda v: v.memset(onescol[:, 0:1], 1.0))
    t_onesr = s.dve(lambda v: v.memset(onesrow[0:1, :], 1.0))
    t_eps = s.dve(lambda v: v.memset(SM[:, 213:214], EPS_LN))
    t_nh = s.dve(lambda v: v.memset(SM[:, 214:215], -0.5))
    t_mg = s.dve(lambda v: v.memset(SM[:, 216:217], MAGIC))
    t_one = s.dve(lambda v: v.memset(SM[:, 217:218], 1.0))
    t_zero = s.dve(lambda v: v.memset(SM[:, 218:219], 0.0))
    t_kn0 = s.dve(lambda v: v.memset(rowf[0:1, 32:48], 0.0))

    t_wraw = s.pool(lambda g: g.iota(out=WalRaw, pattern=[[1, 9216]], base=0,
                                     channel_multiplier=-1,
                                     allow_small_or_imprecise_dtypes=True))
    t_wal = s.act(Wal, WalRaw, AF.Abs, bias=SM[:, 176:177], waits=(t_wraw, t_wofs))

    def ln_apply(xbuf, dst_bf, waits):
        waits = lat(*waits) if not isinstance(waits, list) else lat(*waits)
        t1 = s.dve(lambda v: v.bn_stats(out=SM[:, 184:190], in_=xbuf[:, 0:512]),
                   waits=waits)
        t2 = s.dve(lambda v: v.bn_stats(out=SM[:, 190:196], in_=xbuf[:, 512:1024]))
        t3 = s.dve(lambda v: v.bn_stats(out=SM[:, 196:202], in_=xbuf[:, 1024:1536]))
        t4 = s.dve(lambda v: v.bn_stats(out=SM[:, 202:208], in_=xbuf[:, 1536:2048]))
        t5 = s.dve(lambda v: v.bn_aggr(out=SM[:, 208:210], in_=SM[:, 184:208]),
                   waits=(t4,))
        t6 = s.act(SM[:, 210:211], SM[:, 209:210], AF.Ln, bias=SM[:, 213:214],
                   waits=(t5, t_eps))
        t7 = s.act(SM[:, 211:212], SM[:, 210:211], AF.Exp, scale=SM[:, 214:215],
                   waits=(t6, t_nh))
        t8 = s.dve(lambda v: v.scalar_tensor_tensor(out=SM[:, 212:213],
                                                    in0=SM[:, 208:209], scalar=-1.0,
                                                    in1=SM[:, 211:212],
                                                    op0=ALU.mult, op1=ALU.mult),
                   waits=(t7,))
        t9 = s.dve(lambda v: v.tensor_scalar(out=dst_bf, in0=xbuf,
                                             scalar1=SM[:, 211:212],
                                             scalar2=SM[:, 212:213],
                                             op0=ALU.mult, op1=ALU.add),
                   waits=(t8,))
        return t9

    last_apply = [None, None]
    t_xn_store = []
    for i in range(NJT):
        b = i % 2
        qeng = "sync" if b == 0 else "gpsimd"
        tl = s.dma(qeng, xslab[b], xf[sl(i, P), :], waits=(last_apply[b],))
        t9 = ln_apply(xslab[b], xnst[b], (tl, t_wal if i == 0 else None))
        last_apply[b] = t9
        t_xn_store.append(s.dma(qeng, xn_d[sl(i, P), :], xnst[b], waits=(t9,)))
    t_xno_store = []
    for i in range(8):
        b = i % 2
        qeng = "sync" if b == 0 else "gpsimd"
        tl = s.dma(qeng, xslab[b], xo[sl(i, P), :], waits=(last_apply[b],))
        t9 = ln_apply(xslab[b], xnst[b], (tl,))
        last_apply[b] = t9
        t_xno_store.append(s.dma(qeng, xno_d[sl(i, P), :], xnst[b], waits=(t9,)))
    t_xn_all = t_xn_store[-1]
    t_xno_all = t_xno_store[-1]

    t_cast_q = s.dma("gpsimd", wbf[0], Wq)
    t_cast_k = s.dma("gpsimd", wbf[1], Wk)

    def build_WT(src, waits):
        toks = []
        for ec in range(EC):
            toks.append(s.dma("sync", WTc[ec], src[:, sl(ec, P)],
                              waits=waits if ec == 0 else (), transpose=True))
        return toks

    t_wtq = build_WT(wbf[0], (t_cast_q, t_wal, t_xn_all))
    t_xnoT = []
    for ec in range(EC):
        t_xnoT.append(s.dma("sync", xnoTc[ec], xno_d[:, sl(ec, P)],
                            waits=(t_xno_all,) if ec == 0 else (), transpose=True))
    t_q_evac = {}
    t_qn_red = {}
    qpp_done = []
    for ft in range(H):
        b = ft % 2
        mm_half = []
        for half in range(2):
            pb = psb[b * 2 + half]
            mmt = None
            for ec in range(EC):
                w = []
                if ft < 2 and half == 0 and ec == 0:
                    w = [t_wtq[-1], t_xnoT[-1], t_ones]
                if ec == 0:
                    w = w + [t_q_evac.get((b, half))]
                mmt = s.mm(pb[:, 0:512], WTc[ec][:, sl(ft, P)],
                           xnoTc[ec][:, sl(half, 512)],
                           start=(ec == 0), stop=(ec == EC - 1), waits=w,
                           inc=(ec == EC - 1))
            mm_half.append(mmt)
        evs = []
        for half in range(2):
            pb = psb[b * 2 + half]
            ev = s.dve(lambda v, _b=b, _half=half, _pb=pb, _ft=ft: v.tensor_scalar(
                out=qh[_b][:, sl(_half, 512)], in0=_pb[:, 0:512],
                scalar1=SM[:, 32 + _ft:33 + _ft], scalar2=SM[:, 48 + _ft:49 + _ft],
                op0=ALU.mult, op1=ALU.add),
                waits=(mm_half[half], t_qbias))
            t_q_evac[(b, half)] = ev
            evs.append(ev)
        tq = s.dma("gpsimd", qpp_d[ft], qh[b], waits=evs)
        qpp_done.append(tq)
        tsq = s.act(sqst, qh[b][:, 0:512], AF.Square, waits=evs)
        pbq = psb[4 + b]
        tn1 = s.mm(pbq[0:1, 0:512], onescol[:, 0:1], sqst, start=True, stop=False,
                   waits=[tsq, t_ones, t_qn_red.get(b)], inc=True)
        tsq2 = s.act(sqst, qh[b][:, 512:1024], AF.Square, waits=(tn1,))
        tn2 = s.mm(pbq[0:1, 0:512], onescol[:, 0:1], sqst, start=False, stop=True,
                   waits=(tsq2,), inc=True)
        trd = s.dve(lambda v, _pb=pbq, _ft=ft: v.reduce_max(
            out=SM[0:1, 160 + _ft:161 + _ft], in_=_pb[0:1, 0:512], axis=AX),
            waits=(tn2,))
        t_qn_red[b] = trd
        qpp_done.append(trd)
    t_qpp_all = lat(*qpp_done)
    t_mq1 = s.dma("gpsimd", mqrow_d, SM[0:1, 160:176], waits=t_qpp_all)
    t_maxq2 = s.dma("gpsimd", SM[:, 160:176], mqrow_d.partition_broadcast(P),
                    waits=(t_mq1,))

    qunits = []
    for wsrc, qdst, widx in ((w1, q1_d, 0), (w2, q2_d, 1)):
        for i in range(NHT):
            for ch in range(2):
                qunits.append((wsrc[sl(i, P), sl(ch, TOK)],
                               qdst[sl(i, P), sl(ch, TOK)], widx, TOK))
    for i in range(EC):
        for cc in range(6):
            c0 = cc * TOK
            c1 = min((cc + 1) * TOK, HID)
            qunits.append((w3[sl(i, P), c0:c1], q3_d[sl(i, P), c0:c1], 2, c1 - c0))
    t_gz = s.pool(lambda g: g.memset(SM[:, 180:183], 0.0))
    QS = {"in": [None, None, None, None], "acc": t_gz, "t": None, "t2": None,
          "qs": [None, None], "rgam": None, "v0": None, "p1": 0, "p2": 0,
          "loaded": set(), "ld": {}}

    def q_load(i):
        if i >= len(qunits) or i in QS["loaded"]:
            return
        QS["loaded"].add(i)
        src_, dst_, widx, cols = qunits[i]
        b = i % NQIN
        eng = "gpsimd" if i % 2 == 0 else "scalar"
        QS["ld"][i] = s.dma(eng, qin[b][:, 0:cols], src_, waits=(QS["in"][b],))

    def emit_qpass1(n):
        while n > 0 and QS["p1"] < len(qunits):
            i = QS["p1"]
            QS["p1"] += 1
            n -= 1
            for la in range(NQIN):
                q_load(i + la)
            src_, dst_, widx, cols = qunits[i]
            b = i % NQIN
            tr = s.dve(lambda v, _b=b, _c=cols: v.tensor_reduce(
                out=SM[:, 215:216], in_=qin[_b][:, 0:_c], axis=AX, op=ALU.add,
                apply_absolute_value=True), waits=(QS["ld"][i], QS["acc"]))
            ta = s.dve(lambda v, _w=widx: v.tensor_tensor(
                out=SM[:, 180 + _w:181 + _w], in0=SM[:, 180 + _w:181 + _w],
                in1=SM[:, 215:216], op=ALU.add), waits=(tr,))
            QS["in"][b] = ta
            QS["acc"] = ta

    def emit_gamma_final():
        tw = s.dma("gpsimd", gsc_d, SM[:, 180:183], waits=(QS["acc"],))
        trd = s.dma("gpsimd", grow[0:1, 0:384],
                    gsc_d.rearrange("p c -> (p c)").unsqueeze(0), waits=(tw,))
        tred = s.dve(lambda v: v.tensor_reduce(
            out=rowf[0:1, 1088:1091], in_=grow[0:1, 0:384].rearrange("o (p c) -> o c p", c=3),
            axis=AX, op=ALU.add), waits=(trd,))
        tg = s.dve(lambda v: v.tensor_scalar(out=rowf[0:1, 1092:1095], in0=rowf[0:1, 1088:1091],
                                             scalar1=1.0 / QCOUNT, scalar2=EPS_Q,
                                             op0=ALU.mult, op1=ALU.add), waits=(tred,))
        tr = s.dve(lambda v: v.reciprocal(out=rowf[0:1, 1088:1091], in_=rowf[0:1, 1092:1095]),
                   waits=(tg,))
        tw2 = s.dma("gpsimd", gsc_d[0:1, 0:3], rowf[0:1, 1088:1091], waits=(tr,))
        tb = s.dma("gpsimd", SM[:, 177:180], gsc_d[0:1, 0:3].partition_broadcast(P),
                   waits=(tw2,))
        QS["rgam"] = tb
        QS["acc"] = tb
        QS["loaded"] = set()
        QS["ld"] = {}

    def emit_qpass2(n):
        while n > 0 and QS["p2"] < len(qunits):
            i = QS["p2"]
            QS["p2"] += 1
            n -= 1
            for la in range(NQIN):
                q_load(i + la)
            src_, dst_, widx, cols = qunits[i]
            b = i % NQIN
            d1 = s.dve(lambda v, _b=b, _c=cols, _w=widx: v.tensor_scalar(
                out=tb_a2[:, 0:_c], in0=qin[_b][:, 0:_c],
                scalar1=SM[:, 177 + _w:178 + _w], scalar2=SM[:, 216:217],
                op0=ALU.mult, op1=ALU.add),
                waits=(QS["ld"][i], QS["rgam"], QS["t"], QS["v0"]))
            d2 = s.dve(lambda v, _c=cols: v.tensor_scalar(
                out=tb_b2[:, 0:_c], in0=tb_a2[:, 0:_c], scalar1=MAGIC, scalar2=1.0,
                op0=ALU.subtract, op1=ALU.min), waits=(d1, QS["t2"]))
            QS["t"] = d2
            b2 = i % 2
            t3 = s.dve(lambda v, _b=b2, _c=cols: v.tensor_scalar(
                out=qst[_b][:, 0:_c], in0=tb_b2[:, 0:_c], scalar1=-1.0,
                scalar2=1.0, op0=ALU.max, op1=ALU.mult), waits=(d2, QS["qs"][b2]))
            QS["t2"] = t3
            tso = s.dma("gpsimd", dst_, qst[b2][:, 0:cols], waits=(t3,))
            QS["qs"][b2] = tso
            QS["in"][b] = d1


    t_cast_v = s.dma("gpsimd", wbf[0], Wv, waits=(t_wtq[-1], t_qpp_all))
    t_wtk = build_WT(wbf[1], (t_cast_k, t_qpp_all))
    t_kev = {}
    kst_last = {}
    knorm_last = None
    t_kscr = []
    xnT_lastuse = [None, None]
    def k_tsp(sp_):
        bsp_ = sp_ % 2
        out = []
        for ec in range(EC):
            w = (t_xn_all, xnT_lastuse[bsp_]) if ec == 0 else ()
            out.append(s.dma("sync", xnTc[bsp_][ec], xn_d[sl(sp_, 512), sl(ec, P)],
                             waits=w, transpose=True))
        return out

    tsp_pre = k_tsp(0)
    for sp in range(16):
        bsp = sp % 2
        tsp = tsp_pre
        if sp + 1 < 16:
            tsp_pre = k_tsp(sp + 1)
        sp_last_mm = None
        for ft in range(H):
            pb = psb[ft % 2]
            mmt = None
            for ec in range(EC):
                w = []
                if ec == 0:
                    w = [tsp[-1], t_kev.get(ft % 2)]
                    if sp == 0 and ft == 0:
                        w.append(t_wtk[-1])
                mmt = s.mm(pb[:, 0:512], WTc[ec][:, sl(ft, P)], xnTc[bsp][ec],
                           start=(ec == 0), stop=(ec == EC - 1), waits=w,
                           inc=(ec == EC - 1))
            sp_last_mm = mmt
            bst = ft % 2
            ev = s.dve(lambda v, _bst=bst, _pb=pb, _ft=ft: v.tensor_scalar_add(
                out=kvst[_bst][:, 0:512], in0=_pb[:, 0:512],
                scalar1=SM[:, 80 + _ft:81 + _ft]),
                waits=(mmt, t_bkT, kst_last.get(bst), knorm_last))
            t_kev[ft % 2] = ev
            tst = s.dma("sync", kscr[ft][:, sl(sp, 512)], kvst[bst][:, 0:512],
                        waits=(ev,))
            kst_last[bst] = tst
            t_kscr.append(tst)
            tks = s.act(sqst, kvst[bst][:, 0:512], AF.Square, waits=(ev, knorm_last))
            tkm = s.mm(psb[6][0:1, 0:512], onescol[:, 0:1], sqst, start=True, stop=True,
                       waits=(tks, knorm_last), inc=True)
            tkr = s.dve(lambda v: v.reduce_max(out=rowf[0:1, 24:25],
                                               in_=psb[6][0:1, 0:512], axis=AX),
                        waits=(tkm,))
            knorm_last = s.dve(lambda v, _ft=ft: v.tensor_max(
                out=rowf[0:1, 32 + _ft:33 + _ft], in0=rowf[0:1, 32 + _ft:33 + _ft],
                in1=rowf[0:1, 24:25]), waits=(tkr, t_kn0))
        xnT_lastuse[bsp] = sp_last_mm
        emit_qpass1(17)
    emit_qpass1(len(qunits))
    emit_gamma_final()
    t_kscr_all = lat(*t_kscr)

    t_cast_o = s.dma("gpsimd", wbf[1], Wo, waits=(t_wtk[-1],))
    t_wtv = build_WT(wbf[0], [t_cast_v] + [xnT_lastuse[0], xnT_lastuse[1]])
    t_vev = {}
    vst_last = {0: kst_last.get(0), 1: kst_last.get(1)}
    vstb = [0]
    t_vscr = []
    xnT_lastuse2 = [None, None]
    def v_tsp(sp_):
        bsp_ = sp_ % 2
        out = []
        for ec in range(EC):
            w = [xnT_lastuse2[bsp_]] if ec == 0 else []
            if sp_ < 2 and ec == 0:
                w += [xnT_lastuse[0], xnT_lastuse[1], t_kscr_all]
            out.append(s.dma("sync", xnTc[bsp_][ec], xn_d[sl(sp_, 512), sl(ec, P)],
                             waits=w, transpose=True))
        return out

    tsp_pre2 = v_tsp(0)
    for sp in range(16):
        bsp = sp % 2
        tsp = tsp_pre2
        if sp + 1 < 16:
            tsp_pre2 = v_tsp(sp + 1)
        sp_last_mm = None
        for tt4 in range(4):
            gt = sp * 4 + tt4
            for rnd in range(2):
                mm_of = {}
                for j, fs in enumerate((rnd * 2, rnd * 2 + 1)):
                    pb = psb[2 + j]
                    mmt = None
                    for ec in range(EC):
                        w = []
                        if ec == 0:
                            w = [tsp[-1], t_vev.get(j)]
                            if sp == 0 and tt4 == 0 and rnd == 0:
                                w.append(t_wtv[-1])
                        mmt = s.mm(pb[:, 0:512], xnTc[bsp][ec][:, sl(tt4, P)],
                                   WTc[ec][:, sl(fs, 512)],
                                   start=(ec == 0), stop=False, waits=w)
                    mmt = s.mm(pb[:, 0:512], onesrow[0:1, :],
                               rowbf[0:1, fs * 512:(fs + 1) * 512],
                               start=False, stop=True, waits=(t_onesr, t_bvrow),
                               inc=True)
                    mm_of[j] = (fs, mmt)
                for j in range(2):
                    fs, mmt = mm_of[j]
                    stb = vstb[0] % 4
                    vstb[0] += 1
                    ev = s.act(kvst[stb][:, 0:512], psb[2 + j][:, 0:512], AF.Copy,
                               waits=(mmt, vst_last.get(stb)))
                    t_vev[j] = ev
                    lastst = None
                    for hh in range(4):
                        hd = fs * 4 + hh
                        lastst = s.dma("sync", vscr[hd][sl(gt, P), :],
                                       kvst[stb][:, sl(hh, P)], waits=(ev,))
                        t_vscr.append(lastst)
                    vst_last[stb] = lastst
                    sp_last_mm = mmt
        xnT_lastuse2[bsp] = sp_last_mm
        emit_qpass2(17)
    t_vscr_all = lat(*t_vscr)
    emit_qpass2(len(qunits))
    t_quant_all = s.pool(lambda g: g.nop(),
                         waits=(QS["qs"][0], QS["qs"][1], QS["t2"]))

    t_kload = [None, None]
    t_vload = [None, None]
    t_qload = [None, None]
    t_k_lastuse = [None, None]
    t_v_lastuse = [None, None]
    t_q_lastuse = [None, None]
    stps_read = {0: None, 1: None}
    t_pv_done = None
    t_sum_done = None
    t_at_ev = [None, None]
    t_emul_last = None
    attn_store = []

    t_kload[0] = s.dma("sync", ksb[0], kscr[0],
                       waits=(t_kscr_all, xnT_lastuse2[0], xnT_lastuse2[1],
                              t_vev.get(0), t_vev.get(1), t_quant_all))
    def load_v(hh, nb, waits):
        tks = []
        for jc in range(NJT):
            tks.append(s.dma("sync", vsb[nb][:, sl(jc, P)],
                             vscr[hh][sl(jc, P), :], waits=waits if jc == 0 else ()))
        return lat(*tks)

    t_vload[0] = load_v(0, 0, (t_vscr_all,))
    t_qload[0] = s.dma("gpsimd", qh[0], qpp_d[0],
                       waits=(t_q_evac.get((0, 0)), t_q_evac.get((0, 1)),
                              t_kscr_all, knorm_last, t_vscr_all))

    tc1 = s.dve(lambda v: v.tensor_mul(out=rowf[0:1, 48:64], in0=SM[0:1, 160:176],
                                       in1=rowf[0:1, 32:48]),
                waits=[t_maxq2, knorm_last])
    tc2 = s.act(rowf[0:1, 16:32], rowf[0:1, 48:64], AF.Ln, waits=(tc1,))
    tc3 = s.dve(lambda v: v.tensor_scalar_mul(out=rowf[0:1, 48:64],
                                              in0=rowf[0:1, 16:32], scalar1=0.5),
                waits=(tc2,))
    tc4 = s.act(rowf[0:1, 16:32], rowf[0:1, 48:64], AF.Exp, waits=(tc3,))
    tc5 = s.dve(lambda v: v.scalar_tensor_tensor(
        out=rowf[0:1, 48:64], in0=rowf[0:1, 16:32], scalar=-1.0,
        in1=SM[0:1, 0:16], op0=ALU.mult, op1=ALU.mult), waits=(tc4,))
    tc6 = s.dve(lambda v: v.tensor_scalar_add(out=rowf[0:1, 16:32],
                                              in0=rowf[0:1, 48:64],
                                              scalar1=-C_EXTRA), waits=(tc5,))
    t_nc1 = s.dma("gpsimd", mqrow_d, rowf[0:1, 16:32], waits=(tc6,))
    t_negC_all = s.dma("gpsimd", SM[:, 224:240], mqrow_d.partition_broadcast(P),
                       waits=(t_nc1,))

    HH = HID // 2
    ffn_t = []
    units_t = []
    for (qsrc, tdst) in ((q1_d, w1t_d), (q2_d, w2t_d)):
        for dc in range(EC):
            for ch in range(2):
                units_t.append((qsrc[sl(ch, HH), sl(dc, P)],
                                tdst[sl(dc, P), sl(ch, HH)]))
    wr_prev = [None, None]
    wstg_i = [0]

    def emit_wstg(n):
        while n > 0 and wstg_i[0] < len(units_t):
            i = wstg_i[0]
            wstg_i[0] += 1
            n -= 1
            b = i % 2
            w = [wr_prev[b]]
            if i < 2:
                w.append(t_quant_all)
            ttr = s.dma("sync", wstg2[b][:, 0:HH], units_t[i][0], waits=w,
                        transpose=True)
            wr_prev[b] = s.dma("sync", units_t[i][1], wstg2[b][:, 0:HH],
                               waits=(ttr,))
            ffn_t.append(wr_prev[b])

    for h in range(H):
        hb = h % 2
        t_egen = s.act(Eh, Wal, AF.Exp, scale=SM[:, 112 + h:113 + h],
                       waits=(t_negs, t_emul_last))
        t_negC = t_negC_all

        if h + 1 < H:
            nb = (h + 1) % 2
            t_kload[nb] = s.dma("sync", ksb[nb], kscr[h + 1],
                                waits=(t_k_lastuse[nb],
                                       t_quant_all if h == 0 else None))
            t_vload[nb] = load_v(h + 1, nb, (t_v_lastuse[nb],))
            t_qload[nb] = s.dma("gpsimd", qh[nb], qpp_d[h + 1],
                                waits=(t_q_lastuse[nb],
                                       t_kscr_all if h == 0 else None,
                                       knorm_last if h == 0 else None))

        st_tok = {}
        emul = {}

        def emit_st(jt, _h=h, _hb=hb):
            bb = jt % 2
            w = [stps_read[bb]]
            if jt < 2:
                w += [t_kload[_hb], t_qload[_hb]]
            s.mm(psb[bb * 2][:, 0:512], ksb[_hb][:, sl(jt, P)], qh[_hb][:, 0:512],
                 start=True, stop=True, waits=w)
            st_tok[jt] = s.mm(psb[bb * 2 + 1][:, 0:512], ksb[_hb][:, sl(jt, P)],
                              qh[_hb][:, 512:1024], start=True, stop=True, inc=True)

        emit_st(0)
        pv_last = None
        for jt in range(NJT):
            bb = jt % 2
            texp = s.act(p0sb[bb][:, 0:512], psb[bb * 2][:, 0:512], AF.Exp,
                         bias=SM[:, 224 + h:225 + h], scale=SM[:, h:h + 1],
                         waits=(st_tok[jt], t_negC, emul.get(jt - 2)))
            texp2 = s.act(p0sb[bb][:, 512:1024], psb[bb * 2 + 1][:, 0:512], AF.Exp,
                          bias=SM[:, 224 + h:225 + h], scale=SM[:, h:h + 1])
            stps_read[bb] = texp2
            if jt + 1 < NJT:
                emit_st(jt + 1)
            off = 8192 - 128 * jt
            tmul = s.dve(lambda v, _bb=bb, _o=off: v.tensor_tensor(
                out=ptsb[_bb], in0=p0sb[_bb], in1=Eh[:, _o:_o + 1024], op=ALU.mult),
                waits=(texp2, t_egen,
                       t_vscr_all if (h == 0 and jt < 2) else None))
            emul[jt] = tmul
            t_emul_last = tmul
            w_pv = [tmul]
            if jt == 0:
                w_pv += [t_vload[hb], t_pv_done, t_sum_done]
            s.mm(psb[4][:, 0:512], vsb[hb][:, sl(jt, P)], ptsb[bb][:, 0:512],
                 start=(jt == 0), stop=(jt == NJT - 1), waits=w_pv)
            s.mm(psb[5][:, 0:512], vsb[hb][:, sl(jt, P)], ptsb[bb][:, 512:1024],
                 start=(jt == 0), stop=(jt == NJT - 1))
            s.mm(psb[6][0:1, 0:512], onescol[:, 0:1], ptsb[bb][:, 0:512],
                 start=(jt == 0), stop=(jt == NJT - 1))
            pv_last = s.mm(psb[7][0:1, 0:512], onescol[:, 0:1], ptsb[bb][:, 512:1024],
                           start=(jt == 0), stop=(jt == NJT - 1), inc=True)
        t_k_lastuse[hb] = st_tok[NJT - 1]
        t_q_lastuse[hb] = st_tok[NJT - 1]
        t_v_lastuse[hb] = pv_last
        tse = s.act(rowf[0:1, 64:576], psb[6][0:1, 0:512], AF.Copy, waits=(pv_last,))
        tse2 = s.act(rowf[0:1, 576:1088], psb[7][0:1, 0:512], AF.Copy)
        trc = s.dve(lambda v: v.reciprocal(out=rowf[0:1, 1088:2112],
                                           in_=rowf[0:1, 64:1088]), waits=(tse2,))
        trcb = s.dve(lambda v: v.tensor_copy(out=rowbf[0:1, 4096:5120],
                                             in_=rowf[0:1, 1088:2112]), waits=(trc,))
        tb1 = s.mm(psb[2][:, 0:512], onesrow[0:1, :], rowbf[0:1, 4096:4608],
                   start=True, stop=True, waits=(trcb, stps_read[1]))
        tb2 = s.mm(psb[3][:, 0:512], onesrow[0:1, :], rowbf[0:1, 4608:5120],
                   start=True, stop=True, inc=True)
        trb = s.act(rbsb[:, 0:512], psb[2][:, 0:512], AF.Copy, waits=(tb2,))
        trb2 = s.act(rbsb[:, 512:1024], psb[3][:, 0:512], AF.Copy)
        stps_read[1] = trb2
        ab = h % 2
        tat = s.dve(lambda v, _ab=ab: v.tensor_tensor(out=atst[_ab][:, 0:512],
                                                      in0=psb[4][:, 0:512],
                                                      in1=rbsb[:, 0:512],
                                                      op=ALU.mult),
                    waits=(trb2, pv_last, t_at_ev[ab],
                           t_vscr_all if h < 2 else None))
        tat2 = s.dve(lambda v, _ab=ab: v.tensor_tensor(out=atst[_ab][:, 512:1024],
                                                       in0=psb[5][:, 0:512],
                                                       in1=rbsb[:, 512:1024],
                                                       op=ALU.mult))
        t_pv_done = tat2
        t_sum_done = tse2
        tats = s.dma("gpsimd", attn_d[h], atst[ab], waits=(tat2,))
        t_at_ev[ab] = tats
        attn_store.append(tats)
        if h >= 1:
            emit_wstg(5)
    emit_wstg(len(units_t))
    t_ffnw_all = lat(*ffn_t)

    t_wto = build_WT(wbf[1], [t_cast_o, stps_read[0], stps_read[1], t_pv_done,
                              t_quant_all, t_emul_last])
    t_atch = [None, None]
    t_p4ev = {}
    t_xres_store = []
    t_xn2_store = []
    xslab_last = [last_apply[0], last_apply[1]]
    xrst_last = [None, None]
    xn2st_last = [None, None]
    p4items = [(tt_, ec_) for tt_ in range(8) for ec_ in range(EC)]
    tld_tok = {}

    def emit_tld(k):
        if k >= len(p4items) or k in tld_tok:
            return
        tt_, ec_ = p4items[k]
        w = [t_atch[0], t_atch[1]]
        if tt_ == 0:
            w.append(attn_store[ec_])
        tld_tok[k] = s.dma("sync", atch[k % 2][:, 0:TOK], attn_d[ec_], waits=w)

    for tt in range(8):
        bx = tt % 2
        txo = s.dma("sync", xslab[bx], xo[sl(tt, P), :], waits=(xslab_last[bx],))
        mm_fs = {}
        for ec in range(EC):
            k = tt * EC + ec
            bc = k % 2
            emit_tld(k)
            emit_tld(k + 1)
            for fs in range(4):
                w2_ = [tld_tok[k]]
                if ec == 0:
                    w2_ += [t_wto[-1], t_p4ev.get(fs)]
                mm_fs[fs] = s.mm(psb[4 + fs][:, 0:512], atch[bc][:, sl(tt, P)],
                                 WTc[ec][:, sl(fs, 512)],
                                 start=(ec == 0), stop=False, waits=w2_,
                                 inc=(fs == 3))
            t_atch[bc] = mm_fs[3]
        for fs in range(4):
            mm_fs[fs] = s.mm(psb[4 + fs][:, 0:512], onesrow[0:1, :],
                             rowbf[0:1, E + fs * 512:E + (fs + 1) * 512],
                             start=False, stop=True, waits=(t_borow,), inc=True)
        tt_ev = []
        for fs in range(4):
            ev = s.dve(lambda v, _bx=bx, _fs=fs: v.tensor_tensor(
                out=xrst[_bx][:, sl(_fs, 512)], in0=psb[4 + _fs][:, 0:512],
                in1=xslab[_bx][:, sl(_fs, 512)], op=ALU.add),
                waits=(mm_fs[fs], txo, xrst_last[bx]))
            t_p4ev[fs] = ev
            tt_ev.append(ev)
        t9 = ln_apply(xrst[bx], xn2st[bx], tt_ev + [xn2st_last[bx]])
        ts1 = s.dma("sync", xres_d[sl(tt, P), :], xrst[bx], waits=(t9,))
        ts2 = s.dma("sync", xn2_d[sl(tt, P), :], xn2st[bx], waits=(t9,))
        xrst_last[bx] = ts1
        xn2st_last[bx] = ts2
        xslab_last[bx] = t_p4ev[3]
        t_xres_store.append(ts1)
        t_xn2_store.append(ts2)
    t_xres_all = lat(*t_xres_store)
    t_xn2_all = lat(*t_xn2_store)

    t_xn2T = []
    for ec in range(EC):
        w = [t_xn2_all, t_p4ev[3]] if ec == 0 else ()
        t_xn2T.append(s.dma("sync", xn2Tc[ec], xn2_d[:, sl(ec, P)], waits=w,
                            transpose=True))
    t_out_final = []
    f1_psum_free = [stps_read[0], stps_read[0], stps_read[1], stps_read[1],
                    t_p4ev[0], t_p4ev[1], t_p4ev[2], t_p4ev[3]]
    w1sl_last = [None, None]
    hid_last = lat(t_xres_all, t_xn2_all)
    f2_done_prev = None
    for th in range(2):
        hs_ev = []
        for hs in range(11):
            tw1 = s.dma("sync", w1sl[0].rearrange("p (ec c) -> p ec c", c=512),
                        w1t_d[:, sl(hs, 512)].rearrange("(ec p) c -> p ec c", p=P),
                        waits=[t_ffnw_all, w1sl_last[0]] if hs == 0 else (w1sl_last[0],))
            tw2 = s.dma("sync", w1sl[1].rearrange("p (ec c) -> p ec c", c=512),
                        w2t_d[:, sl(hs, 512)].rearrange("(ec p) c -> p ec c", p=P),
                        waits=[t_ffnw_all, w1sl_last[1]] if hs == 0 else (w1sl_last[1],))
            mm1 = {}
            mm2 = {}
            for dc in range(EC):
                for tt in range(4):
                    w = []
                    if dc == 0:
                        w = [tw1, f1_psum_free[tt],
                             t_xn2T[-1] if hs == 0 and th == 0 else None,
                             f2_done_prev if hs == 0 else None]
                    mm1[tt] = s.mm(psb[tt][:, 0:512],
                                   xn2Tc[dc][:, th * 512 + tt * 128:th * 512 + tt * 128 + 128],
                                   w1sl[0][:, sl(dc, 512)],
                                   start=(dc == 0), stop=(dc == EC - 1), waits=w,
                                   inc=(dc == EC - 1))
                    w = []
                    if dc == 0:
                        w = [tw2, f1_psum_free[4 + tt]]
                    mm2[tt] = s.mm(psb[4 + tt][:, 0:512],
                                   xn2Tc[dc][:, th * 512 + tt * 128:th * 512 + tt * 128 + 128],
                                   w1sl[1][:, sl(dc, 512)],
                                   start=(dc == 0), stop=(dc == EC - 1), waits=w,
                                   inc=(dc == EC - 1))
            last_w1_mm = mm2[3]
            w1sl_last[0] = last_w1_mm
            w1sl_last[1] = last_w1_mm
            for tt in range(4):
                bsg = tt % 2
                tsg = s.act(sgst[bsg], psb[tt][:, 0:512], AF.Sigmoid,
                            waits=(mm1[tt], f1_psum_free[(tt + 3) % 4]))
                f1_psum_free[tt] = tsg
                thd = s.dve(lambda v, _t=tt, _hs=hs, _b=bsg: v.tensor_tensor(
                    out=hid_tt[_t][:, _hs * 512:_hs * 512 + 512],
                    in0=sgst[_b], in1=psb[4 + _t][:, 0:512], op=ALU.mult),
                    waits=(tsg, mm2[tt], hid_last))
                f1_psum_free[4 + tt] = thd
                hs_ev.append(thd)
        thw_parts = []
        for _t in range(4):
            thw_parts.append(s.dma("sync", hid_d[sl(_t, P), :], hid_tt[_t],
                                   waits=hs_ev if _t == 0 else ()))
        thw = lat(*thw_parts)
        hid_last = thw
        t_hT = []
        for hc in range(NHT):
            w = [thw] if hc == 0 else ()
            t_hT.append(s.dma("sync", hidTc[hc], hid_d[:, sl(hc, P)], waits=w,
                              transpose=True))
        q3_last = [None, None]
        f2_mm = {}
        for fh in range(2):
            for hc in range(NHT):
                bq3 = hc % 2
                w = [q3_last[bq3]]
                if fh == 0 and hc == 0:
                    w += [t_ffnw_all]
                if fh == 0 and hc == 0:
                    w += [t_at_ev[0], t_at_ev[1]]
                tq3 = s.dma("sync", q3ch[bq3], q3_d[sl(fh, TOK), sl(hc, P)], waits=w,
                            transpose=True)
                for tt in range(4):
                    w2_ = [tq3, t_hT[-1] if hc == 0 else None]
                    if hc == 0:
                        w2_.append(f2_mm.get((1 - fh, tt)))
                        w2_.append(f1_psum_free[2 * tt])
                        w2_.append(f1_psum_free[2 * tt + 1])
                    s.mm(psb[2 * tt][:, 0:512], hidTc[hc][:, sl(tt, P)],
                         q3ch[bq3][:, 0:512], start=(hc == 0), stop=(hc == NHT - 1),
                         waits=w2_)
                    f2_mm[(fh, tt)] = s.mm(psb[2 * tt + 1][:, 0:512],
                                           hidTc[hc][:, sl(tt, P)],
                                           q3ch[bq3][:, 512:1024],
                                           start=(hc == 0), stop=(hc == NHT - 1),
                                           waits=(),
                                           inc=(hc == NHT - 1 or tt == 3))
                q3_last[bq3] = f2_mm[(fh, 3)]
            for tt in range(4):
                bxr = tt % 2
                bo2 = tt % 2
                txr = s.dma("sync", xrtile[bxr], xres_d[sl(th * 4 + tt, P), sl(fh, TOK)],
                            waits=(t_xres_all if fh == 0 and tt < 2 and th == 0 else None,
                                   t_ffnw_all if fh == 0 and tt < 2 and th == 0 else None,
                                   t_p4ev.get(("xr", bxr))))
                e1 = s.dve(lambda v, _tt=tt, _b=bxr, _o=bo2: v.tensor_tensor(
                    out=outst2[_o][:, 0:512], in0=psb[2 * _tt][:, 0:512],
                    in1=xrtile[_b][:, 0:512], op=ALU.add),
                    waits=(f2_mm[(fh, tt)], txr, t_p4ev.get(("out", bo2))))
                e2 = s.dve(lambda v, _tt=tt, _b=bxr, _o=bo2: v.tensor_tensor(
                    out=outst2[_o][:, 512:1024], in0=psb[2 * _tt + 1][:, 0:512],
                    in1=xrtile[_b][:, 512:1024], op=ALU.add))
                t_p4ev[("xr", bxr)] = e2
                f1_psum_free[2 * tt] = e2
                f1_psum_free[2 * tt + 1] = e2
                tso = s.dma_out("gpsimd", yout[sl(th * 4 + tt, P), sl(fh, TOK)],
                                outst2[bo2], waits=(e2,))
                t_p4ev[("out", bo2)] = tso
                t_out_final.append(tso)
                f2_done_prev = lat(t_p4ev.get(("xr", 0)), t_p4ev.get(("xr", 1)))
        if th == 0:
            hid_last = lat(hid_last, t_p4ev.get(("out", 0)), t_p4ev.get(("out", 1)))

    s.fin("gpsimd", lat(*t_out_final))
    s.fin("sync", lat(*t_out_final))

    return nc, s


def finalize(nc, s):
    sems = {}
    ctxs = []
    for name in SEM_NAMES:
        cm = nc.semaphore(name)
        sems[name] = cm.__enter__()
        ctxs.append(cm)
    blk = nc.Block()
    block = blk.__enter__()
    block.sync(lambda e: s.emit(e, "sync", sems))
    block.gpsimd(lambda e: s.emit(e, "gpsimd", sems))
    block.vector(lambda e: s.emit(e, "vector", sems))
    block.scalar(lambda e: s.emit(e, "scalar", sems))
    block.tensor(lambda e: s.emit(e, "tensor", sems))
    blk.__exit__(None, None, None)
    for cm in reversed(ctxs):
        cm.__exit__(None, None, None)
    return nc


def build(dbg=0):
    nc, s = build_program(dbg=dbg)
    return finalize(nc, s)


def make_inputs(inputs, core):
    x = np.asarray(inputs["L0_attn_x"]).reshape(T, E)
    im = {
        "x_full": np.ascontiguousarray(x),
        "x_own": np.ascontiguousarray(x[core * TOK:(core + 1) * TOK]),
        "Wq": np.asarray(inputs["L0_attn_Wq"]),
        "Wk": np.asarray(inputs["L0_attn_Wk"]),
        "Wv": np.asarray(inputs["L0_attn_Wv"]),
        "Wo": np.asarray(inputs["L0_attn_Wo"]),
        "bq": np.asarray(inputs["L0_attn_bq"]).reshape(1, E),
        "bk": np.asarray(inputs["L0_attn_bk"]).reshape(1, E),
        "bv": np.asarray(inputs["L0_attn_bv"]).reshape(1, E),
        "bo": np.asarray(inputs["L0_attn_bo"]).reshape(1, E),
        "slopes": np.asarray(inputs["L0_attn_slopes"]).reshape(1, H),
        "w1": np.asarray(inputs["L0_ffn_w1"]),
        "w2": np.asarray(inputs["L0_ffn_w2"]),
        "w3": np.asarray(inputs["L0_ffn_w3"]),
        "wofs": np.full((P, 1), core * TOK - 8192, np.float32),
    }
    return im


_NC_CACHE = {}


def kernel(**inputs):
    from concourse.bass_utils import run_bass_kernel_spmd
    if "nc" not in _NC_CACHE:
        _NC_CACHE["nc"] = build(dbg=0)
    nc = _NC_CACHE["nc"]
    ims = [make_inputs(inputs, c) for c in range(8)]
    res = run_bass_kernel_spmd(nc, ims, core_ids=list(range(8)))
    return np.concatenate([res.results[c]["y"] for c in range(8)], axis=0).reshape(1, T, E)
